# version 09
# baseline (speedup 1.0000x reference)
import numpy as np
import ml_dtypes
from contextlib import ExitStack

import concourse.bass as bass
import concourse.mybir as mybir
from concourse.bass_utils import run_bass_kernel_spmd

F32 = mybir.dt.float32
BF16 = mybir.dt.bfloat16
AX = mybir.AxisListType.X
OP = mybir.AluOpType
ACT = mybir.ActivationFunctionType

D = 1024
DT = 8
VKV = 16
ROT_BASE = 10000.0
EXP_SHIFT = -20.0
QSC = 0.25
TAYSC = 2.0 ** -0.25
L1QSC = 32.0 ** -0.5
bfdt = ml_dtypes.bfloat16


def host_constants(T):
    c = {}
    c["ident32"] = np.eye(128, dtype=np.float32)
    c["ident16"] = np.eye(128, dtype=np.float32).astype(bfdt)
    c["ones_row_bf"] = np.ones((1, 128), dtype=np.float32).astype(bfdt)
    c["ones_row_f32"] = np.ones((1, 128), dtype=np.float32)
    c["ones_col_f32"] = np.ones((128, 1), dtype=np.float32)
    pos = np.arange(T, dtype=np.float64)
    freq = ROT_BASE ** (np.arange(16, dtype=np.float64) / 16.0)
    ang = pos[None, :] / freq[(np.arange(128) % 16)][:, None]
    c["sin"] = np.sin(ang).astype(np.float32)
    c["cos"] = np.cos(ang).astype(np.float32)
    pf = np.zeros((128, 128), dtype=np.float32)
    for m in range(128):
        if m % 32 < 16:
            pf[m + 16, m] = -1.0
        else:
            pf[m - 16, m] = 1.0
    c["pflip"] = pf.astype(bfdt)
    kvl = np.arange(128)[:, None]
    ql = np.arange(512)[None, :]
    c["masks"] = np.stack(
        [(kvl - ql <= 16 - 128 * dd).astype(np.float32) for dd in range(5)]
    ).astype(bfdt)
    sel = np.zeros((32, 2048), dtype=np.float32)
    for h in range(32):
        sel[h, h * 64:(h + 1) * 64] = 1.0
    c["sel32"] = sel.astype(bfdt)
    c["iota8"] = np.broadcast_to(np.arange(8, dtype=np.float32), (128, 8)).copy()
    c["iota8big"] = (c["iota8"] + 1e6).astype(np.float32)
    return c


CSHAPES = dict(ident32=(128, 128, F32), ident16=(128, 128, BF16),
               ones_row_bf=(1, 128, BF16), ones_row_f32=(1, 128, F32),
               ones_col_f32=(128, 1, F32), pflip=(128, 128, BF16),
               iota8=(128, 8, F32), iota8big=(128, 8, F32))


class Sem:
    def __init__(self, es, nc, name, dma=False):
        self.h = es.enter_context(nc.semaphore(name))
        self.v = 0
        self.name = name
        self.dma = dma

    def mark(self, instr, n=1):
        instr.then_inc(self.h, n)
        self.v += n
        return (self, self.v)

    def tok(self):
        return (self, self.v)


class Eng:
    def __init__(self, raw):
        self.raw = raw
        self.seen = {}

    def wait(self, *deps):
        for d in deps:
            if d is None:
                continue
            if isinstance(d, (list, tuple)) and d and isinstance(d[0], (list, tuple, type(None), Sem)) \
                    and not (len(d) == 2 and isinstance(d[0], Sem)):
                self.wait(*d)
                continue
            sem, val = d
            if sem.dma:
                val = sem.v
            if val > self.seen.get(sem, 0):
                self.raw.wait_ge(sem.h, val)
                self.seen[sem] = val


def build(T=2048, upto=99, debug=(), l0sub=9):
    NT = T // 128
    NCH = T // 512
    nc = bass.Bass("TRN2", target_bir_lowering=False)
    dt = nc.dram_tensor

    x_in = dt("x", [T, D], F32, kind="ExternalInput").ap()
    g0 = dt("L0_gamma", [8, 128], F32, kind="ExternalInput").ap()
    wq0 = dt("L0_w_q", [D, 128], F32, kind="ExternalInput").ap()
    wkv0 = dt("L0_w_kv", [D, 256], F32, kind="ExternalInput").ap()
    wo0 = dt("L0_w_out", [128, D], F32, kind="ExternalInput").ap()
    wr0T = dt("L0_WrT", [8, D], F32, kind="ExternalInput").ap()
    br0 = dt("L0_br", [1, 8], F32, kind="ExternalInput").ap()
    w1_0 = dt("L0_W1", [8, D, 4096], F32, kind="ExternalInput").ap()
    b1_0 = dt("L0_b1", [8, 4096], F32, kind="ExternalInput").ap()
    w2_0 = dt("L0_W2", [8, 4096, D], F32, kind="ExternalInput").ap()
    b2_0 = dt("L0_b2", [8, D], F32, kind="ExternalInput").ap()
    wqf1 = dt("L1_WQ", [D, D], F32, kind="ExternalInput").ap()
    wkf1 = dt("L1_WK", [D, D], F32, kind="ExternalInput").ap()
    wvf1 = dt("L1_WV", [D, 2048], F32, kind="ExternalInput").ap()
    wof1 = dt("L1_WO", [2048, D], F32, kind="ExternalInput").ap()
    bq1 = dt("L1_bQ", [8, 128], F32, kind="ExternalInput").ap()
    bk1 = dt("L1_bK", [8, 128], F32, kind="ExternalInput").ap()
    bv1 = dt("L1_bV", [1, 2048], F32, kind="ExternalInput").ap()
    vk1 = dt("L1_vk", [D, VKV], F32, kind="ExternalInput").ap()
    vv1_in = dt("L1_vv", [VKV, 2048], F32, kind="ExternalInput").ap()
    w1_1 = dt("L1_W1", [4096, D], F32, kind="ExternalInput").ap()
    b1_1 = dt("L1_b1", [32, 128], F32, kind="ExternalInput").ap()
    w2_1 = dt("L1_W2", [D, 4096], F32, kind="ExternalInput").ap()
    b2_1 = dt("L1_b2", [1, D], F32, kind="ExternalInput").ap()
    ci = {nm: dt("c_" + nm, [p_, f_], ty, kind="ExternalInput").ap()
          for nm, (p_, f_, ty) in CSHAPES.items()}
    sel_in = dt("c_sel32", [32, 2048], BF16, kind="ExternalInput").ap()
    sin_in = dt("c_sin", [128, T], F32, kind="ExternalInput").ap()
    cos_in = dt("c_cos", [128, T], F32, kind="ExternalInput").ap()
    masks_in = dt("c_masks", [5, 128, 512], BF16, kind="ExternalInput").ap()

    o_out = dt("o", [T, D], F32, kind="ExternalOutput").ap()
    xnsp = dt("xnspill", [128, DT, T], BF16, kind="Internal").ap()
    w1t_sc = dt("w1t_sc", [128, DT, 4096], BF16, kind="Internal").ap()
    w2t_sc = dt("w2t_sc", [128, 32, D], BF16, kind="Internal").ap()

    dbg_specs = {"xnT": ([128, DT, T], BF16), "srow": ([1, T], F32),
                 "acc1": ([128, NT, D], F32), "acc2": ([128, NT, D], F32),
                 "G": ([128, NT, 8], F32),
                 "qrot": ([128, 8, T], BF16), "krot": ([128, 8, T + VKV], BF16),
                 "vext": ([128, (NT + 1) * 32 * 65], BF16)}
    dbg_aps = {nm: dt("dbg_" + nm, s, ty, kind="ExternalOutput").ap()
               for nm, (s, ty) in dbg_specs.items() if nm in debug}

    es = ExitStack()
    st_acc = ExitStack()
    st_xn = ExitStack()

    _uniq = [0]

    def sbl(st, name, shape, ty):
        _uniq[0] += 1
        return st.enter_context(nc.sbuf_tensor(f"{name}_{_uniq[0]}", list(shape), ty))

    xnT = st_xn.enter_context(nc.sbuf_tensor("xnT", [128, DT, T], BF16, side="right"))
    acc = st_acc.enter_context(nc.sbuf_tensor("acc", [128, NT, D], F32, side="right"))

    consts = {nm: es.enter_context(nc.sbuf_tensor("s_" + nm, [p_, f_], ty))
              for nm, (p_, f_, ty) in CSHAPES.items()}
    gamma_c = es.enter_context(nc.sbuf_tensor("gamma_c", [128, DT], F32))
    wq_b = es.enter_context(nc.sbuf_tensor("wq_b", [128, DT, 128], BF16))
    wkv_b = es.enter_context(nc.sbuf_tensor("wkv_b", [128, DT, 256], BF16))
    wr_b = es.enter_context(nc.sbuf_tensor("wr_b", [128, DT, 8], BF16))
    br_b = es.enter_context(nc.sbuf_tensor("br_b", [1, 8], BF16))
    b1c0 = es.enter_context(nc.sbuf_tensor("b1c0", [128, 8, 32], F32))
    bq_c = es.enter_context(nc.sbuf_tensor("bq_c", [128, 8], F32))
    bk_c = es.enter_context(nc.sbuf_tensor("bk_c", [128, 8], F32))
    b1c1 = es.enter_context(nc.sbuf_tensor("b1c1", [128, 32], F32))
    G = es.enter_context(nc.sbuf_tensor("G", [128, NT, 8], F32))

    pA = es.enter_context(nc.psum_tensor("pA", [128, 1024], F32))
    pB = es.enter_context(nc.psum_tensor("pB", [128, 1024], F32))
    pC = es.enter_context(nc.psum_tensor("pC", [128, 1024], F32))
    pD = es.enter_context(nc.psum_tensor("pD", [128, 1024], F32))
    PS = [pA[:, 0:512], pA[:, 512:1024], pB[:, 0:512], pB[:, 512:1024],
          pC[:, 0:512], pC[:, 512:1024], pD[:, 0:512], pD[:, 512:1024]]
    bank_free = [None] * 8

    sd = Sem(es, nc, "sd", dma=True)
    gd = Sem(es, nc, "gd", dma=True)
    gw = Sem(es, nc, "gw", dma=True)
    moes = [Sem(es, nc, "moe0", dma=True), Sem(es, nc, "moe1", dma=True)]
    tls = [Sem(es, nc, "tl0", dma=True), Sem(es, nc, "tl1", dma=True)]
    accs = [Sem(es, nc, "acc0", dma=True), Sem(es, nc, "acc1", dma=True)]
    ffws = [Sem(es, nc, "ffw1", dma=True), Sem(es, nc, "ffw2", dma=True),
            Sem(es, nc, "ffw3", dma=True)]
    b2ss = [Sem(es, nc, "b20", dma=True), Sem(es, nc, "b21", dma=True)]
    tst = Sem(es, nc, "tst", dma=True)
    mm = Sem(es, nc, "mm")
    vv = Sem(es, nc, "vv")
    sc = Sem(es, nc, "sc")
    ALLSEMS = (sd, gd, gw, moes[0], moes[1], tls[0], tls[1], b2ss[0], b2ss[1],
               accs[0], accs[1], ffws[0], ffws[1], ffws[2], tst, mm, vv, sc)

    ident32 = consts["ident32"]
    ident16 = consts["ident16"]
    ones_row_bf = consts["ones_row_bf"]
    ones_row_f32 = consts["ones_row_f32"]
    tk = {}
    final_outs = []

    with es, nc.Block() as block:
        @block.sync
        def _(sync_unused):
            TE = Eng(nc.tensor)
            VE = Eng(nc.vector)
            SE = Eng(nc.scalar)
            GE = Eng(nc.gpsimd)
            SY = Eng(nc.sync)

            def dma(q, sem, out, in_, deps=(), **kw):
                q.wait(*deps)
                return sem.mark(q.raw.dma_start(out=out, in_=in_, **kw), 16)

            def use_bank(bk):
                TE.wait(bank_free[bk])

            def barrier():
                VE.raw.drain()
                toks = [s.tok() for s in ALLSEMS]
                for E in (TE, VE, SE, GE, SY):
                    E.wait(*toks)

            def small_transpose(src_ap, dst_ap, bk, p_, f_, deps=()):
                use_bank(bk)
                TE.wait(tk.get("consts"), *deps)
                tr = mm.mark(TE.raw.matmul(PS[bk][0:f_, 0:p_], lhsT=src_ap,
                                           rhs=ident32[0:p_, 0:p_], start=True, stop=True))
                VE.wait(tr)
                t = vv.mark(VE.raw.tensor_copy(dst_ap, PS[bk][0:f_, 0:p_]))
                bank_free[bk] = t
                return t

            for nm in CSHAPES:
                tk["consts"] = dma(SY, sd, consts[nm][:], ci[nm])
            tk["x"] = dma(SY, sd, acc[:], x_in.rearrange("(i p) d -> p i d", p=128))

            with ExitStack() as st0:
                graw = sbl(st0, "graw", (8, 128), F32)
                wqf = sbl(st0, "wqf", (128, DT, 384), F32)
                wrT = sbl(st0, "wrT", (8, DT, 128), F32)
                b1row = sbl(st0, "b1row", (32, 8, 128), F32)
                bqrow = sbl(st0, "bqrow", (8, 2, 128), F32)
                b1frow = sbl(st0, "b1frow", (32, 128), F32)

                t_g = dma(GE, gd, graw[:], g0)
                t_wq = dma(GE, gd, wqf[:, :, 0:128], wq0.rearrange("(a p) m -> p a m", p=128))
                t_wkv = dma(GE, gd, wqf[:, :, 128:384], wkv0.rearrange("(a p) m -> p a m", p=128))
                t_wr = dma(GE, gd, wrT[:], wr0T.rearrange("e (a p) -> e a p", p=128))
                t_br = dma(GE, gd, br_b[:], br0)
                t_b1r = dma(GE, gd, b1row[:], b1_0.rearrange("e (c p) -> c e p", p=128),
                            allow_slow_non_contiguous=True)
                t_bq = dma(GE, gd, bqrow[:, 0, :], bq1)
                t_bk = dma(GE, gd, bqrow[:, 1, :], bk1)
                t_b1f = dma(GE, gd, b1frow[:], b1_1)

                small_transpose(graw[:], gamma_c[:], 6, 8, 128, deps=(t_g,))
                t_gc = vv.tok()
                VE.wait(t_wq, t_wkv, t_gc)
                VE.raw.drain()
                for a in range(DT):
                    VE.raw.tensor_scalar(out=wq_b[:, a, :], in0=wqf[:, a, 0:128],
                                         scalar1=gamma_c[:, a:a + 1], scalar2=QSC,
                                         op0=OP.mult, op1=OP.mult)
                    tk["wfold"] = vv.mark(VE.raw.tensor_scalar(
                        out=wkv_b[:, a, :], in0=wqf[:, a, 128:384],
                        scalar1=gamma_c[:, a:a + 1], scalar2=None, op0=OP.mult))
                with ExitStack() as st00:
                    wrtmp = sbl(st00, "wrtmp", (128, DT, 8), F32)
                    for a in range(DT):
                        small_transpose(wrT[:, a, :], wrtmp[:, a, :], a % 2, 8, 128,
                                        deps=(t_wr,))
                    VE.raw.drain()
                    tk["wr"] = vv.mark(VE.raw.tensor_copy(wr_b[:], wrtmp[:]))
                for e in range(8):
                    small_transpose(b1row[:, e, :], b1c0[:, e, :], 2 + e % 2, 32, 128,
                                    deps=(t_b1r,))
                small_transpose(bqrow[:, 0, :], bq_c[:], 4, 8, 128, deps=(t_bq,))
                small_transpose(bqrow[:, 1, :], bk_c[:], 5, 8, 128, deps=(t_bk,))
                small_transpose(b1frow[:], b1c1[:], 6, 32, 128, deps=(t_b1f,))
                tk["smallpar"] = vv.tok()
                barrier()

            def layernorm_phase(want_s, S_full):
                with ExitStack() as st:
                    xtmp = sbl(st, "ln_xtmp", (128, 2, D), F32)
                    sq_t = sbl(st, "ln_sq", (128, 2, D), F32)
                    cols = sbl(st, "ln_cols", (128, 8, NT), F32)
                    s_tok = sbl(st, "ln_stok", (128, NT), F32)
                    s_row = sbl(st, "ln_srow", (1, T), F32)
                    t_sq_rd = [None, None]
                    for i in range(NT):
                        VE.raw.tensor_reduce(cols[:, 0, i:i + 1], acc[:, i, :], AX, OP.add)
                        sl = i % 2
                        SE.wait(t_sq_rd[sl])
                        t_sqw = sc.mark(SE.raw.activation(sq_t[:, sl, :], acc[:, i, :],
                                                          ACT.Square))
                        VE.wait(t_sqw)
                        t_sq_rd[sl] = vv.mark(VE.raw.tensor_reduce(
                            cols[:, 1, i:i + 1], sq_t[:, sl, :], AX, OP.add))
                    VE.raw.drain()
                    VE.raw.tensor_scalar(out=cols[:, 2, :], in0=cols[:, 0, :],
                                         scalar1=-1.0 / D, scalar2=None, op0=OP.mult)
                    VE.raw.tensor_scalar(out=cols[:, 3, :], in0=cols[:, 1, :],
                                         scalar1=1.0 / D, scalar2=None, op0=OP.mult)
                    VE.raw.drain()
                    VE.raw.tensor_tensor(out=cols[:, 4, :], in0=cols[:, 2, :],
                                         in1=cols[:, 2, :], op=OP.mult)
                    VE.raw.drain()
                    VE.raw.tensor_tensor(out=cols[:, 4, :], in0=cols[:, 3, :],
                                         in1=cols[:, 4, :], op=OP.subtract)
                    VE.raw.drain()
                    VE.raw.tensor_scalar(out=cols[:, 5, :], in0=cols[:, 4, :],
                                         scalar1=1e-5, scalar2=None, op0=OP.add)
                    VE.raw.drain()
                    t_rp = vv.mark(VE.raw.reciprocal(cols[:, 6, :], cols[:, 5, :]))
                    SE.wait(t_rp)
                    t_rstd = sc.mark(SE.raw.activation(cols[:, 7, :], cols[:, 6, :],
                                                       ACT.Sqrt))
                    if want_s:
                        VE.wait(t_rstd)
                        VE.raw.tensor_tensor(out=cols[:, 5, :], in0=cols[:, 4, :],
                                             in1=cols[:, 6, :], op=OP.mult)
                        VE.raw.drain()
                        VE.raw.tensor_scalar(out=cols[:, 5, :], in0=cols[:, 5, :],
                                             scalar1=9.765625e-28, scalar2=None, op0=OP.max)
                        VE.raw.drain()
                        t_si = vv.mark(VE.raw.reciprocal(cols[:, 5, :], cols[:, 5, :]))
                        SE.wait(t_si)
                        t_sv = sc.mark(SE.raw.activation(s_tok[:], cols[:, 5, :], ACT.Sqrt))
                    t_xuse = [None, None]
                    for i in range(NT):
                        sl = i % 2
                        VE.wait(t_rstd, t_xuse[sl])
                        VE.raw.drain()
                        t_xn = vv.mark(VE.raw.tensor_scalar(
                            out=xtmp[:, sl, :], in0=acc[:, i, :],
                            scalar1=cols[:, 2, i:i + 1],
                            scalar2=cols[:, 7, i:i + 1], op0=OP.add, op1=OP.mult))
                        TE.wait(t_xn, tk["consts"])
                        for a in range(DT):
                            bk = a % 2
                            use_bank(bk)
                            tr = mm.mark(TE.raw.transpose(
                                PS[bk][:, 0:128], xtmp[:, sl, a * 128:(a + 1) * 128],
                                ident32[:]))
                            VE.wait(tr)
                            bank_free[bk] = vv.mark(VE.raw.tensor_copy(
                                xnT[:, a, i * 128:(i + 1) * 128], PS[bk][:, 0:128]))
                        t_xuse[sl] = tr
                    if want_s:
                        t_stok = sc.tok()
                        for i in range(NT):
                            bk = 2 + i % 2
                            use_bank(bk)
                            TE.wait(t_stok, tk["consts"])
                            tr = mm.mark(TE.raw.matmul(
                                PS[bk][0:1, 0:128], lhsT=s_tok[:, i:i + 1],
                                rhs=ident32[:], start=True, stop=True))
                            SE.wait(tr)
                            bank_free[bk] = sc.mark(SE.raw.copy(
                                s_row[0:1, i * 128:(i + 1) * 128], PS[bk][0:1, 0:128]))
                        t_srow = sc.tok()
                        if "srow" in dbg_aps:
                            final_outs.append(dma(SY, sd, dbg_aps["srow"], s_row[:],
                                                  deps=(t_srow,)))
                        for cch in range(NCH):
                            bk = 4 + cch % 2
                            use_bank(bk)
                            TE.wait(t_srow, tk["consts"])
                            tr = mm.mark(TE.raw.matmul(
                                PS[bk], lhsT=ones_row_f32[:],
                                rhs=s_row[0:1, cch * 512:(cch + 1) * 512],
                                start=True, stop=True))
                            VE.wait(tr)
                            bank_free[bk] = vv.mark(VE.raw.tensor_copy(
                                S_full[:, cch * 512:(cch + 1) * 512], PS[bk]))
                    barrier()

            def l0_attn(S_full, st_S):
                st_l0 = ExitStack()
                qT = sbl(st_l0, "qT", (128, T), BF16)
                kT = sbl(st_l0, "kT", (128, T), BF16)
                vT = sbl(st_l0, "vT", (128, T), BF16)
                q_aug = sbl(st_l0, "q_aug", (17, 8, T), BF16)
                q_tok = sbl(st_l0, "q_tok", (128, NT, 128), BF16)
                with st_l0:
                    t_ev = {0: None, 1: None, 2: None}
                    for cch in range(NCH):
                        for wi in range(3):
                            buf = (qT, kT, vT)[wi]
                            bk = (cch * 3 + wi) % 4
                            use_bank(bk)
                            TE.wait(tk["wfold"])
                            for a in range(DT):
                                lhsT = (wq_b[:, a, :] if wi == 0
                                        else wkv_b[:, a, (wi - 1) * 128:wi * 128])
                                t_mm = mm.mark(TE.raw.matmul(
                                    PS[bk], lhsT=lhsT, rhs=xnT[:, a, cch * 512:(cch + 1) * 512],
                                    start=(a == 0), stop=(a == DT - 1)))
                            VE.wait(t_mm)
                            t_c = vv.mark(VE.raw.tensor_tensor(
                                out=buf[:, cch * 512:(cch + 1) * 512], in0=PS[bk],
                                in1=S_full[:, cch * 512:(cch + 1) * 512], op=OP.mult))
                            bank_free[bk] = t_c
                            t_ev[wi] = t_c
                    barrier()
                    st_S.close()
                    if l0sub < 2:
                        return

                    kvA = sbl(st_l0, "kvA", (17, 8, 17), BF16)
                    kvB = sbl(st_l0, "kvB", (128, 8, 17), BF16)
                    kvC = sbl(st_l0, "kvC", (128, 8, 17), BF16)
                    with ExitStack() as st:
                        k_tok = sbl(st, "k_tok", (128, NT, 8, 17), BF16)
                        v_tok = sbl(st, "v_tok", (128, NT, 8, 17), BF16)
                        ksc = sbl(st, "ksc", (128, 8, 16), BF16)
                        kf2 = sbl(st, "kf2", (128, 2, 8, 256), BF16)
                        kvS = sbl(st, "kvS", (17, 8, 273), BF16)

                        t_qaug = [vv.mark(VE.raw.memset(q_aug[0:1, :, :], 1.0))]
                        for h in range(8):
                            t_qaug.append(dma(SY, sd, q_aug[1:17, h, :],
                                              qT[h * 16:(h + 1) * 16, :], deps=(t_ev[0],)))
                        t_kones = vv.mark(VE.raw.memset(k_tok[:, :, :, 0:1], 1.0))
                        VE.raw.drain()
                        t_vones = vv.mark(VE.raw.memset(v_tok[:, :, :, 0:1], 1.0))

                        t_tok = [[None] * 3 for _ in range(NT)]
                        for i in range(NT):
                            for wi, src in enumerate((qT, kT, vT)):
                                bk = (i * 3 + wi) % 4
                                use_bank(bk)
                                TE.wait(t_ev[wi], tk["consts"])
                                tr = mm.mark(TE.raw.matmul(
                                    PS[bk][:, 0:128], lhsT=src[:, i * 128:(i + 1) * 128],
                                    rhs=ident16[:], start=True, stop=True))
                                VE.wait(tr)
                                if wi == 0:
                                    t_tok[i][wi] = vv.mark(VE.raw.tensor_copy(
                                        q_tok[:, i, :], PS[bk][:, 0:128]))
                                else:
                                    dst = (k_tok if wi == 1 else v_tok)[:, i, :, 1:17]
                                    t_tok[i][wi] = vv.mark(VE.raw.tensor_copy(
                                        dst, PS[bk][:, 0:128].rearrange("p (h d) -> p h d", h=8)))
                                bank_free[bk] = t_tok[i][wi]

                        if l0sub < 3:
                            barrier()
                            return
                        kvps = [pA[0:17, 0:273], pA[0:17, 512:785],
                                pB[0:17, 0:273], pB[0:17, 512:785],
                                pC[0:17, 0:273], pC[0:17, 512:785],
                                pD[0:17, 0:273], pD[0:17, 512:785]]
                        t_kf2_used = None
                        for i in range(NT):
                            par = i % 2
                            VE.wait(t_tok[i][1], t_tok[i][2], t_kones, t_vones, t_kf2_used)
                            VE.raw.drain()
                            VE.raw.tensor_scalar(out=ksc[:], in0=k_tok[:, i, :, 1:17],
                                                 scalar1=TAYSC, scalar2=None, op0=OP.mult)
                            VE.raw.drain()
                            for j in range(16):
                                t_kf2 = vv.mark(VE.raw.tensor_tensor(
                                    out=kf2[:, par, :, j * 16:(j + 1) * 16], in0=ksc[:],
                                    in1=ksc[:, :, j:j + 1].broadcast_to([128, 8, 16]),
                                    op=OP.mult))
                            TE.wait(t_kf2)
                            if i == 0:
                                for bk in range(8):
                                    use_bank(bk)
                            for h in range(8):
                                TE.raw.matmul(kvps[h][:, 0:17], lhsT=v_tok[:, i, h, :],
                                              rhs=k_tok[:, i, h, :],
                                              start=(i == 0), stop=False)
                                t_kv = mm.mark(TE.raw.matmul(
                                    kvps[h][:, 17:273], lhsT=v_tok[:, i, h, :],
                                    rhs=kf2[:, par, h, :],
                                    start=False, stop=(i == NT - 1)))
                            if i >= 1:
                                t_kf2_used = t_kv

                        t_kvS = []
                        for h in range(8):
                            SE.wait(t_kv)
                            t_kvS.append(sc.mark(SE.raw.copy(kvS[:, h, :], kvps[h])))
                            bank_free[h] = t_kvS[-1]
                        for h in range(8):
                            for pi, (lo, hi, dst) in enumerate(((0, 17, kvA), (17, 145, kvB),
                                                                (145, 273, kvC))):
                                bk = (h * 3 + pi) % 2
                                use_bank(bk)
                                TE.wait(t_kvS[h])
                                tr = mm.mark(TE.raw.matmul(
                                    PS[bk][0:(hi - lo), 0:17], lhsT=kvS[:, h, lo:hi],
                                    rhs=ident16[0:17, 0:17], start=True, stop=True))
                                VE.wait(tr)
                                bank_free[bk] = vv.mark(VE.raw.tensor_copy(
                                    dst[:, h, :], PS[bk][0:(hi - lo), 0:17]))
                        t_kvT = vv.tok()
                        barrier()

                    if l0sub < 4:
                        return
                    with ExitStack() as st:
                        qsc = sbl(st, "qsc", (128, 8, 16), BF16)
                        qf2 = sbl(st, "qf2", (128, 8, 256), BF16)
                        qf2T = sbl(st, "qf2T", (128, 2, 8, 512), BF16)
                        outS = sbl(st, "outS", (17, 8, 512), BF16)
                        recS = sbl(st, "recS", (1, 2, 512), F32)
                        wout2 = sbl(st, "wout2", (17, 8, D), BF16)
                        t_wo2z = vv.mark(VE.raw.memset(wout2[0:1, :, :], 0.0))
                        t_wo2 = dma(GE, gd, wout2[1:17, :, :],
                                    wo0.rearrange("(h e) m -> e h m", e=16),
                                    allow_slow_non_contiguous=True)
                        for cch in range(NCH):
                            t_qf2T = []
                            t_qf2_used = mm.tok()
                            for jj in range(4):
                                i = 4 * cch + jj
                                VE.wait(t_tok[i][0], t_qf2_used)
                                VE.raw.drain()
                                VE.raw.tensor_scalar(
                                    out=qsc[:], in0=q_tok[:, i, :].rearrange(
                                        "p (h d) -> p h d", h=8),
                                    scalar1=TAYSC, scalar2=None, op0=OP.mult)
                                VE.raw.drain()
                                for j in range(16):
                                    t_qf2 = vv.mark(VE.raw.tensor_tensor(
                                        out=qf2[:, :, j * 16:(j + 1) * 16], in0=qsc[:],
                                        in1=qsc[:, :, j:j + 1].broadcast_to([128, 8, 16]),
                                        op=OP.mult))
                                TE.wait(t_qf2)
                                for h in range(8):
                                    for piece in range(2):
                                        bk = (h * 2 + piece) % 2
                                        use_bank(bk)
                                        tr = mm.mark(TE.raw.matmul(
                                            PS[bk][:, 0:128],
                                            lhsT=qf2[:, h, piece * 128:(piece + 1) * 128],
                                            rhs=ident16[:], start=True, stop=True))
                                        VE.wait(tr)
                                        bank_free[bk] = vv.mark(VE.raw.tensor_copy(
                                            qf2T[:, piece, h, jj * 128:(jj + 1) * 128],
                                            PS[bk][:, 0:128]))
                                        t_qf2T.append(bank_free[bk])
                                t_qf2_used = mm.tok()
                            if l0sub < 5:
                                continue
                            t_sc_h = []
                            for h in range(8):
                                bko = 2 + h % 2
                                use_bank(bko)
                                ps17 = PS[bko][0:17, :]
                                TE.wait(t_qaug, t_qf2T[-1], t_kvT)
                                TE.raw.matmul(ps17, lhsT=kvA[:, h, :],
                                              rhs=q_aug[:, h, cch * 512:(cch + 1) * 512],
                                              start=True, stop=False)
                                TE.raw.matmul(ps17, lhsT=kvB[:, h, :], rhs=qf2T[:, 0, h, :],
                                              start=False, stop=False)
                                t3 = mm.mark(TE.raw.matmul(ps17, lhsT=kvC[:, h, :],
                                                           rhs=qf2T[:, 1, h, :],
                                                           start=False, stop=True))
                                SE.wait(t3)
                                t_outS = sc.mark(SE.raw.copy(outS[:, h, :], ps17))
                                if l0sub == 50:
                                    bank_free[bko] = t_outS
                                    t_sc_h.append(t_outS)
                                    continue
                                t_den = sc.mark(SE.raw.copy(recS[0:1, 0, :], ps17[0:1, :]))
                                bank_free[bko] = t_den
                                VE.wait(t_den)
                                VE.raw.drain()
                                VE.raw.tensor_scalar(out=recS[0:1, 0, :],
                                                     in0=recS[0:1, 0, :],
                                                     scalar1=1e-5, scalar2=None, op0=OP.max)
                                VE.raw.drain()
                                t_rec = vv.mark(VE.raw.reciprocal(recS[0:1, 1, :],
                                                                  recS[0:1, 0, :]))
                                if l0sub == 51:
                                    t_sc_h.append(t_rec)
                                    continue
                                bk4 = 4 + h % 2
                                use_bank(bk4)
                                TE.wait(t_rec)
                                mmb = mm.mark(TE.raw.matmul(PS[bk4][0:17, :],
                                                            lhsT=ones_row_f32[0:1, 0:17],
                                                            rhs=recS[0:1, 1, :],
                                                            start=True, stop=True))
                                if l0sub == 52:
                                    bank_free[bk4] = mmb
                                    t_sc_h.append(mmb)
                                    continue
                                VE.wait(mmb, t_outS)
                                t_sch = vv.mark(VE.raw.tensor_tensor(
                                    out=outS[:, h, :], in0=outS[:, h, :],
                                    in1=PS[bk4][0:17, :], op=OP.mult))
                                bank_free[bk4] = t_sch
                                t_sc_h.append(t_sch)
                            if l0sub < 6 or l0sub in (50, 51, 52):
                                continue
                            for tt_ in range(4):
                                for dch in range(2):
                                    bk = 6 + (tt_ * 2 + dch) % 2
                                    use_bank(bk)
                                    TE.wait(t_sc_h, t_wo2z, t_wo2)
                                    for h in range(8):
                                        t_wo = mm.mark(TE.raw.matmul(
                                            PS[bk], lhsT=outS[:, h, tt_ * 128:(tt_ + 1) * 128],
                                            rhs=wout2[:, h, dch * 512:(dch + 1) * 512],
                                            start=(h == 0), stop=(h == 7)))
                                    VE.wait(t_wo)
                                    aslc = acc[:, 4 * cch + tt_, dch * 512:(dch + 1) * 512]
                                    bank_free[bk] = vv.mark(VE.raw.tensor_tensor(
                                        out=aslc, in0=aslc, in1=PS[bk], op=OP.add))
                    barrier()

            def l0_ffn(tern_gen):
                with ExitStack() as st:
                    W1q = sbl(st, "W1q", (128, 2, DT, 1024), BF16)
                    W2q = sbl(st, "W2q", (128, 2, 8, 1024), BF16)
                    h1 = sbl(st, "h1", (128, 8, 512), BF16)
                    b2s = sbl(st, "b2s", (1, 2, D), BF16)
                    lgt = sbl(st, "lgt", (128, 8), F32)
                    lgt2 = sbl(st, "lgt2", (128, 8), F32)
                    colsr = sbl(st, "colsr", (128, 6), F32)
                    for i in range(NT):
                        bk = 6 + i % 2
                        use_bank(bk)
                        TE.wait(tk["wr"], tk.get("smallpar"))
                        psl = PS[bk][:, 0:8]
                        for a in range(DT):
                            TE.raw.matmul(psl, lhsT=xnT[:, a, i * 128:(i + 1) * 128],
                                          rhs=wr_b[:, a, :], start=(a == 0), stop=False)
                        t_lg = mm.mark(TE.raw.matmul(psl, lhsT=ones_row_bf[:],
                                                     rhs=br_b[:], start=False, stop=True))
                        VE.wait(t_lg)
                        VE.raw.tensor_reduce(colsr[:, 0:1], psl, AX, OP.max)
                        VE.raw.drain()
                        t_mx = vv.mark(VE.raw.tensor_scalar(
                            out=colsr[:, 1:2], in0=colsr[:, 0:1],
                            scalar1=-1.0, scalar2=None, op0=OP.mult))
                        SE.wait(t_mx)
                        t_e = sc.mark(SE.raw.activation(lgt[:], psl, ACT.Exp,
                                                        bias=colsr[:, 1:2]))
                        VE.wait(t_e)
                        VE.raw.tensor_reduce(colsr[:, 2:3], lgt[:], AX, OP.add)
                        VE.raw.drain()
                        VE.raw.reciprocal(colsr[:, 3:4], colsr[:, 2:3])
                        VE.raw.tensor_scalar(out=lgt2[:], in0=psl, scalar1=colsr[:, 0:1],
                                             scalar2=-1e6, op0=OP.is_ge, op1=OP.mult)
                        VE.raw.drain()
                        bank_free[bk] = vv.mark(VE.raw.tensor_tensor(
                            out=lgt2[:], in0=lgt2[:], in1=consts["iota8big"][:], op=OP.add))
                        VE.raw.drain()
                        VE.raw.tensor_reduce(colsr[:, 4:5], lgt2[:], AX, OP.min)
                        VE.raw.drain()
                        VE.raw.tensor_scalar(out=lgt2[:], in0=consts["iota8"][:],
                                             scalar1=colsr[:, 4:5], scalar2=None,
                                             op0=OP.is_equal)
                        VE.raw.drain()
                        t_G = vv.mark(VE.raw.tensor_scalar(
                            out=G[:, i, :], in0=lgt2[:], scalar1=colsr[:, 3:4],
                            scalar2=None, op0=OP.mult))
                    if "G" in dbg_aps:
                        final_outs.append(dma(SY, sd, dbg_aps["G"], G[:], deps=(t_G,)))

                    t_w1use = [None, None]
                    t_w2use = [None, None]
                    t_b2use = [None, None]
                    t_h1use = mm.tok()
                    for e in range(8):
                        sl = e % 2
                        t_b2 = dma(GE, b2ss[sl], b2s[0:1, sl:sl + 1, :],
                                   b2_0[e:e + 1, :].rearrange("(o e) d -> o e d", o=1),
                                   deps=(t_b2use[sl],))
                        for qq in range(4):
                            s = (e * 4 + qq) % 2
                            t_w1 = dma(GE, moes[s], W1q[:, s, :, :],
                                       w1_0[e].rearrange("(a p) f -> p a f", p=128)
                                       [:, :, qq * 1024:(qq + 1) * 1024],
                                       deps=(t_w1use[s],))
                            t_w2 = dma(GE, moes[s], W2q[:, s, :, :],
                                       w2_0[e].rearrange("(b p) d -> p b d", p=128)
                                       [:, qq * 8:(qq + 1) * 8, :],
                                       deps=(t_w2use[s],))
                            for tq in range(max(1, NCH)):
                                for ft in range(8):
                                    bk = ft % 2
                                    use_bank(bk)
                                    TE.wait(t_w1)
                                    for a in range(DT):
                                        t_p1 = mm.mark(TE.raw.matmul(
                                            PS[bk], lhsT=W1q[:, s, a, ft * 128:(ft + 1) * 128],
                                            rhs=xnT[:, a, tq * 512:(tq + 1) * 512],
                                            start=(a == 0), stop=(a == DT - 1)))
                                    SE.wait(t_p1, t_h1use)
                                    bank_free[bk] = sc.mark(SE.raw.activation(
                                        h1[:, ft, :], PS[bk], ACT.Relu,
                                        bias=b1c0[:, e, qq * 8 + ft:qq * 8 + ft + 1]))
                                t_h1 = sc.tok()
                                VE.raw.drain()
                                for dch in range(2):
                                    for ts_ in range(4):
                                        bk = 2 + (dch * 4 + ts_) % 4
                                        use_bank(bk)
                                        TE.wait(t_h1, t_w2, t_G, t_b2)
                                        for ft in range(8):
                                            ins = TE.raw.matmul(
                                                PS[bk],
                                                lhsT=h1[:, ft, ts_ * 128:(ts_ + 1) * 128],
                                                rhs=W2q[:, s, ft, dch * 512:(dch + 1) * 512],
                                                start=(ft == 0),
                                                stop=(ft == 7 and qq != 0))
                                            if ft == 7 and qq == 0:
                                                ins = TE.raw.matmul(
                                                    PS[bk], lhsT=ones_row_bf[:],
                                                    rhs=b2s[0:1, sl, dch * 512:(dch + 1) * 512],
                                                    start=False, stop=True)
                                        t_p2 = mm.mark(ins)
                                        VE.wait(t_p2)
                                        aslc = acc[:, tq * 4 + ts_, dch * 512:(dch + 1) * 512]
                                        bank_free[bk] = vv.mark(VE.raw.scalar_tensor_tensor(
                                            out=aslc, in0=PS[bk],
                                            scalar=G[:, tq * 4 + ts_, e:e + 1],
                                            in1=aslc, op0=OP.mult, op1=OP.add))
                                t_h1use = mm.tok()
                            t_w1use[s] = mm.tok()
                            t_w2use[s] = t_w1use[s]
                            if tern_gen is not None:
                                for _ in range(3):
                                    if next(tern_gen, "END") == "END":
                                        tern_gen = None
                                        break
                        t_b2use[sl] = mm.tok()
                    if tern_gen is not None:
                        for _ in tern_gen:
                            pass
                    barrier()

            def tern_steps():
                st = ExitStack()
                wbuf = sbl(st, "wbuf", (128, 2, 2048), F32)
                redc = sbl(st, "redc", (128, 52), F32)
                thr = sbl(st, "thr", (1, 4), F32)
                thrc = sbl(st, "thrc", (128, 4), F32)
                ttmp = sbl(st, "ttmp", (128, 4, 128), BF16)
                buf1 = sbl(st, "buf1", (128, 8, 256), BF16)
                buf2 = sbl(st, "buf2", (128, 4, 512), BF16)
                slot_used = [None, None]
                t_thrc = None
                tern_ctr = [0]

                def ternarize(ps_ap, dst_ap, thc, deps):
                    VE.wait(t_thrc, *deps)
                    sl_ = tern_ctr[0] % 4
                    if sl_ == 0:
                        VE.raw.drain()
                    tern_ctr[0] += 1
                    VE.raw.tensor_scalar(out=ttmp[:, sl_, :], in0=ps_ap,
                                         scalar1=thrc[:, thc + 1:thc + 2],
                                         scalar2=None, op0=OP.is_lt)
                    VE.raw.drain()
                    return vv.mark(VE.raw.scalar_tensor_tensor(
                        out=dst_ap, in0=ps_ap, scalar=thrc[:, thc:thc + 1],
                        in1=ttmp[:, sl_, :], op0=OP.is_gt, op1=OP.subtract))

                for wi in range(2):
                    nld = 32 if wi == 0 else 16
                    for t in range(nld):
                        s = t % 2
                        if wi == 0:
                            srcap = w1_1[t * 128:(t + 1) * 128, :]
                            cdst = redc[:, t:t + 1]
                        else:
                            srcap = w2_1[(t // 2) * 128:(t // 2 + 1) * 128,
                                         (t % 2) * 2048:(t % 2 + 1) * 2048]
                            cdst = redc[:, 32 + t:33 + t]
                        rl = 1024 if wi == 0 else 2048
                        t_l = dma(SY, tls[s], wbuf[:, s, 0:rl], srcap,
                                  deps=(slot_used[s],))
                        VE.wait(t_l)
                        slot_used[s] = vv.mark(VE.raw.tensor_reduce(
                            cdst, wbuf[:, s, 0:rl], AX, OP.add,
                            apply_absolute_value=True))
                        if t % 4 == 3:
                            yield
                    VE.raw.drain()
                    csrc = redc[:, 0:32] if wi == 0 else redc[:, 32:48]
                    rcol = redc[:, 48 + wi:49 + wi]
                    t_red = vv.mark(VE.raw.tensor_reduce(rcol, csrc, AX, OP.add))
                    use_bank(6)
                    TE.wait(t_red)
                    t_ps = mm.mark(TE.raw.matmul(PS[6][0:1, 0:1], lhsT=rcol,
                                                 rhs=consts["ones_col_f32"][:],
                                                 start=True, stop=True))
                    VE.wait(t_ps)
                    nelem = float(1024 * 4096)
                    VE.raw.tensor_scalar(out=thr[0:1, 0:1], in0=PS[6][0:1, 0:1],
                                         scalar1=1.0 / nelem, scalar2=1e-8,
                                         op0=OP.mult, op1=OP.max)
                    VE.raw.drain()
                    VE.raw.tensor_scalar(out=thr[0:1, 1:2], in0=thr[0:1, 0:1],
                                         scalar1=0.5, scalar2=None, op0=OP.mult)
                    VE.raw.drain()
                    t_thr = vv.mark(VE.raw.tensor_scalar(
                        out=thr[0:1, 2:3], in0=thr[0:1, 1:2],
                        scalar1=-1.0, scalar2=None, op0=OP.mult))
                    bank_free[6] = t_thr
                    use_bank(6)
                    TE.wait(t_thr)
                    t_bc = mm.mark(TE.raw.matmul(PS[6][:, 0:2],
                                                 lhsT=ones_row_f32[:],
                                                 rhs=thr[0:1, 1:3], start=True, stop=True))
                    VE.wait(t_bc)
                    bank_free[6] = vv.mark(VE.raw.tensor_copy(
                        thrc[:, 2 * wi:2 * wi + 2], PS[6][:, 0:2]))
                    t_thrc = vv.tok()
                    yield

                t_st = None
                for fb in range(16):
                    s = fb % 2
                    t_l = dma(SY, tls[s],
                              wbuf[:, s, 0:2048].rearrange("p (t d) -> p t d", t=2),
                              w1_1.rearrange("(t p) d -> p t d", p=128)
                              [:, fb * 2:(fb + 1) * 2, :],
                              deps=(slot_used[s],))
                    for ftl in range(2):
                        for a in range(DT):
                            bk = 6 + (ftl * 8 + a) % 2
                            use_bank(bk)
                            TE.wait(t_l, tk["consts"])
                            tr = mm.mark(TE.raw.transpose(
                                PS[bk][:, 0:128],
                                wbuf[:, s, ftl * 1024 + a * 128:ftl * 1024 + (a + 1) * 128],
                                ident32[:]))
                            tt_ = ternarize(PS[bk][:, 0:128],
                                            buf1[:, a, ftl * 128:(ftl + 1) * 128], 0,
                                            (tr, t_st))
                            bank_free[bk] = tt_
                    slot_used[s] = tt_
                    t_st = dma(SY, tst, w1t_sc[:, :, fb * 256:(fb + 1) * 256],
                               buf1[:], deps=(tt_,))
                    yield
                for ftb in range(8):
                    for dh in range(2):
                        for dc2 in range(4):
                            dc = dh * 4 + dc2
                            s = dc % 2
                            t_l = dma(SY, tls[s], wbuf[:, s, 0:512],
                                      w2_1.rearrange("(dc p) f -> p dc f", p=128)
                                      [:, dc, ftb * 512:(ftb + 1) * 512],
                                      deps=(slot_used[s],))
                            for fl in range(4):
                                bk = 6 + (dc * 4 + fl) % 2
                                use_bank(bk)
                                TE.wait(t_l, tk["consts"])
                                tr = mm.mark(TE.raw.transpose(
                                    PS[bk][:, 0:128],
                                    wbuf[:, s, fl * 128:(fl + 1) * 128], ident32[:]))
                                tt_ = ternarize(PS[bk][:, 0:128],
                                                buf2[:, fl, dc2 * 128:(dc2 + 1) * 128],
                                                2, (tr, t_st))
                                bank_free[bk] = tt_
                            slot_used[s] = tt_
                        t_st = dma(SY, tst,
                                   w2t_sc[:, ftb * 4:(ftb + 1) * 4,
                                          dh * 512:(dh + 1) * 512],
                                   buf2[:], deps=(tt_,))
                        yield
                tk["tern"] = t_st
                st.close()

            def l1_attn():
                st_l1 = ExitStack()
                qrotT = sbl(st_l1, "qrotT", (128, 8, T), BF16)
                krotT = sbl(st_l1, "krotT", (128, 8, T + VKV), BF16)
                v_ext = sbl(st_l1, "v_ext", (128, NT + 1, 32, 65), BF16)
                with st_l1:
                    with ExitStack() as st:
                        sincos = sbl(st, "sincos", (128, 2, T), BF16)
                        Wb = sbl(st, "Wqk", (128, DT, D), BF16)
                        qb = sbl(st, "qb", (128, 512), BF16)
                        rt1 = sbl(st, "rt1", (128, 512), BF16)
                        rt2 = sbl(st, "rt2", (128, 512), BF16)
                        tk["sc"] = dma(GE, gd, sincos[:, 0, :], sin_in)
                        tk["sc"] = dma(GE, gd, sincos[:, 1, :], cos_in)
                        t_W = None
                        t_Wuse = None
                        t_qb_used = None
                        for which in range(2):
                            t_W = dma(GE, gw, Wb[:], (wqf1 if which == 0 else wkf1)
                                      .rearrange("(a p) m -> p a m", p=128), deps=(t_Wuse,))
                            dstT = qrotT if which == 0 else krotT
                            for a2 in range(8):
                                for cch in range(NCH):
                                    bk = (a2 * NCH + cch) % 2
                                    use_bank(bk)
                                    TE.wait(t_W)
                                    for a in range(DT):
                                        t_pq = mm.mark(TE.raw.matmul(
                                            PS[bk], lhsT=Wb[:, a, a2 * 128:(a2 + 1) * 128],
                                            rhs=xnT[:, a, cch * 512:(cch + 1) * 512],
                                            start=(a == 0), stop=(a == DT - 1)))
                                    VE.wait(t_pq, t_qb_used)
                                    VE.raw.drain()
                                    if which == 0:
                                        t_evq = vv.mark(VE.raw.tensor_scalar(
                                            out=qb[:], in0=PS[bk],
                                            scalar1=bq_c[:, a2:a2 + 1], scalar2=L1QSC,
                                            op0=OP.add, op1=OP.mult))
                                    else:
                                        t_evq = vv.mark(VE.raw.tensor_scalar(
                                            out=qb[:], in0=PS[bk],
                                            scalar1=bk_c[:, a2:a2 + 1], scalar2=None,
                                            op0=OP.add))
                                    bank_free[bk] = t_evq
                                    bk2 = 2 + (a2 * NCH + cch) % 2
                                    use_bank(bk2)
                                    TE.wait(t_evq, tk["consts"])
                                    t_fl = mm.mark(TE.raw.matmul(
                                        PS[bk2], lhsT=consts["pflip"][:], rhs=qb[:],
                                        start=True, stop=True))
                                    VE.wait(t_fl, tk["sc"])
                                    VE.raw.drain()
                                    VE.raw.tensor_tensor(
                                        out=rt1[:], in0=qb[:],
                                        in1=sincos[:, 1, cch * 512:(cch + 1) * 512], op=OP.mult)
                                    VE.raw.tensor_tensor(
                                        out=rt2[:], in0=PS[bk2],
                                        in1=sincos[:, 0, cch * 512:(cch + 1) * 512], op=OP.mult)
                                    VE.raw.drain()
                                    t_rot = vv.mark(VE.raw.tensor_tensor(
                                        out=dstT[:, a2, cch * 512:(cch + 1) * 512],
                                        in0=rt1[:], in1=rt2[:], op=OP.add))
                                    bank_free[bk2] = t_rot
                                    t_qb_used = t_rot
                            t_Wuse = mm.tok()
                        tk["rotdone"] = vv.tok()
                        tk["vk"] = dma(GE, gd, krotT[:, :, T:T + VKV],
                                       vk1.rearrange("(a p) m -> p a m", p=128),
                                       allow_slow_non_contiguous=True)
                        barrier()

                    with ExitStack() as st:
                        WVb = sbl(st, "WVb", (128, DT, 1024), BF16)
                        bv_r = sbl(st, "bv_r", (1, 2048), BF16)
                        t_bv = dma(GE, gd, bv_r[:], bv1)
                        t_vz = vv.mark(VE.raw.memset(v_ext[:], 0.0))
                        VE.raw.drain()
                        t_vo = vv.mark(VE.raw.memset(v_ext[:, :, :, 64:65], 1.0))
                        tk["vv"] = dma(GE, gd, v_ext[0:VKV, NT, :, 0:64],
                                       vv1_in.rearrange("v (h r) -> v h r", h=32),
                                       deps=(t_vz, t_vo))
                        t_WVuse = None
                        for half in range(2):
                            t_WV = dma(GE, gw, WVb[:],
                                       wvf1.rearrange("(a p) m -> p a m", p=128)
                                       [:, :, half * 1024:(half + 1) * 1024],
                                       deps=(t_WVuse,))
                            for kvt in range(NT):
                                for hc2 in range(2):
                                    hc = half * 2 + hc2
                                    bk = 4 + (kvt * 2 + hc2) % 4
                                    use_bank(bk)
                                    TE.wait(t_WV, t_bv)
                                    for a in range(DT):
                                        TE.raw.matmul(
                                            PS[bk], lhsT=xnT[:, a, kvt * 128:(kvt + 1) * 128],
                                            rhs=WVb[:, a, hc2 * 512:(hc2 + 1) * 512],
                                            start=(a == 0), stop=False)
                                    t_pv = mm.mark(TE.raw.matmul(
                                        PS[bk], lhsT=ones_row_bf[:],
                                        rhs=bv_r[0:1, hc * 512:(hc + 1) * 512],
                                        start=False, stop=True))
                                    VE.wait(t_pv, t_vo)
                                    bank_free[bk] = vv.mark(VE.raw.tensor_copy(
                                        v_ext[:, kvt, hc * 8:(hc + 1) * 8, 0:64],
                                        PS[bk].rearrange("p (h r) -> p h r", h=8)))
                            t_WVuse = mm.tok()
                        barrier()

                    if "qrot" in dbg_aps:
                        final_outs.append(dma(SY, sd, dbg_aps["qrot"], qrotT[:]))
                    if "krot" in dbg_aps:
                        final_outs.append(dma(SY, sd, dbg_aps["krot"], krotT[:]))
                    if "vext" in dbg_aps:
                        final_outs.append(dma(SY, sd, dbg_aps["vext"],
                                              v_ext[:].rearrange("p a b c -> p (a b c)")))

                    tk["xnspill"] = dma(SY, sd, xnsp, xnT[:])
                    barrier()
                    st_xn.close()

                    with ExitStack() as st:
                        zT = sbl(st, "zT", (128, 16, 512), BF16)
                        expT = sbl(st, "expT", (128, 8, 512), BF16)
                        denM = sbl(st, "denM", (32, 512), F32)
                        recMb = sbl(st, "recMb", (32, 512), BF16)
                        den_stg = sbl(st, "den_stg", (1, 2, 512), F32)
                        sel_sb = sbl(st, "sel_sb", (32, 2048), BF16)
                        WOb = sbl(st, "WOb", (128, 16, 512), BF16)
                        stg = sbl(st, "stg", (128, 2, 512), F32)
                        shcol = sbl(st, "shcol", (128, 1), F32)
                        s_masks = sbl(st, "s_masks", (128, 5, 512), BF16)
                        t_sh = vv.mark(VE.raw.memset(shcol[:], EXP_SHIFT))
                        SE.wait(t_sh)
                        t_mskld = dma(SY, sd, s_masks[:], masks_in.rearrange("m p q -> p m q"))
                        t_selld = dma(SY, sd, sel_sb[:], sel_in)
                        wof_r = wof1.rearrange("(vc p) d -> p vc d", p=128)
                        t_wob_used = None
                        t_stg_used = [None, None]
                        t_zT_used = None
                        t_dstg_used = [None, None]
                        t_exp_used = [[mm.tok(), mm.tok()] for _ in range(4)]
                        for cch in range(NCH):
                            full = [kvt for kvt in range(NT) if kvt * 128 <= cch * 512 + 527]
                            has_virt = T <= cch * 512 + 527
                            nsteps = len(full) + (1 if has_virt else 0)
                            t_zT = []
                            t_den_dmas = []
                            t_wob = dma(GE, gw, WOb[:], wof_r[:, :, 0:512],
                                        deps=(t_wob_used,))
                            for a2 in range(8):
                                t_e = [[None, None] for _ in range(4)]
                                t_z = [None] * 4
                                for hh in range(4):
                                    use_bank(4 + hh)
                                for p in range(nsteps + 1):
                                    if p < nsteps:
                                        virt = p >= len(full)
                                        kvt = full[p] if not virt else NT
                                        rows = 128 if not virt else VKV
                                        delta = (kvt * 128 if not virt else T) - cch * 512
                                        sl = p % 2
                                        for hh in range(4):
                                            hb = 32 * hh
                                            use_bank(hh)
                                            TE.wait(tk["rotdone"], tk["vk"])
                                            t_s = mm.mark(TE.raw.matmul(
                                                PS[hh][0:rows, :],
                                                lhsT=krotT[hb:hb + 32, a2,
                                                           kvt * 128:kvt * 128 + rows],
                                                rhs=qrotT[hb:hb + 32, a2,
                                                          cch * 512:(cch + 1) * 512],
                                                start=True, stop=True,
                                                tile_position=(hb, 0)))
                                            SE.wait(t_s, t_exp_used[hh][sl])
                                            t_ex = sc.mark(SE.raw.activation(
                                                expT[0:rows, 2 * hh + sl, :],
                                                PS[hh][0:rows, :],
                                                ACT.Exp, bias=shcol[0:rows, :]))
                                            bank_free[hh] = t_ex
                                            if 0 <= delta <= 512:
                                                VE.wait(t_ex, t_mskld)
                                                t_ex = vv.mark(VE.raw.tensor_tensor(
                                                    out=expT[0:rows, 2 * hh + sl, :],
                                                    in0=expT[0:rows, 2 * hh + sl, :],
                                                    in1=s_masks[0:rows, delta // 128, :],
                                                    op=OP.mult))
                                            t_e[hh][sl] = t_ex
                                    if p >= 1:
                                        pz = p - 1
                                        virt = pz >= len(full)
                                        kvt = full[pz] if not virt else NT
                                        rows = 128 if not virt else VKV
                                        sl = pz % 2
                                        for hh in range(4):
                                            h = a2 * 4 + hh
                                            TE.wait(t_e[hh][sl], tk["vv"], t_zT_used)
                                            t_z[hh] = mm.mark(TE.raw.matmul(
                                                PS[4 + hh][0:65, :],
                                                lhsT=v_ext[0:rows, kvt, h, :],
                                                rhs=expT[0:rows, 2 * hh + sl, :],
                                                start=(pz == 0), stop=(pz == nsteps - 1)))
                                            t_exp_used[hh][sl] = t_z[hh]
                                for hh in range(4):
                                    h = a2 * 4 + hh
                                    zoff, vc = 64 * (h % 2), h // 2
                                    sl = hh % 2
                                    VE.wait(t_z[hh], t_dstg_used[sl])
                                    VE.raw.drain()
                                    VE.raw.tensor_copy(den_stg[0:1, sl, :],
                                                       PS[4 + hh][64:65, :])
                                    VE.raw.drain()
                                    t_zc = vv.mark(VE.raw.tensor_copy(
                                        zT[zoff:zoff + 64, vc, :], PS[4 + hh][0:64, :]))
                                    bank_free[4 + hh] = t_zc
                                    t_dd = dma(SY, sd, denM[h:h + 1, :],
                                               den_stg[0:1, sl, :], deps=(t_zc,))
                                    t_dstg_used[sl] = t_dd
                                    t_den_dmas.append(t_dd)
                            VE.wait(*t_den_dmas)
                            VE.raw.drain()
                            VE.raw.reciprocal(denM[:], denM[:])
                            VE.raw.drain()
                            t_rcb = vv.mark(VE.raw.tensor_copy(recMb[:], denM[:]))
                            for h in range(32):
                                zoff, vc = 64 * (h % 2), h // 2
                                bk = h % 8
                                use_bank(bk)
                                TE.wait(t_rcb, t_selld)
                                t_rb = mm.mark(TE.raw.matmul(
                                    PS[bk][0:64, :],
                                    lhsT=sel_sb[:, h * 64:(h + 1) * 64],
                                    rhs=recMb[:], start=True, stop=True))
                                VE.wait(t_rb)
                                t_zn = vv.mark(VE.raw.tensor_tensor(
                                    out=zT[zoff:zoff + 64, vc, :],
                                    in0=zT[zoff:zoff + 64, vc, :],
                                    in1=PS[bk][0:64, :], op=OP.mult))
                                bank_free[bk] = t_zn
                                t_zT.append(t_zn)
                            for dch in range(2):
                                if dch == 1:
                                    t_wob = dma(GE, gw, WOb[:], wof_r[:, :, 512:1024],
                                                deps=(t_wob_used,))
                                for qt in range(4):
                                    bk = (qt + dch * 2) % 4
                                    use_bank(bk)
                                    TE.wait(t_zT, t_wob)
                                    for vc in range(16):
                                        t_wo = mm.mark(TE.raw.matmul(
                                            PS[bk], lhsT=zT[:, vc, qt * 128:(qt + 1) * 128],
                                            rhs=WOb[:, vc, :],
                                            start=(vc == 0), stop=(vc == 15)))
                                    sp = qt % 2
                                    SE.wait(t_wo, t_stg_used[sp])
                                    t_sg = sc.mark(SE.raw.copy(stg[:, sp, :], PS[bk]))
                                    bank_free[bk] = t_sg
                                    r0 = cch * 512 + qt * 128
                                    t_acd = dma(GE, accs[sp],
                                                o_out[r0:r0 + 128, dch * 512:(dch + 1) * 512],
                                                stg[:, sp, :], deps=(t_sg, tk["spill"]),
                                                accum_op=OP.add)
                                    t_stg_used[sp] = t_acd
                                    final_outs.append(t_acd)
                                t_wob_used = mm.tok()
                            t_zT_used = sc.tok()
                    barrier()

            def l1_ffn():
                with ExitStack() as st:
                    xnb = sbl(st, "xnb", (128, DT, T), BF16)
                    b2r1 = sbl(st, "b2r1", (1, D), BF16)
                    t_b2f = dma(GE, gd, b2r1[:], b2_1)
                    w1tb = sbl(st, "w1tb", (128, DT, 4096), BF16)
                    w2tb = sbl(st, "w2tb", (128, 2, 32, 512), BF16)
                    hbuf = sbl(st, "hbuf", (128, 32, 512), BF16)
                    stg = sbl(st, "stg", (128, 2, 512), F32)
                    t_xn = dma(SY, sd, xnb[:], xnsp, deps=(tk["xnspill"],))
                    t_w1t = dma(GE, ffws[0], w1tb[:], w1t_sc, deps=(tk["tern"],))
                    t_w2use = [None, None]
                    t_stg_used = [None, None]
                    t_huse = mm.tok()
                    for tq in range(max(1, NCH)):
                        for dch in range(2):
                            ws = dch
                            t_w2t = dma(GE, ffws[1 + dch], w2tb[:, ws, :, :],
                                        w2t_sc[:, :, dch * 512:(dch + 1) * 512],
                                        deps=(t_w2use[ws], tk["tern"]))
                            if dch == 0:
                                for fc in range(32):
                                    bk = fc % 2
                                    use_bank(bk)
                                    TE.wait(t_w1t, t_xn)
                                    for a in range(DT):
                                        t_p1 = mm.mark(TE.raw.matmul(
                                            PS[bk], lhsT=w1tb[:, a, fc * 128:(fc + 1) * 128],
                                            rhs=xnb[:, a, tq * 512:(tq + 1) * 512],
                                            start=(a == 0), stop=(a == DT - 1)))
                                    SE.wait(t_p1, t_huse)
                                    bank_free[bk] = sc.mark(SE.raw.activation(
                                        hbuf[:, fc, :], PS[bk], ACT.Gelu,
                                        bias=b1c1[:, fc:fc + 1]))
                                t_h = sc.tok()
                            for qt in range(4):
                                bk = 2 + (qt % 4)
                                use_bank(bk)
                                TE.wait(t_h, t_w2t, t_b2f)
                                for fc in range(32):
                                    TE.raw.matmul(
                                        PS[bk], lhsT=hbuf[:, fc, qt * 128:(qt + 1) * 128],
                                        rhs=w2tb[:, ws, fc, :],
                                        start=(fc == 0), stop=False)
                                t_p2 = mm.mark(TE.raw.matmul(
                                    PS[bk], lhsT=ones_row_bf[:],
                                    rhs=b2r1[0:1, dch * 512:(dch + 1) * 512],
                                    start=False, stop=True))
                                sp = qt % 2
                                SE.wait(t_p2, t_stg_used[sp])
                                t_sg = sc.mark(SE.raw.copy(stg[:, sp, :], PS[bk]))
                                bank_free[bk] = t_sg
                                r0 = tq * 512 + qt * 128
                                t_acd = dma(GE, accs[sp],
                                            o_out[r0:r0 + 128, dch * 512:(dch + 1) * 512],
                                            stg[:, sp, :], deps=(t_sg, tk["spill"]),
                                            accum_op=OP.add)
                                t_stg_used[sp] = t_acd
                                final_outs.append(t_acd)
                            t_w2use[ws] = mm.tok()
                        t_huse = mm.tok()
                    barrier()

            st_S = ExitStack()
            S_full = st_S.enter_context(
                nc.sbuf_tensor("S_full", [128, T], F32, side="right"))
            layernorm_phase(True, S_full)
            if "xnT" in dbg_aps and upto < 3:
                final_outs.append(dma(SY, sd, dbg_aps["xnT"], xnT[:]))
            if upto >= 2:
                l0_attn(S_full, st_S)
            else:
                st_S.close()
            if "acc1" in dbg_aps:
                final_outs.append(dma(SY, sd, dbg_aps["acc1"], acc[:],
                                      deps=(vv.tok(), mm.tok(), sc.tok())))
            if upto >= 3:
                l0_ffn(tern_steps() if upto >= 4 else None)
            if "acc2" in dbg_aps:
                final_outs.append(dma(SY, sd, dbg_aps["acc2"], acc[:],
                                      deps=(vv.tok(), mm.tok(), sc.tok())))
            if upto >= 5:
                layernorm_phase(False, None)
                if "xnT" in dbg_aps:
                    final_outs.append(dma(SY, sd, dbg_aps["xnT"], xnT[:]))
                tk["spill"] = dma(SY, sd, o_out.rearrange("(i p) d -> p i d", p=128),
                                  acc[:], deps=(vv.tok(), mm.tok()))
                final_outs.append(tk["spill"])
                barrier()
                st_acc.close()
            if upto >= 6:
                l1_attn()
            if upto >= 7:
                l1_ffn()
            if upto < 5:
                tk["spill"] = dma(SY, sd, o_out.rearrange("(i p) d -> p i d", p=128),
                                  acc[:], deps=(vv.tok(), mm.tok(), sc.tok()))
                final_outs.append(tk["spill"])
            SY.wait(*[s.tok() for s in ALLSEMS])
            SY.wait(*final_outs)

    return nc


def host_inputs(inputs, T=2048):
    f32 = np.float32
    c = host_constants(T)
    WQ = np.ascontiguousarray(
        inputs["L1_attn_W_Q"].reshape(8, 4, 1024, 32).transpose(2, 0, 1, 3).reshape(1024, 1024))
    WK = np.ascontiguousarray(
        inputs["L1_attn_W_K"].reshape(8, 4, 1024, 32).transpose(2, 0, 1, 3).reshape(1024, 1024))
    WV = np.ascontiguousarray(inputs["L1_attn_W_V"][:, :, 0].T)
    WO = np.ascontiguousarray(inputs["L1_attn_W_O"][:, 0, :])
    bQ = np.ascontiguousarray(inputs["L1_attn_b_Q"].reshape(8, 4, 32).reshape(8, 128))
    bK = np.ascontiguousarray(inputs["L1_attn_b_K"].reshape(8, 4, 32).reshape(8, 128))
    bV = np.ascontiguousarray(inputs["L1_attn_b_V"][:, 0][None, :])
    vk = np.ascontiguousarray(
        inputs["L1_attn_virtual_k"].reshape(16, 8, 4, 32).transpose(1, 2, 3, 0).reshape(1024, 16))
    vvv = np.ascontiguousarray(inputs["L1_attn_virtual_v"][:, :, 0])
    base = {
        "L0_gamma": np.ascontiguousarray(inputs["L0_attn_gamma"].reshape(8, 128), dtype=f32),
        "L0_w_q": np.ascontiguousarray(inputs["L0_attn_w_q"], dtype=f32),
        "L0_w_kv": np.ascontiguousarray(inputs["L0_attn_w_kv"], dtype=f32),
        "L0_w_out": np.ascontiguousarray(inputs["L0_attn_w_out"], dtype=f32),
        "L0_WrT": np.ascontiguousarray(inputs["L0_ffn_Wr"].T, dtype=f32),
        "L0_br": np.ascontiguousarray(inputs["L0_ffn_br"][None, :], dtype=f32),
        "L0_W1": np.ascontiguousarray(inputs["L0_ffn_W1"], dtype=f32),
        "L0_b1": np.ascontiguousarray(inputs["L0_ffn_b1"], dtype=f32),
        "L0_W2": np.ascontiguousarray(inputs["L0_ffn_W2"], dtype=f32),
        "L0_b2": np.ascontiguousarray(inputs["L0_ffn_b2"], dtype=f32),
        "L1_WQ": WQ.astype(f32), "L1_WK": WK.astype(f32),
        "L1_WV": WV.astype(f32), "L1_WO": WO.astype(f32),
        "L1_bQ": bQ.astype(f32), "L1_bK": bK.astype(f32), "L1_bV": bV.astype(f32),
        "L1_vk": vk.astype(f32), "L1_vv": vvv.astype(f32),
        "L1_W1": np.ascontiguousarray(inputs["L1_ffn_W1"], dtype=f32),
        "L1_b1": np.ascontiguousarray(inputs["L1_ffn_b1"].reshape(32, 128), dtype=f32),
        "L1_W2": np.ascontiguousarray(inputs["L1_ffn_W2"], dtype=f32),
        "L1_b2": np.ascontiguousarray(inputs["L1_ffn_b2"][None, :], dtype=f32),
        "c_sin": c["sin"][:, :T], "c_cos": c["cos"][:, :T],
        "c_masks": np.ascontiguousarray(c["masks"]),
    }
    for nm in CSHAPES:
        base["c_" + nm] = np.ascontiguousarray(c[nm])
    base["c_sel32"] = np.ascontiguousarray(c["sel32"])
    x = np.asarray(inputs["L0_attn_x"], dtype=f32)
    in_maps = []
    for i in range(8):
        m = dict(base)
        m["x"] = np.ascontiguousarray(x[i, :T, :])
        in_maps.append(m)
    return in_maps


def kernel(**inputs):
    T = 2048
    nc = build(T=T)
    in_maps = host_inputs(inputs, T=T)
    res = run_bass_kernel_spmd(nc, in_maps, core_ids=list(range(8)))
    out = np.stack([np.asarray(r["o"], dtype=np.float32) for r in res.results], axis=0)
    return out


# version 10
# speedup vs baseline: 1.0020x; 1.0020x over previous
import numpy as np
import ml_dtypes
from contextlib import ExitStack

import concourse.bass as bass
import concourse.mybir as mybir
from concourse.bass_utils import run_bass_kernel_spmd

F32 = mybir.dt.float32
BF16 = mybir.dt.bfloat16
AX = mybir.AxisListType.X
OP = mybir.AluOpType
ACT = mybir.ActivationFunctionType

D = 1024
DT = 8
VKV = 16
ROT_BASE = 10000.0
EXP_SHIFT = -20.0
QSC = 0.25
TAYSC = 2.0 ** -0.25
L1QSC = 32.0 ** -0.5
bfdt = ml_dtypes.bfloat16


def host_constants(T):
    c = {}
    c["ident32"] = np.eye(128, dtype=np.float32)
    c["ident16"] = np.eye(128, dtype=np.float32).astype(bfdt)
    c["ones_row_bf"] = np.ones((1, 128), dtype=np.float32).astype(bfdt)
    c["ones_row_f32"] = np.ones((1, 128), dtype=np.float32)
    c["ones_col_f32"] = np.ones((128, 1), dtype=np.float32)
    pos = np.arange(T, dtype=np.float64)
    freq = ROT_BASE ** (np.arange(16, dtype=np.float64) / 16.0)
    ang = pos[None, :] / freq[(np.arange(128) % 16)][:, None]
    c["sin"] = np.sin(ang).astype(np.float32)
    c["cos"] = np.cos(ang).astype(np.float32)
    pf = np.zeros((128, 128), dtype=np.float32)
    for m in range(128):
        if m % 32 < 16:
            pf[m + 16, m] = -1.0
        else:
            pf[m - 16, m] = 1.0
    c["pflip"] = pf.astype(bfdt)
    kvl = np.arange(128)[:, None]
    ql = np.arange(512)[None, :]
    c["masks"] = np.stack(
        [(kvl - ql <= 16 - 128 * dd).astype(np.float32) for dd in range(5)]
    ).astype(bfdt)
    sel = np.zeros((32, 2048), dtype=np.float32)
    for h in range(32):
        sel[h, h * 64:(h + 1) * 64] = 1.0
    c["sel32"] = sel.astype(bfdt)
    c["iota8"] = np.broadcast_to(np.arange(8, dtype=np.float32), (128, 8)).copy()
    c["iota8big"] = (c["iota8"] + 1e6).astype(np.float32)
    return c


CSHAPES = dict(ident32=(128, 128, F32), ident16=(128, 128, BF16),
               ones_row_bf=(1, 128, BF16), ones_row_f32=(1, 128, F32),
               ones_col_f32=(128, 1, F32), pflip=(128, 128, BF16),
               iota8=(128, 8, F32), iota8big=(128, 8, F32))


class Sem:
    def __init__(self, es, nc, name, dma=False):
        self.h = es.enter_context(nc.semaphore(name))
        self.v = 0
        self.name = name
        self.dma = dma

    def mark(self, instr, n=1):
        instr.then_inc(self.h, n)
        self.v += n
        return (self, self.v)

    def tok(self):
        return (self, self.v)


class Eng:
    def __init__(self, raw):
        self.raw = raw
        self.seen = {}

    def wait(self, *deps):
        for d in deps:
            if d is None:
                continue
            if isinstance(d, (list, tuple)) and d and isinstance(d[0], (list, tuple, type(None), Sem)) \
                    and not (len(d) == 2 and isinstance(d[0], Sem)):
                self.wait(*d)
                continue
            sem, val = d
            if sem.dma:
                val = sem.v
            if val > self.seen.get(sem, 0):
                self.raw.wait_ge(sem.h, val)
                self.seen[sem] = val


def build(T=2048, upto=99, debug=(), l0sub=9):
    NT = T // 128
    NCH = T // 512
    nc = bass.Bass("TRN2", target_bir_lowering=False)
    dt = nc.dram_tensor

    x_in = dt("x", [T, D], F32, kind="ExternalInput").ap()
    g0 = dt("L0_gamma", [8, 128], F32, kind="ExternalInput").ap()
    wq0 = dt("L0_w_q", [D, 128], F32, kind="ExternalInput").ap()
    wkv0 = dt("L0_w_kv", [D, 256], F32, kind="ExternalInput").ap()
    wo0 = dt("L0_w_out", [128, D], F32, kind="ExternalInput").ap()
    wr0T = dt("L0_WrT", [8, D], F32, kind="ExternalInput").ap()
    br0 = dt("L0_br", [1, 8], F32, kind="ExternalInput").ap()
    w1_0 = dt("L0_W1", [8, D, 4096], F32, kind="ExternalInput").ap()
    b1_0 = dt("L0_b1", [8, 4096], F32, kind="ExternalInput").ap()
    w2_0 = dt("L0_W2", [8, 4096, D], F32, kind="ExternalInput").ap()
    b2_0 = dt("L0_b2", [8, D], F32, kind="ExternalInput").ap()
    wqf1 = dt("L1_WQ", [D, D], F32, kind="ExternalInput").ap()
    wkf1 = dt("L1_WK", [D, D], F32, kind="ExternalInput").ap()
    wvf1 = dt("L1_WV", [D, 2048], F32, kind="ExternalInput").ap()
    wof1 = dt("L1_WO", [2048, D], F32, kind="ExternalInput").ap()
    bq1 = dt("L1_bQ", [8, 128], F32, kind="ExternalInput").ap()
    bk1 = dt("L1_bK", [8, 128], F32, kind="ExternalInput").ap()
    bv1 = dt("L1_bV", [1, 2048], F32, kind="ExternalInput").ap()
    vk1 = dt("L1_vk", [D, VKV], F32, kind="ExternalInput").ap()
    vv1_in = dt("L1_vv", [VKV, 2048], F32, kind="ExternalInput").ap()
    w1_1 = dt("L1_W1", [4096, D], F32, kind="ExternalInput").ap()
    b1_1 = dt("L1_b1", [32, 128], F32, kind="ExternalInput").ap()
    w2_1 = dt("L1_W2", [D, 4096], F32, kind="ExternalInput").ap()
    b2_1 = dt("L1_b2", [1, D], F32, kind="ExternalInput").ap()
    ci = {nm: dt("c_" + nm, [p_, f_], ty, kind="ExternalInput").ap()
          for nm, (p_, f_, ty) in CSHAPES.items()}
    sel_in = dt("c_sel32", [32, 2048], BF16, kind="ExternalInput").ap()
    sin_in = dt("c_sin", [128, T], F32, kind="ExternalInput").ap()
    cos_in = dt("c_cos", [128, T], F32, kind="ExternalInput").ap()
    masks_in = dt("c_masks", [5, 128, 512], BF16, kind="ExternalInput").ap()

    o_out = dt("o", [T, D], F32, kind="ExternalOutput").ap()
    xnsp = dt("xnspill", [128, DT, T], BF16, kind="Internal").ap()
    w1t_sc = dt("w1t_sc", [128, DT, 4096], BF16, kind="Internal").ap()
    w2t_sc = dt("w2t_sc", [128, 32, D], BF16, kind="Internal").ap()

    dbg_specs = {"xnT": ([128, DT, T], BF16), "srow": ([1, T], F32),
                 "acc1": ([128, NT, D], F32), "acc2": ([128, NT, D], F32),
                 "G": ([128, NT, 8], F32),
                 "qrot": ([128, 8, T], BF16), "krot": ([128, 8, T + VKV], BF16),
                 "vext": ([128, (NT + 1) * 32 * 65], BF16)}
    dbg_aps = {nm: dt("dbg_" + nm, s, ty, kind="ExternalOutput").ap()
               for nm, (s, ty) in dbg_specs.items() if nm in debug}

    es = ExitStack()
    st_acc = ExitStack()
    st_xn = ExitStack()

    _uniq = [0]

    def sbl(st, name, shape, ty):
        _uniq[0] += 1
        return st.enter_context(nc.sbuf_tensor(f"{name}_{_uniq[0]}", list(shape), ty))

    xnT = st_xn.enter_context(nc.sbuf_tensor("xnT", [128, DT, T], BF16, side="right"))
    acc = st_acc.enter_context(nc.sbuf_tensor("acc", [128, NT, D], F32, side="right"))

    consts = {nm: es.enter_context(nc.sbuf_tensor("s_" + nm, [p_, f_], ty))
              for nm, (p_, f_, ty) in CSHAPES.items()}
    gamma_c = es.enter_context(nc.sbuf_tensor("gamma_c", [128, DT], F32))
    wq_b = es.enter_context(nc.sbuf_tensor("wq_b", [128, DT, 128], BF16))
    wkv_b = es.enter_context(nc.sbuf_tensor("wkv_b", [128, DT, 256], BF16))
    wr_b = es.enter_context(nc.sbuf_tensor("wr_b", [128, DT, 8], BF16))
    br_b = es.enter_context(nc.sbuf_tensor("br_b", [1, 8], BF16))
    b1c0 = es.enter_context(nc.sbuf_tensor("b1c0", [128, 8, 32], F32))
    bq_c = es.enter_context(nc.sbuf_tensor("bq_c", [128, 8], F32))
    bk_c = es.enter_context(nc.sbuf_tensor("bk_c", [128, 8], F32))
    b1c1 = es.enter_context(nc.sbuf_tensor("b1c1", [128, 32], F32))
    G = es.enter_context(nc.sbuf_tensor("G", [128, NT, 8], F32))

    pA = es.enter_context(nc.psum_tensor("pA", [128, 1024], F32))
    pB = es.enter_context(nc.psum_tensor("pB", [128, 1024], F32))
    pC = es.enter_context(nc.psum_tensor("pC", [128, 1024], F32))
    pD = es.enter_context(nc.psum_tensor("pD", [128, 1024], F32))
    PS = [pA[:, 0:512], pA[:, 512:1024], pB[:, 0:512], pB[:, 512:1024],
          pC[:, 0:512], pC[:, 512:1024], pD[:, 0:512], pD[:, 512:1024]]
    bank_free = [None] * 8

    sd = Sem(es, nc, "sd", dma=True)
    gd = Sem(es, nc, "gd", dma=True)
    gw = Sem(es, nc, "gw", dma=True)
    moes = [Sem(es, nc, "moe0", dma=True), Sem(es, nc, "moe1", dma=True)]
    tls = [Sem(es, nc, "tl0", dma=True), Sem(es, nc, "tl1", dma=True)]
    accs = [Sem(es, nc, "acc0", dma=True), Sem(es, nc, "acc1", dma=True)]
    ffws = [Sem(es, nc, "ffw1", dma=True), Sem(es, nc, "ffw2", dma=True),
            Sem(es, nc, "ffw3", dma=True)]
    b2ss = [Sem(es, nc, "b20", dma=True), Sem(es, nc, "b21", dma=True)]
    tst = Sem(es, nc, "tst", dma=True)
    mm = Sem(es, nc, "mm")
    vv = Sem(es, nc, "vv")
    sc = Sem(es, nc, "sc")
    ALLSEMS = (sd, gd, gw, moes[0], moes[1], tls[0], tls[1], b2ss[0], b2ss[1],
               accs[0], accs[1], ffws[0], ffws[1], ffws[2], tst, mm, vv, sc)

    ident32 = consts["ident32"]
    ident16 = consts["ident16"]
    ones_row_bf = consts["ones_row_bf"]
    ones_row_f32 = consts["ones_row_f32"]
    tk = {}
    final_outs = []

    with es, nc.Block() as block:
        @block.sync
        def _(sync_unused):
            TE = Eng(nc.tensor)
            VE = Eng(nc.vector)
            SE = Eng(nc.scalar)
            GE = Eng(nc.gpsimd)
            SY = Eng(nc.sync)

            def dma(q, sem, out, in_, deps=(), **kw):
                q.wait(*deps)
                return sem.mark(q.raw.dma_start(out=out, in_=in_, **kw), 16)

            def use_bank(bk):
                TE.wait(bank_free[bk])

            def barrier():
                VE.raw.drain()
                toks = [s.tok() for s in ALLSEMS]
                for E in (TE, VE, SE, GE, SY):
                    E.wait(*toks)

            def small_transpose(src_ap, dst_ap, bk, p_, f_, deps=()):
                use_bank(bk)
                TE.wait(tk.get("consts"), *deps)
                tr = mm.mark(TE.raw.matmul(PS[bk][0:f_, 0:p_], lhsT=src_ap,
                                           rhs=ident32[0:p_, 0:p_], start=True, stop=True))
                VE.wait(tr)
                t = vv.mark(VE.raw.tensor_copy(dst_ap, PS[bk][0:f_, 0:p_]))
                bank_free[bk] = t
                return t

            for nm in CSHAPES:
                tk["consts"] = dma(SY, sd, consts[nm][:], ci[nm])
            tk["x"] = dma(SY, sd, acc[:], x_in.rearrange("(i p) d -> p i d", p=128))

            with ExitStack() as st0:
                graw = sbl(st0, "graw", (8, 128), F32)
                wqf = sbl(st0, "wqf", (128, DT, 384), F32)
                wrT = sbl(st0, "wrT", (8, DT, 128), F32)
                b1row = sbl(st0, "b1row", (32, 8, 128), F32)
                bqrow = sbl(st0, "bqrow", (8, 2, 128), F32)
                b1frow = sbl(st0, "b1frow", (32, 128), F32)

                t_g = dma(GE, gd, graw[:], g0)
                t_wq = dma(GE, gd, wqf[:, :, 0:128], wq0.rearrange("(a p) m -> p a m", p=128))
                t_wkv = dma(GE, gd, wqf[:, :, 128:384], wkv0.rearrange("(a p) m -> p a m", p=128))
                t_wr = dma(GE, gd, wrT[:], wr0T.rearrange("e (a p) -> e a p", p=128))
                t_br = dma(GE, gd, br_b[:], br0)
                t_b1r = dma(GE, gd, b1row[:], b1_0.rearrange("e (c p) -> c e p", p=128),
                            allow_slow_non_contiguous=True)
                t_bq = dma(GE, gd, bqrow[:, 0, :], bq1)
                t_bk = dma(GE, gd, bqrow[:, 1, :], bk1)
                t_b1f = dma(GE, gd, b1frow[:], b1_1)

                small_transpose(graw[:], gamma_c[:], 6, 8, 128, deps=(t_g,))
                t_gc = vv.tok()
                VE.wait(t_wq, t_wkv, t_gc)
                VE.raw.drain()
                for a in range(DT):
                    VE.raw.tensor_scalar(out=wq_b[:, a, :], in0=wqf[:, a, 0:128],
                                         scalar1=gamma_c[:, a:a + 1], scalar2=QSC,
                                         op0=OP.mult, op1=OP.mult)
                    tk["wfold"] = vv.mark(VE.raw.tensor_scalar(
                        out=wkv_b[:, a, :], in0=wqf[:, a, 128:384],
                        scalar1=gamma_c[:, a:a + 1], scalar2=None, op0=OP.mult))
                with ExitStack() as st00:
                    wrtmp = sbl(st00, "wrtmp", (128, DT, 8), F32)
                    for a in range(DT):
                        small_transpose(wrT[:, a, :], wrtmp[:, a, :], a % 2, 8, 128,
                                        deps=(t_wr,))
                    VE.raw.drain()
                    tk["wr"] = vv.mark(VE.raw.tensor_copy(wr_b[:], wrtmp[:]))
                for e in range(8):
                    small_transpose(b1row[:, e, :], b1c0[:, e, :], 2 + e % 2, 32, 128,
                                    deps=(t_b1r,))
                small_transpose(bqrow[:, 0, :], bq_c[:], 4, 8, 128, deps=(t_bq,))
                small_transpose(bqrow[:, 1, :], bk_c[:], 5, 8, 128, deps=(t_bk,))
                small_transpose(b1frow[:], b1c1[:], 6, 32, 128, deps=(t_b1f,))
                tk["smallpar"] = vv.tok()
                barrier()

            def layernorm_phase(want_s, S_full):
                with ExitStack() as st:
                    xtmp = sbl(st, "ln_xtmp", (128, 2, D), F32)
                    sq_t = sbl(st, "ln_sq", (128, 2, D), F32)
                    cols = sbl(st, "ln_cols", (128, 8, NT), F32)
                    s_tok = sbl(st, "ln_stok", (128, NT), F32)
                    s_row = sbl(st, "ln_srow", (1, T), F32)
                    t_sq_rd = [None, None]
                    for i in range(NT):
                        VE.raw.tensor_reduce(cols[:, 0, i:i + 1], acc[:, i, :], AX, OP.add)
                        sl = i % 2
                        SE.wait(t_sq_rd[sl])
                        t_sqw = sc.mark(SE.raw.activation(sq_t[:, sl, :], acc[:, i, :],
                                                          ACT.Square))
                        VE.wait(t_sqw)
                        t_sq_rd[sl] = vv.mark(VE.raw.tensor_reduce(
                            cols[:, 1, i:i + 1], sq_t[:, sl, :], AX, OP.add))
                    VE.raw.drain()
                    VE.raw.tensor_scalar(out=cols[:, 2, :], in0=cols[:, 0, :],
                                         scalar1=-1.0 / D, scalar2=None, op0=OP.mult)
                    VE.raw.tensor_scalar(out=cols[:, 3, :], in0=cols[:, 1, :],
                                         scalar1=1.0 / D, scalar2=None, op0=OP.mult)
                    VE.raw.drain()
                    VE.raw.tensor_tensor(out=cols[:, 4, :], in0=cols[:, 2, :],
                                         in1=cols[:, 2, :], op=OP.mult)
                    VE.raw.drain()
                    VE.raw.tensor_tensor(out=cols[:, 4, :], in0=cols[:, 3, :],
                                         in1=cols[:, 4, :], op=OP.subtract)
                    VE.raw.drain()
                    VE.raw.tensor_scalar(out=cols[:, 5, :], in0=cols[:, 4, :],
                                         scalar1=1e-5, scalar2=None, op0=OP.add)
                    VE.raw.drain()
                    t_rp = vv.mark(VE.raw.reciprocal(cols[:, 6, :], cols[:, 5, :]))
                    SE.wait(t_rp)
                    t_rstd = sc.mark(SE.raw.activation(cols[:, 7, :], cols[:, 6, :],
                                                       ACT.Sqrt))
                    if want_s:
                        VE.wait(t_rstd)
                        VE.raw.tensor_tensor(out=cols[:, 5, :], in0=cols[:, 4, :],
                                             in1=cols[:, 6, :], op=OP.mult)
                        VE.raw.drain()
                        VE.raw.tensor_scalar(out=cols[:, 5, :], in0=cols[:, 5, :],
                                             scalar1=9.765625e-28, scalar2=None, op0=OP.max)
                        VE.raw.drain()
                        t_si = vv.mark(VE.raw.reciprocal(cols[:, 5, :], cols[:, 5, :]))
                        SE.wait(t_si)
                        t_sv = sc.mark(SE.raw.activation(s_tok[:], cols[:, 5, :], ACT.Sqrt))
                    t_xuse = [None, None]
                    for i in range(NT):
                        sl = i % 2
                        VE.wait(t_rstd, t_xuse[sl])
                        VE.raw.drain()
                        t_xn = vv.mark(VE.raw.tensor_scalar(
                            out=xtmp[:, sl, :], in0=acc[:, i, :],
                            scalar1=cols[:, 2, i:i + 1],
                            scalar2=cols[:, 7, i:i + 1], op0=OP.add, op1=OP.mult))
                        TE.wait(t_xn, tk["consts"])
                        for a in range(DT):
                            bk = a % 2
                            use_bank(bk)
                            tr = mm.mark(TE.raw.transpose(
                                PS[bk][:, 0:128], xtmp[:, sl, a * 128:(a + 1) * 128],
                                ident32[:]))
                            VE.wait(tr)
                            bank_free[bk] = vv.mark(VE.raw.tensor_copy(
                                xnT[:, a, i * 128:(i + 1) * 128], PS[bk][:, 0:128]))
                        t_xuse[sl] = tr
                    if want_s:
                        t_stok = sc.tok()
                        for i in range(NT):
                            bk = 2 + i % 2
                            use_bank(bk)
                            TE.wait(t_stok, tk["consts"])
                            tr = mm.mark(TE.raw.matmul(
                                PS[bk][0:1, 0:128], lhsT=s_tok[:, i:i + 1],
                                rhs=ident32[:], start=True, stop=True))
                            SE.wait(tr)
                            bank_free[bk] = sc.mark(SE.raw.copy(
                                s_row[0:1, i * 128:(i + 1) * 128], PS[bk][0:1, 0:128]))
                        t_srow = sc.tok()
                        if "srow" in dbg_aps:
                            final_outs.append(dma(SY, sd, dbg_aps["srow"], s_row[:],
                                                  deps=(t_srow,)))
                        for cch in range(NCH):
                            bk = 4 + cch % 2
                            use_bank(bk)
                            TE.wait(t_srow, tk["consts"])
                            tr = mm.mark(TE.raw.matmul(
                                PS[bk], lhsT=ones_row_f32[:],
                                rhs=s_row[0:1, cch * 512:(cch + 1) * 512],
                                start=True, stop=True))
                            VE.wait(tr)
                            bank_free[bk] = vv.mark(VE.raw.tensor_copy(
                                S_full[:, cch * 512:(cch + 1) * 512], PS[bk]))
                    barrier()

            def l0_attn(S_full, st_S):
                st_l0 = ExitStack()
                qT = sbl(st_l0, "qT", (128, T), BF16)
                kT = sbl(st_l0, "kT", (128, T), BF16)
                vT = sbl(st_l0, "vT", (128, T), BF16)
                q_aug = sbl(st_l0, "q_aug", (17, 8, T), BF16)
                q_tok = sbl(st_l0, "q_tok", (128, NT, 128), BF16)
                with st_l0:
                    t_ev = {0: None, 1: None, 2: None}
                    for cch in range(NCH):
                        for wi in range(3):
                            buf = (qT, kT, vT)[wi]
                            bk = (cch * 3 + wi) % 4
                            use_bank(bk)
                            TE.wait(tk["wfold"])
                            for a in range(DT):
                                lhsT = (wq_b[:, a, :] if wi == 0
                                        else wkv_b[:, a, (wi - 1) * 128:wi * 128])
                                t_mm = mm.mark(TE.raw.matmul(
                                    PS[bk], lhsT=lhsT, rhs=xnT[:, a, cch * 512:(cch + 1) * 512],
                                    start=(a == 0), stop=(a == DT - 1)))
                            VE.wait(t_mm)
                            t_c = vv.mark(VE.raw.tensor_tensor(
                                out=buf[:, cch * 512:(cch + 1) * 512], in0=PS[bk],
                                in1=S_full[:, cch * 512:(cch + 1) * 512], op=OP.mult))
                            bank_free[bk] = t_c
                            t_ev[wi] = t_c
                    barrier()
                    st_S.close()
                    if l0sub < 2:
                        return

                    kvA = sbl(st_l0, "kvA", (17, 8, 17), BF16)
                    kvB = sbl(st_l0, "kvB", (128, 8, 17), BF16)
                    kvC = sbl(st_l0, "kvC", (128, 8, 17), BF16)
                    with ExitStack() as st:
                        k_tok = sbl(st, "k_tok", (128, NT, 8, 17), BF16)
                        v_tok = sbl(st, "v_tok", (128, NT, 8, 17), BF16)
                        ksc = sbl(st, "ksc", (128, 8, 16), BF16)
                        kf2 = sbl(st, "kf2", (128, 2, 8, 256), BF16)
                        kvS = sbl(st, "kvS", (17, 8, 273), BF16)

                        t_qaug = [vv.mark(VE.raw.memset(q_aug[0:1, :, :], 1.0))]
                        for h in range(8):
                            t_qaug.append(dma(SY, sd, q_aug[1:17, h, :],
                                              qT[h * 16:(h + 1) * 16, :], deps=(t_ev[0],)))
                        t_kones = vv.mark(VE.raw.memset(k_tok[:, :, :, 0:1], 1.0))
                        VE.raw.drain()
                        t_vones = vv.mark(VE.raw.memset(v_tok[:, :, :, 0:1], 1.0))

                        t_tok = [[None] * 3 for _ in range(NT)]
                        for i in range(NT):
                            for wi, src in enumerate((qT, kT, vT)):
                                bk = (i * 3 + wi) % 4
                                use_bank(bk)
                                TE.wait(t_ev[wi], tk["consts"])
                                tr = mm.mark(TE.raw.matmul(
                                    PS[bk][:, 0:128], lhsT=src[:, i * 128:(i + 1) * 128],
                                    rhs=ident16[:], start=True, stop=True))
                                VE.wait(tr)
                                if wi == 0:
                                    t_tok[i][wi] = vv.mark(VE.raw.tensor_copy(
                                        q_tok[:, i, :], PS[bk][:, 0:128]))
                                else:
                                    dst = (k_tok if wi == 1 else v_tok)[:, i, :, 1:17]
                                    t_tok[i][wi] = vv.mark(VE.raw.tensor_copy(
                                        dst, PS[bk][:, 0:128].rearrange("p (h d) -> p h d", h=8)))
                                bank_free[bk] = t_tok[i][wi]

                        if l0sub < 3:
                            barrier()
                            return
                        kvps = [pA[0:17, 0:273], pA[0:17, 512:785],
                                pB[0:17, 0:273], pB[0:17, 512:785],
                                pC[0:17, 0:273], pC[0:17, 512:785],
                                pD[0:17, 0:273], pD[0:17, 512:785]]
                        t_kf2_used = None
                        for i in range(NT):
                            par = i % 2
                            VE.wait(t_tok[i][1], t_tok[i][2], t_kones, t_vones, t_kf2_used)
                            VE.raw.drain()
                            VE.raw.tensor_scalar(out=ksc[:], in0=k_tok[:, i, :, 1:17],
                                                 scalar1=TAYSC, scalar2=None, op0=OP.mult)
                            VE.raw.drain()
                            for j in range(16):
                                t_kf2 = vv.mark(VE.raw.tensor_tensor(
                                    out=kf2[:, par, :, j * 16:(j + 1) * 16], in0=ksc[:],
                                    in1=ksc[:, :, j:j + 1].broadcast_to([128, 8, 16]),
                                    op=OP.mult))
                            TE.wait(t_kf2)
                            if i == 0:
                                for bk in range(8):
                                    use_bank(bk)
                            for h in range(8):
                                TE.raw.matmul(kvps[h][:, 0:17], lhsT=v_tok[:, i, h, :],
                                              rhs=k_tok[:, i, h, :],
                                              start=(i == 0), stop=False)
                                t_kv = mm.mark(TE.raw.matmul(
                                    kvps[h][:, 17:273], lhsT=v_tok[:, i, h, :],
                                    rhs=kf2[:, par, h, :],
                                    start=False, stop=(i == NT - 1)))
                            if i >= 1:
                                t_kf2_used = t_kv

                        t_kvS = []
                        for h in range(8):
                            SE.wait(t_kv)
                            t_kvS.append(sc.mark(SE.raw.copy(kvS[:, h, :], kvps[h])))
                            bank_free[h] = t_kvS[-1]
                        for h in range(8):
                            for pi, (lo, hi, dst) in enumerate(((0, 17, kvA), (17, 145, kvB),
                                                                (145, 273, kvC))):
                                bk = (h * 3 + pi) % 2
                                use_bank(bk)
                                TE.wait(t_kvS[h])
                                tr = mm.mark(TE.raw.matmul(
                                    PS[bk][0:(hi - lo), 0:17], lhsT=kvS[:, h, lo:hi],
                                    rhs=ident16[0:17, 0:17], start=True, stop=True))
                                VE.wait(tr)
                                bank_free[bk] = vv.mark(VE.raw.tensor_copy(
                                    dst[:, h, :], PS[bk][0:(hi - lo), 0:17]))
                        t_kvT = vv.tok()
                        barrier()

                    if l0sub < 4:
                        return
                    with ExitStack() as st:
                        qsc = sbl(st, "qsc", (128, 8, 16), BF16)
                        qf2 = sbl(st, "qf2", (128, 8, 256), BF16)
                        qf2T = sbl(st, "qf2T", (128, 2, 8, 512), BF16)
                        outS = sbl(st, "outS", (17, 8, 512), BF16)
                        recS = sbl(st, "recS", (1, 2, 512), F32)
                        wout2 = sbl(st, "wout2", (17, 8, D), BF16)
                        t_wo2z = vv.mark(VE.raw.memset(wout2[0:1, :, :], 0.0))
                        t_wo2 = dma(GE, gd, wout2[1:17, :, :],
                                    wo0.rearrange("(h e) m -> e h m", e=16),
                                    allow_slow_non_contiguous=True)
                        for cch in range(NCH):
                            t_qf2T = []
                            t_qf2_used = mm.tok()
                            for jj in range(4):
                                i = 4 * cch + jj
                                VE.wait(t_tok[i][0], t_qf2_used)
                                VE.raw.drain()
                                VE.raw.tensor_scalar(
                                    out=qsc[:], in0=q_tok[:, i, :].rearrange(
                                        "p (h d) -> p h d", h=8),
                                    scalar1=TAYSC, scalar2=None, op0=OP.mult)
                                VE.raw.drain()
                                for j in range(16):
                                    t_qf2 = vv.mark(VE.raw.tensor_tensor(
                                        out=qf2[:, :, j * 16:(j + 1) * 16], in0=qsc[:],
                                        in1=qsc[:, :, j:j + 1].broadcast_to([128, 8, 16]),
                                        op=OP.mult))
                                TE.wait(t_qf2)
                                for h in range(8):
                                    for piece in range(2):
                                        bk = (h * 2 + piece) % 2
                                        use_bank(bk)
                                        tr = mm.mark(TE.raw.matmul(
                                            PS[bk][:, 0:128],
                                            lhsT=qf2[:, h, piece * 128:(piece + 1) * 128],
                                            rhs=ident16[:], start=True, stop=True))
                                        VE.wait(tr)
                                        bank_free[bk] = vv.mark(VE.raw.tensor_copy(
                                            qf2T[:, piece, h, jj * 128:(jj + 1) * 128],
                                            PS[bk][:, 0:128]))
                                        t_qf2T.append(bank_free[bk])
                                t_qf2_used = mm.tok()
                            if l0sub < 5:
                                continue
                            t_sc_h = []
                            for h in range(8):
                                bko = 2 + h % 2
                                use_bank(bko)
                                ps17 = PS[bko][0:17, :]
                                TE.wait(t_qaug, t_qf2T[-1], t_kvT)
                                TE.raw.matmul(ps17, lhsT=kvA[:, h, :],
                                              rhs=q_aug[:, h, cch * 512:(cch + 1) * 512],
                                              start=True, stop=False)
                                TE.raw.matmul(ps17, lhsT=kvB[:, h, :], rhs=qf2T[:, 0, h, :],
                                              start=False, stop=False)
                                t3 = mm.mark(TE.raw.matmul(ps17, lhsT=kvC[:, h, :],
                                                           rhs=qf2T[:, 1, h, :],
                                                           start=False, stop=True))
                                SE.wait(t3)
                                t_outS = sc.mark(SE.raw.copy(outS[:, h, :], ps17))
                                if l0sub == 50:
                                    bank_free[bko] = t_outS
                                    t_sc_h.append(t_outS)
                                    continue
                                t_den = sc.mark(SE.raw.copy(recS[0:1, 0, :], ps17[0:1, :]))
                                bank_free[bko] = t_den
                                VE.wait(t_den)
                                VE.raw.drain()
                                VE.raw.tensor_scalar(out=recS[0:1, 0, :],
                                                     in0=recS[0:1, 0, :],
                                                     scalar1=1e-5, scalar2=None, op0=OP.max)
                                VE.raw.drain()
                                t_rec = vv.mark(VE.raw.reciprocal(recS[0:1, 1, :],
                                                                  recS[0:1, 0, :]))
                                if l0sub == 51:
                                    t_sc_h.append(t_rec)
                                    continue
                                bk4 = 4 + h % 2
                                use_bank(bk4)
                                TE.wait(t_rec)
                                mmb = mm.mark(TE.raw.matmul(PS[bk4][0:17, :],
                                                            lhsT=ones_row_f32[0:1, 0:17],
                                                            rhs=recS[0:1, 1, :],
                                                            start=True, stop=True))
                                if l0sub == 52:
                                    bank_free[bk4] = mmb
                                    t_sc_h.append(mmb)
                                    continue
                                VE.wait(mmb, t_outS)
                                t_sch = vv.mark(VE.raw.tensor_tensor(
                                    out=outS[:, h, :], in0=outS[:, h, :],
                                    in1=PS[bk4][0:17, :], op=OP.mult))
                                bank_free[bk4] = t_sch
                                t_sc_h.append(t_sch)
                            if l0sub < 6 or l0sub in (50, 51, 52):
                                continue
                            for tt_ in range(4):
                                for dch in range(2):
                                    bk = 6 + (tt_ * 2 + dch) % 2
                                    use_bank(bk)
                                    TE.wait(t_sc_h, t_wo2z, t_wo2)
                                    for h in range(8):
                                        t_wo = mm.mark(TE.raw.matmul(
                                            PS[bk], lhsT=outS[:, h, tt_ * 128:(tt_ + 1) * 128],
                                            rhs=wout2[:, h, dch * 512:(dch + 1) * 512],
                                            start=(h == 0), stop=(h == 7)))
                                    VE.wait(t_wo)
                                    aslc = acc[:, 4 * cch + tt_, dch * 512:(dch + 1) * 512]
                                    bank_free[bk] = vv.mark(VE.raw.tensor_tensor(
                                        out=aslc, in0=aslc, in1=PS[bk], op=OP.add))
                    barrier()

            def l0_ffn(tern_gen):
                with ExitStack() as st:
                    W1q = sbl(st, "W1q", (128, 2, DT, 1024), BF16)
                    W2q = sbl(st, "W2q", (128, 2, 8, 1024), BF16)
                    h1 = sbl(st, "h1", (128, 8, 512), BF16)
                    b2s = sbl(st, "b2s", (1, 2, D), BF16)
                    lgt = sbl(st, "lgt", (128, 8), F32)
                    lgt2 = sbl(st, "lgt2", (128, 8), F32)
                    colsr = sbl(st, "colsr", (128, 6), F32)
                    for i in range(NT):
                        bk = 6 + i % 2
                        use_bank(bk)
                        TE.wait(tk["wr"], tk.get("smallpar"))
                        psl = PS[bk][:, 0:8]
                        for a in range(DT):
                            TE.raw.matmul(psl, lhsT=xnT[:, a, i * 128:(i + 1) * 128],
                                          rhs=wr_b[:, a, :], start=(a == 0), stop=False)
                        t_lg = mm.mark(TE.raw.matmul(psl, lhsT=ones_row_bf[:],
                                                     rhs=br_b[:], start=False, stop=True))
                        VE.wait(t_lg)
                        VE.raw.tensor_reduce(colsr[:, 0:1], psl, AX, OP.max)
                        VE.raw.drain()
                        t_mx = vv.mark(VE.raw.tensor_scalar(
                            out=colsr[:, 1:2], in0=colsr[:, 0:1],
                            scalar1=-1.0, scalar2=None, op0=OP.mult))
                        SE.wait(t_mx)
                        t_e = sc.mark(SE.raw.activation(lgt[:], psl, ACT.Exp,
                                                        bias=colsr[:, 1:2]))
                        VE.wait(t_e)
                        VE.raw.tensor_reduce(colsr[:, 2:3], lgt[:], AX, OP.add)
                        VE.raw.drain()
                        VE.raw.reciprocal(colsr[:, 3:4], colsr[:, 2:3])
                        VE.raw.tensor_scalar(out=lgt2[:], in0=psl, scalar1=colsr[:, 0:1],
                                             scalar2=-1e6, op0=OP.is_ge, op1=OP.mult)
                        VE.raw.drain()
                        bank_free[bk] = vv.mark(VE.raw.tensor_tensor(
                            out=lgt2[:], in0=lgt2[:], in1=consts["iota8big"][:], op=OP.add))
                        VE.raw.drain()
                        VE.raw.tensor_reduce(colsr[:, 4:5], lgt2[:], AX, OP.min)
                        VE.raw.drain()
                        VE.raw.tensor_scalar(out=lgt2[:], in0=consts["iota8"][:],
                                             scalar1=colsr[:, 4:5], scalar2=None,
                                             op0=OP.is_equal)
                        VE.raw.drain()
                        t_G = vv.mark(VE.raw.tensor_scalar(
                            out=G[:, i, :], in0=lgt2[:], scalar1=colsr[:, 3:4],
                            scalar2=None, op0=OP.mult))
                    if "G" in dbg_aps:
                        final_outs.append(dma(SY, sd, dbg_aps["G"], G[:], deps=(t_G,)))

                    t_w1use = [None, None]
                    t_w2use = [None, None]
                    t_b2use = [None, None]
                    t_h1use = mm.tok()
                    for e in range(8):
                        sl = e % 2
                        t_b2 = dma(GE, b2ss[sl], b2s[0:1, sl:sl + 1, :],
                                   b2_0[e:e + 1, :].rearrange("(o e) d -> o e d", o=1),
                                   deps=(t_b2use[sl],))
                        for qq in range(4):
                            s = (e * 4 + qq) % 2
                            t_w1 = dma(GE, moes[s], W1q[:, s, :, :],
                                       w1_0[e].rearrange("(a p) f -> p a f", p=128)
                                       [:, :, qq * 1024:(qq + 1) * 1024],
                                       deps=(t_w1use[s],))
                            t_w2 = dma(GE, moes[s], W2q[:, s, :, :],
                                       w2_0[e].rearrange("(b p) d -> p b d", p=128)
                                       [:, qq * 8:(qq + 1) * 8, :],
                                       deps=(t_w2use[s],))
                            for tq in range(max(1, NCH)):
                                for ft in range(8):
                                    bk = ft % 2
                                    use_bank(bk)
                                    TE.wait(t_w1)
                                    for a in range(DT):
                                        t_p1 = mm.mark(TE.raw.matmul(
                                            PS[bk], lhsT=W1q[:, s, a, ft * 128:(ft + 1) * 128],
                                            rhs=xnT[:, a, tq * 512:(tq + 1) * 512],
                                            start=(a == 0), stop=(a == DT - 1)))
                                    SE.wait(t_p1, t_h1use)
                                    bank_free[bk] = sc.mark(SE.raw.activation(
                                        h1[:, ft, :], PS[bk], ACT.Relu,
                                        bias=b1c0[:, e, qq * 8 + ft:qq * 8 + ft + 1]))
                                t_h1 = sc.tok()
                                VE.raw.drain()
                                for dch in range(2):
                                    for ts_ in range(4):
                                        bk = 2 + (dch * 4 + ts_) % 4
                                        use_bank(bk)
                                        TE.wait(t_h1, t_w2, t_G, t_b2)
                                        for ft in range(8):
                                            ins = TE.raw.matmul(
                                                PS[bk],
                                                lhsT=h1[:, ft, ts_ * 128:(ts_ + 1) * 128],
                                                rhs=W2q[:, s, ft, dch * 512:(dch + 1) * 512],
                                                start=(ft == 0),
                                                stop=(ft == 7 and qq != 0))
                                            if ft == 7 and qq == 0:
                                                ins = TE.raw.matmul(
                                                    PS[bk], lhsT=ones_row_bf[:],
                                                    rhs=b2s[0:1, sl, dch * 512:(dch + 1) * 512],
                                                    start=False, stop=True)
                                        t_p2 = mm.mark(ins)
                                        VE.wait(t_p2)
                                        aslc = acc[:, tq * 4 + ts_, dch * 512:(dch + 1) * 512]
                                        bank_free[bk] = vv.mark(VE.raw.scalar_tensor_tensor(
                                            out=aslc, in0=PS[bk],
                                            scalar=G[:, tq * 4 + ts_, e:e + 1],
                                            in1=aslc, op0=OP.mult, op1=OP.add))
                                t_h1use = mm.tok()
                            t_w1use[s] = mm.tok()
                            t_w2use[s] = t_w1use[s]
                            if tern_gen is not None:
                                for _ in range(2):
                                    if next(tern_gen, "END") == "END":
                                        tern_gen = None
                                        break
                        t_b2use[sl] = mm.tok()
                    if tern_gen is not None:
                        for _ in tern_gen:
                            pass
                    barrier()

            def tern_steps():
                st = ExitStack()
                wbuf = sbl(st, "wbuf", (128, 2, 2048), F32)
                redc = sbl(st, "redc", (128, 52), F32)
                thr = sbl(st, "thr", (1, 4), F32)
                thrc = sbl(st, "thrc", (128, 4), F32)
                ttmp = sbl(st, "ttmp", (128, 4, 128), BF16)
                buf1 = sbl(st, "buf1", (128, 8, 256), BF16)
                buf2 = sbl(st, "buf2", (128, 4, 512), BF16)
                slot_used = [None, None]
                t_thrc = None
                tern_ctr = [0]

                def ternarize(ps_ap, dst_ap, thc, deps):
                    VE.wait(t_thrc, *deps)
                    sl_ = tern_ctr[0] % 4
                    if sl_ == 0:
                        VE.raw.drain()
                    tern_ctr[0] += 1
                    VE.raw.tensor_scalar(out=ttmp[:, sl_, :], in0=ps_ap,
                                         scalar1=thrc[:, thc + 1:thc + 2],
                                         scalar2=None, op0=OP.is_lt)
                    VE.raw.drain()
                    return vv.mark(VE.raw.scalar_tensor_tensor(
                        out=dst_ap, in0=ps_ap, scalar=thrc[:, thc:thc + 1],
                        in1=ttmp[:, sl_, :], op0=OP.is_gt, op1=OP.subtract))

                def p1_src(t):
                    if t < 32:
                        return w1_1[t * 128:(t + 1) * 128, :], redc[:, t:t + 1], 1024
                    tt = t - 32
                    return (w2_1[(tt // 2) * 128:(tt // 2 + 1) * 128,
                                 (tt % 2) * 2048:(tt % 2 + 1) * 2048],
                            redc[:, 32 + tt:33 + tt], 2048)

                t_l = [None, None]
                srcap, _, rl = p1_src(0)
                t_l[0] = dma(SY, tls[0], wbuf[:, 0, 0:rl], srcap, deps=(None,))
                for t in range(48):
                    s = t % 2
                    if t + 1 < 48:
                        srcap, _, rl1 = p1_src(t + 1)
                        t_l[(t + 1) % 2] = dma(SY, tls[(t + 1) % 2],
                                               wbuf[:, (t + 1) % 2, 0:rl1], srcap,
                                               deps=(slot_used[(t + 1) % 2],))
                    _, cdst, rl0 = p1_src(t)
                    VE.wait(t_l[s])
                    slot_used[s] = vv.mark(VE.raw.tensor_reduce(
                        cdst, wbuf[:, s, 0:rl0], AX, OP.add,
                        apply_absolute_value=True))
                    if t % 2 == 1:
                        yield
                for wi in range(2):
                    VE.raw.drain()
                    csrc = redc[:, 0:32] if wi == 0 else redc[:, 32:48]
                    rcol = redc[:, 48 + wi:49 + wi]
                    t_red = vv.mark(VE.raw.tensor_reduce(rcol, csrc, AX, OP.add))
                    use_bank(6)
                    TE.wait(t_red)
                    t_ps = mm.mark(TE.raw.matmul(PS[6][0:1, 0:1], lhsT=rcol,
                                                 rhs=consts["ones_col_f32"][:],
                                                 start=True, stop=True))
                    VE.wait(t_ps)
                    nelem = float(1024 * 4096)
                    VE.raw.tensor_scalar(out=thr[0:1, 0:1], in0=PS[6][0:1, 0:1],
                                         scalar1=1.0 / nelem, scalar2=1e-8,
                                         op0=OP.mult, op1=OP.max)
                    VE.raw.drain()
                    VE.raw.tensor_scalar(out=thr[0:1, 1:2], in0=thr[0:1, 0:1],
                                         scalar1=0.5, scalar2=None, op0=OP.mult)
                    VE.raw.drain()
                    t_thr = vv.mark(VE.raw.tensor_scalar(
                        out=thr[0:1, 2:3], in0=thr[0:1, 1:2],
                        scalar1=-1.0, scalar2=None, op0=OP.mult))
                    bank_free[6] = t_thr
                    use_bank(6)
                    TE.wait(t_thr)
                    t_bc = mm.mark(TE.raw.matmul(PS[6][:, 0:2],
                                                 lhsT=ones_row_f32[:],
                                                 rhs=thr[0:1, 1:3], start=True, stop=True))
                    VE.wait(t_bc)
                    bank_free[6] = vv.mark(VE.raw.tensor_copy(
                        thrc[:, 2 * wi:2 * wi + 2], PS[6][:, 0:2]))
                    t_thrc = vv.tok()
                yield

                t_st = None
                w1r = w1_1.rearrange("(t p) d -> p t d", p=128)
                t_l[0] = dma(SY, tls[0],
                             wbuf[:, 0, 0:2048].rearrange("p (t d) -> p t d", t=2),
                             w1r[:, 0:2, :], deps=(slot_used[0],))
                for fb in range(16):
                    s = fb % 2
                    if fb + 1 < 16:
                        s1 = (fb + 1) % 2
                        t_l[s1] = dma(SY, tls[s1],
                                      wbuf[:, s1, 0:2048].rearrange("p (t d) -> p t d", t=2),
                                      w1r[:, (fb + 1) * 2:(fb + 2) * 2, :],
                                      deps=(slot_used[s1],))
                    for ftl in range(2):
                        for a2_ in range(DT):
                            bk = 6 + (ftl * 8 + a2_) % 2
                            use_bank(bk)
                            TE.wait(t_l[s], tk["consts"])
                            tr = mm.mark(TE.raw.transpose(
                                PS[bk][:, 0:128],
                                wbuf[:, s,
                                     ftl * 1024 + a2_ * 128:ftl * 1024 + (a2_ + 1) * 128],
                                ident32[:]))
                            tt_ = ternarize(PS[bk][:, 0:128],
                                            buf1[:, a2_, ftl * 128:(ftl + 1) * 128], 0,
                                            (tr, t_st))
                            bank_free[bk] = tt_
                    slot_used[s] = tt_
                    t_st = dma(SY, tst, w1t_sc[:, :, fb * 256:(fb + 1) * 256],
                               buf1[:], deps=(tt_,))
                    yield
                w2r = w2_1.rearrange("(dc p) f -> p dc f", p=128)

                def p2b_ap(u):
                    ftb, dc = u // 8, u % 8
                    return w2r[:, dc, ftb * 512:(ftb + 1) * 512]

                t_l[0] = dma(SY, tls[0], wbuf[:, 0, 0:512], p2b_ap(0),
                             deps=(slot_used[0],))
                for u in range(64):
                    ftb, dc = u // 8, u % 8
                    dh, dc2 = dc // 4, dc % 4
                    s = u % 2
                    if u + 1 < 64:
                        s1 = (u + 1) % 2
                        t_l[s1] = dma(SY, tls[s1], wbuf[:, s1, 0:512], p2b_ap(u + 1),
                                      deps=(slot_used[s1],))
                    for fl in range(4):
                        bk = 6 + (dc * 4 + fl) % 2
                        use_bank(bk)
                        TE.wait(t_l[s], tk["consts"])
                        tr = mm.mark(TE.raw.transpose(
                            PS[bk][:, 0:128],
                            wbuf[:, s, fl * 128:(fl + 1) * 128], ident32[:]))
                        tt_ = ternarize(PS[bk][:, 0:128],
                                        buf2[:, fl, dc2 * 128:(dc2 + 1) * 128],
                                        2, (tr, t_st))
                        bank_free[bk] = tt_
                    slot_used[s] = tt_
                    if dc2 == 3:
                        t_st = dma(SY, tst,
                                   w2t_sc[:, ftb * 4:(ftb + 1) * 4,
                                          dh * 512:(dh + 1) * 512],
                                   buf2[:], deps=(tt_,))
                    if u % 2 == 1:
                        yield
                tk["tern"] = t_st
                st.close()

            def l1_attn():
                st_l1 = ExitStack()
                qrotT = sbl(st_l1, "qrotT", (128, 8, T), BF16)
                krotT = sbl(st_l1, "krotT", (128, 8, T + VKV), BF16)
                v_ext = sbl(st_l1, "v_ext", (128, NT + 1, 32, 65), BF16)
                with st_l1:
                    with ExitStack() as st:
                        sincos = sbl(st, "sincos", (128, 2, T), BF16)
                        Wb = sbl(st, "Wqk", (128, DT, D), BF16)
                        qb = sbl(st, "qb", (128, 512), BF16)
                        rt1 = sbl(st, "rt1", (128, 512), BF16)
                        rt2 = sbl(st, "rt2", (128, 512), BF16)
                        tk["sc"] = dma(GE, gd, sincos[:, 0, :], sin_in)
                        tk["sc"] = dma(GE, gd, sincos[:, 1, :], cos_in)
                        t_W = None
                        t_Wuse = None
                        t_qb_used = None
                        for which in range(2):
                            t_W = dma(GE, gw, Wb[:], (wqf1 if which == 0 else wkf1)
                                      .rearrange("(a p) m -> p a m", p=128), deps=(t_Wuse,))
                            dstT = qrotT if which == 0 else krotT
                            for a2 in range(8):
                                for cch in range(NCH):
                                    bk = (a2 * NCH + cch) % 2
                                    use_bank(bk)
                                    TE.wait(t_W)
                                    for a in range(DT):
                                        t_pq = mm.mark(TE.raw.matmul(
                                            PS[bk], lhsT=Wb[:, a, a2 * 128:(a2 + 1) * 128],
                                            rhs=xnT[:, a, cch * 512:(cch + 1) * 512],
                                            start=(a == 0), stop=(a == DT - 1)))
                                    VE.wait(t_pq, t_qb_used)
                                    VE.raw.drain()
                                    if which == 0:
                                        t_evq = vv.mark(VE.raw.tensor_scalar(
                                            out=qb[:], in0=PS[bk],
                                            scalar1=bq_c[:, a2:a2 + 1], scalar2=L1QSC,
                                            op0=OP.add, op1=OP.mult))
                                    else:
                                        t_evq = vv.mark(VE.raw.tensor_scalar(
                                            out=qb[:], in0=PS[bk],
                                            scalar1=bk_c[:, a2:a2 + 1], scalar2=None,
                                            op0=OP.add))
                                    bank_free[bk] = t_evq
                                    bk2 = 2 + (a2 * NCH + cch) % 2
                                    use_bank(bk2)
                                    TE.wait(t_evq, tk["consts"])
                                    t_fl = mm.mark(TE.raw.matmul(
                                        PS[bk2], lhsT=consts["pflip"][:], rhs=qb[:],
                                        start=True, stop=True))
                                    VE.wait(t_fl, tk["sc"])
                                    VE.raw.drain()
                                    VE.raw.tensor_tensor(
                                        out=rt1[:], in0=qb[:],
                                        in1=sincos[:, 1, cch * 512:(cch + 1) * 512], op=OP.mult)
                                    VE.raw.tensor_tensor(
                                        out=rt2[:], in0=PS[bk2],
                                        in1=sincos[:, 0, cch * 512:(cch + 1) * 512], op=OP.mult)
                                    VE.raw.drain()
                                    t_rot = vv.mark(VE.raw.tensor_tensor(
                                        out=dstT[:, a2, cch * 512:(cch + 1) * 512],
                                        in0=rt1[:], in1=rt2[:], op=OP.add))
                                    bank_free[bk2] = t_rot
                                    t_qb_used = t_rot
                            t_Wuse = mm.tok()
                        tk["rotdone"] = vv.tok()
                        tk["vk"] = dma(GE, gd, krotT[:, :, T:T + VKV],
                                       vk1.rearrange("(a p) m -> p a m", p=128),
                                       allow_slow_non_contiguous=True)
                        barrier()

                    with ExitStack() as st:
                        WVb = sbl(st, "WVb", (128, DT, 1024), BF16)
                        bv_r = sbl(st, "bv_r", (1, 2048), BF16)
                        t_bv = dma(GE, gd, bv_r[:], bv1)
                        t_vz = vv.mark(VE.raw.memset(v_ext[:], 0.0))
                        VE.raw.drain()
                        t_vo = vv.mark(VE.raw.memset(v_ext[:, :, :, 64:65], 1.0))
                        tk["vv"] = dma(GE, gd, v_ext[0:VKV, NT, :, 0:64],
                                       vv1_in.rearrange("v (h r) -> v h r", h=32),
                                       deps=(t_vz, t_vo))
                        t_WVuse = None
                        for half in range(2):
                            t_WV = dma(GE, gw, WVb[:],
                                       wvf1.rearrange("(a p) m -> p a m", p=128)
                                       [:, :, half * 1024:(half + 1) * 1024],
                                       deps=(t_WVuse,))
                            for kvt in range(NT):
                                for hc2 in range(2):
                                    hc = half * 2 + hc2
                                    bk = 4 + (kvt * 2 + hc2) % 4
                                    use_bank(bk)
                                    TE.wait(t_WV, t_bv)
                                    for a in range(DT):
                                        TE.raw.matmul(
                                            PS[bk], lhsT=xnT[:, a, kvt * 128:(kvt + 1) * 128],
                                            rhs=WVb[:, a, hc2 * 512:(hc2 + 1) * 512],
                                            start=(a == 0), stop=False)
                                    t_pv = mm.mark(TE.raw.matmul(
                                        PS[bk], lhsT=ones_row_bf[:],
                                        rhs=bv_r[0:1, hc * 512:(hc + 1) * 512],
                                        start=False, stop=True))
                                    VE.wait(t_pv, t_vo)
                                    bank_free[bk] = vv.mark(VE.raw.tensor_copy(
                                        v_ext[:, kvt, hc * 8:(hc + 1) * 8, 0:64],
                                        PS[bk].rearrange("p (h r) -> p h r", h=8)))
                            t_WVuse = mm.tok()
                        barrier()

                    if "qrot" in dbg_aps:
                        final_outs.append(dma(SY, sd, dbg_aps["qrot"], qrotT[:]))
                    if "krot" in dbg_aps:
                        final_outs.append(dma(SY, sd, dbg_aps["krot"], krotT[:]))
                    if "vext" in dbg_aps:
                        final_outs.append(dma(SY, sd, dbg_aps["vext"],
                                              v_ext[:].rearrange("p a b c -> p (a b c)")))

                    tk["xnspill"] = dma(SY, sd, xnsp, xnT[:])
                    barrier()
                    st_xn.close()

                    with ExitStack() as st:
                        zT = sbl(st, "zT", (128, 16, 512), BF16)
                        expT = sbl(st, "expT", (128, 8, 512), BF16)
                        denM = sbl(st, "denM", (32, 512), F32)
                        recMb = sbl(st, "recMb", (32, 512), BF16)
                        den_stg = sbl(st, "den_stg", (1, 2, 512), F32)
                        sel_sb = sbl(st, "sel_sb", (32, 2048), BF16)
                        WOb = sbl(st, "WOb", (128, 16, 512), BF16)
                        stg = sbl(st, "stg", (128, 2, 512), F32)
                        shcol = sbl(st, "shcol", (128, 1), F32)
                        s_masks = sbl(st, "s_masks", (128, 5, 512), BF16)
                        t_sh = vv.mark(VE.raw.memset(shcol[:], EXP_SHIFT))
                        SE.wait(t_sh)
                        t_mskld = dma(SY, sd, s_masks[:], masks_in.rearrange("m p q -> p m q"))
                        t_selld = dma(SY, sd, sel_sb[:], sel_in)
                        wof_r = wof1.rearrange("(vc p) d -> p vc d", p=128)
                        t_wob_used = None
                        t_stg_used = [None, None]
                        t_zT_used = None
                        t_dstg_used = [None, None]
                        t_exp_used = [[mm.tok(), mm.tok()] for _ in range(4)]
                        for cch in range(NCH):
                            full = [kvt for kvt in range(NT) if kvt * 128 <= cch * 512 + 527]
                            has_virt = T <= cch * 512 + 527
                            nsteps = len(full) + (1 if has_virt else 0)
                            t_zT = []
                            t_den_dmas = []
                            t_wob = dma(GE, gw, WOb[:], wof_r[:, :, 0:512],
                                        deps=(t_wob_used,))
                            for a2 in range(8):
                                t_e = [[None, None] for _ in range(4)]
                                t_z = [None] * 4
                                for hh in range(4):
                                    use_bank(4 + hh)
                                for p in range(nsteps + 1):
                                    if p < nsteps:
                                        virt = p >= len(full)
                                        kvt = full[p] if not virt else NT
                                        rows = 128 if not virt else VKV
                                        delta = (kvt * 128 if not virt else T) - cch * 512
                                        sl = p % 2
                                        for hh in range(4):
                                            hb = 32 * hh
                                            use_bank(hh)
                                            TE.wait(tk["rotdone"], tk["vk"])
                                            t_s = mm.mark(TE.raw.matmul(
                                                PS[hh][0:rows, :],
                                                lhsT=krotT[hb:hb + 32, a2,
                                                           kvt * 128:kvt * 128 + rows],
                                                rhs=qrotT[hb:hb + 32, a2,
                                                          cch * 512:(cch + 1) * 512],
                                                start=True, stop=True,
                                                tile_position=(hb, 0)))
                                            SE.wait(t_s, t_exp_used[hh][sl])
                                            t_ex = sc.mark(SE.raw.activation(
                                                expT[0:rows, 2 * hh + sl, :],
                                                PS[hh][0:rows, :],
                                                ACT.Exp, bias=shcol[0:rows, :]))
                                            bank_free[hh] = t_ex
                                            if 0 <= delta <= 512:
                                                VE.wait(t_ex, t_mskld)
                                                t_ex = vv.mark(VE.raw.tensor_tensor(
                                                    out=expT[0:rows, 2 * hh + sl, :],
                                                    in0=expT[0:rows, 2 * hh + sl, :],
                                                    in1=s_masks[0:rows, delta // 128, :],
                                                    op=OP.mult))
                                            t_e[hh][sl] = t_ex
                                    if p >= 1:
                                        pz = p - 1
                                        virt = pz >= len(full)
                                        kvt = full[pz] if not virt else NT
                                        rows = 128 if not virt else VKV
                                        sl = pz % 2
                                        for hh in range(4):
                                            h = a2 * 4 + hh
                                            TE.wait(t_e[hh][sl], tk["vv"], t_zT_used)
                                            t_z[hh] = mm.mark(TE.raw.matmul(
                                                PS[4 + hh][0:65, :],
                                                lhsT=v_ext[0:rows, kvt, h, :],
                                                rhs=expT[0:rows, 2 * hh + sl, :],
                                                start=(pz == 0), stop=(pz == nsteps - 1)))
                                            t_exp_used[hh][sl] = t_z[hh]
                                for hh in range(4):
                                    h = a2 * 4 + hh
                                    zoff, vc = 64 * (h % 2), h // 2
                                    sl = hh % 2
                                    VE.wait(t_z[hh], t_dstg_used[sl])
                                    VE.raw.drain()
                                    VE.raw.tensor_copy(den_stg[0:1, sl, :],
                                                       PS[4 + hh][64:65, :])
                                    VE.raw.drain()
                                    t_zc = vv.mark(VE.raw.tensor_copy(
                                        zT[zoff:zoff + 64, vc, :], PS[4 + hh][0:64, :]))
                                    bank_free[4 + hh] = t_zc
                                    t_dd = dma(SY, sd, denM[h:h + 1, :],
                                               den_stg[0:1, sl, :], deps=(t_zc,))
                                    t_dstg_used[sl] = t_dd
                                    t_den_dmas.append(t_dd)
                            VE.wait(*t_den_dmas)
                            VE.raw.drain()
                            VE.raw.reciprocal(denM[:], denM[:])
                            VE.raw.drain()
                            t_rcb = vv.mark(VE.raw.tensor_copy(recMb[:], denM[:]))
                            for h in range(32):
                                zoff, vc = 64 * (h % 2), h // 2
                                bk = h % 8
                                use_bank(bk)
                                TE.wait(t_rcb, t_selld)
                                t_rb = mm.mark(TE.raw.matmul(
                                    PS[bk][0:64, :],
                                    lhsT=sel_sb[:, h * 64:(h + 1) * 64],
                                    rhs=recMb[:], start=True, stop=True))
                                VE.wait(t_rb)
                                t_zn = vv.mark(VE.raw.tensor_tensor(
                                    out=zT[zoff:zoff + 64, vc, :],
                                    in0=zT[zoff:zoff + 64, vc, :],
                                    in1=PS[bk][0:64, :], op=OP.mult))
                                bank_free[bk] = t_zn
                                t_zT.append(t_zn)
                            for dch in range(2):
                                if dch == 1:
                                    t_wob = dma(GE, gw, WOb[:], wof_r[:, :, 512:1024],
                                                deps=(t_wob_used,))
                                for qt in range(4):
                                    bk = (qt + dch * 2) % 4
                                    use_bank(bk)
                                    TE.wait(t_zT, t_wob)
                                    for vc in range(16):
                                        t_wo = mm.mark(TE.raw.matmul(
                                            PS[bk], lhsT=zT[:, vc, qt * 128:(qt + 1) * 128],
                                            rhs=WOb[:, vc, :],
                                            start=(vc == 0), stop=(vc == 15)))
                                    sp = qt % 2
                                    SE.wait(t_wo, t_stg_used[sp])
                                    t_sg = sc.mark(SE.raw.copy(stg[:, sp, :], PS[bk]))
                                    bank_free[bk] = t_sg
                                    r0 = cch * 512 + qt * 128
                                    t_acd = dma(GE, accs[sp],
                                                o_out[r0:r0 + 128, dch * 512:(dch + 1) * 512],
                                                stg[:, sp, :], deps=(t_sg, tk["spill"]),
                                                accum_op=OP.add)
                                    t_stg_used[sp] = t_acd
                                    final_outs.append(t_acd)
                                t_wob_used = mm.tok()
                            t_zT_used = sc.tok()
                    barrier()

            def l1_ffn():
                with ExitStack() as st:
                    xnb = sbl(st, "xnb", (128, DT, T), BF16)
                    b2r1 = sbl(st, "b2r1", (1, D), BF16)
                    t_b2f = dma(GE, gd, b2r1[:], b2_1)
                    w1tb = sbl(st, "w1tb", (128, DT, 4096), BF16)
                    w2tb = sbl(st, "w2tb", (128, 2, 32, 512), BF16)
                    hbuf = sbl(st, "hbuf", (128, 32, 512), BF16)
                    stg = sbl(st, "stg", (128, 2, 512), F32)
                    t_xn = dma(SY, sd, xnb[:], xnsp, deps=(tk["xnspill"],))
                    t_w1t = dma(GE, ffws[0], w1tb[:], w1t_sc, deps=(tk["tern"],))
                    t_w2use = [None, None]
                    t_stg_used = [None, None]
                    t_huse = mm.tok()
                    for tq in range(max(1, NCH)):
                        for dch in range(2):
                            ws = dch
                            t_w2t = dma(GE, ffws[1 + dch], w2tb[:, ws, :, :],
                                        w2t_sc[:, :, dch * 512:(dch + 1) * 512],
                                        deps=(t_w2use[ws], tk["tern"]))
                            if dch == 0:
                                for fc in range(32):
                                    bk = fc % 2
                                    use_bank(bk)
                                    TE.wait(t_w1t, t_xn)
                                    for a in range(DT):
                                        t_p1 = mm.mark(TE.raw.matmul(
                                            PS[bk], lhsT=w1tb[:, a, fc * 128:(fc + 1) * 128],
                                            rhs=xnb[:, a, tq * 512:(tq + 1) * 512],
                                            start=(a == 0), stop=(a == DT - 1)))
                                    SE.wait(t_p1, t_huse)
                                    bank_free[bk] = sc.mark(SE.raw.activation(
                                        hbuf[:, fc, :], PS[bk], ACT.Gelu,
                                        bias=b1c1[:, fc:fc + 1]))
                                t_h = sc.tok()
                            for qt in range(4):
                                bk = 2 + (qt % 4)
                                use_bank(bk)
                                TE.wait(t_h, t_w2t, t_b2f)
                                for fc in range(32):
                                    TE.raw.matmul(
                                        PS[bk], lhsT=hbuf[:, fc, qt * 128:(qt + 1) * 128],
                                        rhs=w2tb[:, ws, fc, :],
                                        start=(fc == 0), stop=False)
                                t_p2 = mm.mark(TE.raw.matmul(
                                    PS[bk], lhsT=ones_row_bf[:],
                                    rhs=b2r1[0:1, dch * 512:(dch + 1) * 512],
                                    start=False, stop=True))
                                sp = qt % 2
                                SE.wait(t_p2, t_stg_used[sp])
                                t_sg = sc.mark(SE.raw.copy(stg[:, sp, :], PS[bk]))
                                bank_free[bk] = t_sg
                                r0 = tq * 512 + qt * 128
                                t_acd = dma(GE, accs[sp],
                                            o_out[r0:r0 + 128, dch * 512:(dch + 1) * 512],
                                            stg[:, sp, :], deps=(t_sg, tk["spill"]),
                                            accum_op=OP.add)
                                t_stg_used[sp] = t_acd
                                final_outs.append(t_acd)
                            t_w2use[ws] = mm.tok()
                        t_huse = mm.tok()
                    barrier()

            st_S = ExitStack()
            S_full = st_S.enter_context(
                nc.sbuf_tensor("S_full", [128, T], F32, side="right"))
            layernorm_phase(True, S_full)
            if "xnT" in dbg_aps and upto < 3:
                final_outs.append(dma(SY, sd, dbg_aps["xnT"], xnT[:]))
            if upto >= 2:
                l0_attn(S_full, st_S)
            else:
                st_S.close()
            if "acc1" in dbg_aps:
                final_outs.append(dma(SY, sd, dbg_aps["acc1"], acc[:],
                                      deps=(vv.tok(), mm.tok(), sc.tok())))
            if upto >= 3:
                l0_ffn(tern_steps() if upto >= 4 else None)
            if "acc2" in dbg_aps:
                final_outs.append(dma(SY, sd, dbg_aps["acc2"], acc[:],
                                      deps=(vv.tok(), mm.tok(), sc.tok())))
            if upto >= 5:
                layernorm_phase(False, None)
                if "xnT" in dbg_aps:
                    final_outs.append(dma(SY, sd, dbg_aps["xnT"], xnT[:]))
                tk["spill"] = dma(SY, sd, o_out.rearrange("(i p) d -> p i d", p=128),
                                  acc[:], deps=(vv.tok(), mm.tok()))
                final_outs.append(tk["spill"])
                barrier()
                st_acc.close()
            if upto >= 6:
                l1_attn()
            if upto >= 7:
                l1_ffn()
            if upto < 5:
                tk["spill"] = dma(SY, sd, o_out.rearrange("(i p) d -> p i d", p=128),
                                  acc[:], deps=(vv.tok(), mm.tok(), sc.tok()))
                final_outs.append(tk["spill"])
            SY.wait(*[s.tok() for s in ALLSEMS])
            SY.wait(*final_outs)

    return nc


def host_inputs(inputs, T=2048):
    f32 = np.float32
    c = host_constants(T)
    WQ = np.ascontiguousarray(
        inputs["L1_attn_W_Q"].reshape(8, 4, 1024, 32).transpose(2, 0, 1, 3).reshape(1024, 1024))
    WK = np.ascontiguousarray(
        inputs["L1_attn_W_K"].reshape(8, 4, 1024, 32).transpose(2, 0, 1, 3).reshape(1024, 1024))
    WV = np.ascontiguousarray(inputs["L1_attn_W_V"][:, :, 0].T)
    WO = np.ascontiguousarray(inputs["L1_attn_W_O"][:, 0, :])
    bQ = np.ascontiguousarray(inputs["L1_attn_b_Q"].reshape(8, 4, 32).reshape(8, 128))
    bK = np.ascontiguousarray(inputs["L1_attn_b_K"].reshape(8, 4, 32).reshape(8, 128))
    bV = np.ascontiguousarray(inputs["L1_attn_b_V"][:, 0][None, :])
    vk = np.ascontiguousarray(
        inputs["L1_attn_virtual_k"].reshape(16, 8, 4, 32).transpose(1, 2, 3, 0).reshape(1024, 16))
    vvv = np.ascontiguousarray(inputs["L1_attn_virtual_v"][:, :, 0])
    base = {
        "L0_gamma": np.ascontiguousarray(inputs["L0_attn_gamma"].reshape(8, 128), dtype=f32),
        "L0_w_q": np.ascontiguousarray(inputs["L0_attn_w_q"], dtype=f32),
        "L0_w_kv": np.ascontiguousarray(inputs["L0_attn_w_kv"], dtype=f32),
        "L0_w_out": np.ascontiguousarray(inputs["L0_attn_w_out"], dtype=f32),
        "L0_WrT": np.ascontiguousarray(inputs["L0_ffn_Wr"].T, dtype=f32),
        "L0_br": np.ascontiguousarray(inputs["L0_ffn_br"][None, :], dtype=f32),
        "L0_W1": np.ascontiguousarray(inputs["L0_ffn_W1"], dtype=f32),
        "L0_b1": np.ascontiguousarray(inputs["L0_ffn_b1"], dtype=f32),
        "L0_W2": np.ascontiguousarray(inputs["L0_ffn_W2"], dtype=f32),
        "L0_b2": np.ascontiguousarray(inputs["L0_ffn_b2"], dtype=f32),
        "L1_WQ": WQ.astype(f32), "L1_WK": WK.astype(f32),
        "L1_WV": WV.astype(f32), "L1_WO": WO.astype(f32),
        "L1_bQ": bQ.astype(f32), "L1_bK": bK.astype(f32), "L1_bV": bV.astype(f32),
        "L1_vk": vk.astype(f32), "L1_vv": vvv.astype(f32),
        "L1_W1": np.ascontiguousarray(inputs["L1_ffn_W1"], dtype=f32),
        "L1_b1": np.ascontiguousarray(inputs["L1_ffn_b1"].reshape(32, 128), dtype=f32),
        "L1_W2": np.ascontiguousarray(inputs["L1_ffn_W2"], dtype=f32),
        "L1_b2": np.ascontiguousarray(inputs["L1_ffn_b2"][None, :], dtype=f32),
        "c_sin": c["sin"][:, :T], "c_cos": c["cos"][:, :T],
        "c_masks": np.ascontiguousarray(c["masks"]),
    }
    for nm in CSHAPES:
        base["c_" + nm] = np.ascontiguousarray(c[nm])
    base["c_sel32"] = np.ascontiguousarray(c["sel32"])
    x = np.asarray(inputs["L0_attn_x"], dtype=f32)
    in_maps = []
    for i in range(8):
        m = dict(base)
        m["x"] = np.ascontiguousarray(x[i, :T, :])
        in_maps.append(m)
    return in_maps


def kernel(**inputs):
    T = 2048
    nc = build(T=T)
    in_maps = host_inputs(inputs, T=T)
    res = run_bass_kernel_spmd(nc, in_maps, core_ids=list(range(8)))
    out = np.stack([np.asarray(r["o"], dtype=np.float32) for r in res.results], axis=0)
    return out


# version 11
# speedup vs baseline: 1.0502x; 1.0480x over previous
import numpy as np
import ml_dtypes
from contextlib import ExitStack

import concourse.bass as bass
import concourse.mybir as mybir
from concourse.bass_utils import run_bass_kernel_spmd

F32 = mybir.dt.float32
BF16 = mybir.dt.bfloat16
AX = mybir.AxisListType.X
OP = mybir.AluOpType
ACT = mybir.ActivationFunctionType

D = 1024
DT = 8
VKV = 16
ROT_BASE = 10000.0
EXP_SHIFT = -20.0
QSC = 0.25
TAYSC = 2.0 ** -0.25
L1QSC = 32.0 ** -0.5
bfdt = ml_dtypes.bfloat16


def host_constants(T):
    c = {}
    c["ident32"] = np.eye(128, dtype=np.float32)
    c["ident16"] = np.eye(128, dtype=np.float32).astype(bfdt)
    c["ones_row_bf"] = np.ones((1, 128), dtype=np.float32).astype(bfdt)
    c["ones_row_f32"] = np.ones((1, 128), dtype=np.float32)
    c["ones_col_f32"] = np.ones((128, 1), dtype=np.float32)
    pos = np.arange(T, dtype=np.float64)
    freq = ROT_BASE ** (np.arange(16, dtype=np.float64) / 16.0)
    ang = pos[None, :] / freq[(np.arange(128) % 16)][:, None]
    c["sin"] = np.sin(ang).astype(np.float32)
    c["cos"] = np.cos(ang).astype(np.float32)
    pf = np.zeros((128, 128), dtype=np.float32)
    for m in range(128):
        if m % 32 < 16:
            pf[m + 16, m] = -1.0
        else:
            pf[m - 16, m] = 1.0
    c["pflip"] = pf.astype(bfdt)
    kvl = np.arange(128)[:, None]
    ql = np.arange(512)[None, :]
    c["masks"] = np.stack(
        [(kvl - ql <= 16 - 128 * dd).astype(np.float32) for dd in range(5)]
    ).astype(bfdt)
    sel = np.zeros((32, 2048), dtype=np.float32)
    for h in range(32):
        sel[h, h * 64:(h + 1) * 64] = 1.0
    c["sel32"] = sel.astype(bfdt)
    c["iota8"] = np.broadcast_to(np.arange(8, dtype=np.float32), (128, 8)).copy()
    c["iota8big"] = (c["iota8"] + 1e6).astype(np.float32)
    return c


CSHAPES = dict(ident32=(128, 128, F32), ident16=(128, 128, BF16),
               ones_row_bf=(1, 128, BF16), ones_row_f32=(1, 128, F32),
               ones_col_f32=(128, 1, F32), pflip=(128, 128, BF16),
               iota8=(128, 8, F32), iota8big=(128, 8, F32))


class Sem:
    def __init__(self, es, nc, name, dma=False):
        self.h = es.enter_context(nc.semaphore(name))
        self.v = 0
        self.name = name
        self.dma = dma

    def mark(self, instr, n=1):
        instr.then_inc(self.h, n)
        self.v += n
        return (self, self.v)

    def tok(self):
        return (self, self.v)


class Eng:
    def __init__(self, raw):
        self.raw = raw
        self.seen = {}

    def wait(self, *deps):
        for d in deps:
            if d is None:
                continue
            if isinstance(d, (list, tuple)) and d and isinstance(d[0], (list, tuple, type(None), Sem)) \
                    and not (len(d) == 2 and isinstance(d[0], Sem)):
                self.wait(*d)
                continue
            sem, val = d
            if sem.dma:
                val = sem.v
            if val > self.seen.get(sem, 0):
                self.raw.wait_ge(sem.h, val)
                self.seen[sem] = val


def build(T=2048, upto=99, debug=(), l0sub=9):
    NT = T // 128
    NCH = T // 512
    nc = bass.Bass("TRN2", target_bir_lowering=False)
    dt = nc.dram_tensor

    x_in = dt("x", [T, D], F32, kind="ExternalInput").ap()
    g0 = dt("L0_gamma", [8, 128], F32, kind="ExternalInput").ap()
    wq0 = dt("L0_w_q", [D, 128], F32, kind="ExternalInput").ap()
    wkv0 = dt("L0_w_kv", [D, 256], F32, kind="ExternalInput").ap()
    wo0 = dt("L0_w_out", [128, D], F32, kind="ExternalInput").ap()
    wr0T = dt("L0_WrT", [8, D], F32, kind="ExternalInput").ap()
    br0 = dt("L0_br", [1, 8], F32, kind="ExternalInput").ap()
    w1_0 = dt("L0_W1", [8, D, 4096], F32, kind="ExternalInput").ap()
    b1_0 = dt("L0_b1", [8, 4096], F32, kind="ExternalInput").ap()
    w2_0 = dt("L0_W2", [8, 4096, D], F32, kind="ExternalInput").ap()
    b2_0 = dt("L0_b2", [8, D], F32, kind="ExternalInput").ap()
    wqf1 = dt("L1_WQ", [D, D], F32, kind="ExternalInput").ap()
    wkf1 = dt("L1_WK", [D, D], F32, kind="ExternalInput").ap()
    wvf1 = dt("L1_WV", [D, 2048], F32, kind="ExternalInput").ap()
    wof1 = dt("L1_WO", [2048, D], F32, kind="ExternalInput").ap()
    bq1 = dt("L1_bQ", [8, 128], F32, kind="ExternalInput").ap()
    bk1 = dt("L1_bK", [8, 128], F32, kind="ExternalInput").ap()
    bv1 = dt("L1_bV", [1, 2048], F32, kind="ExternalInput").ap()
    vk1 = dt("L1_vk", [D, VKV], F32, kind="ExternalInput").ap()
    vv1_in = dt("L1_vv", [VKV, 2048], F32, kind="ExternalInput").ap()
    w1_1 = dt("L1_W1", [4096, D], F32, kind="ExternalInput").ap()
    b1_1 = dt("L1_b1", [32, 128], F32, kind="ExternalInput").ap()
    w2_1 = dt("L1_W2", [D, 4096], F32, kind="ExternalInput").ap()
    b2_1 = dt("L1_b2", [1, D], F32, kind="ExternalInput").ap()
    ci = {nm: dt("c_" + nm, [p_, f_], ty, kind="ExternalInput").ap()
          for nm, (p_, f_, ty) in CSHAPES.items()}
    sel_in = dt("c_sel32", [32, 2048], BF16, kind="ExternalInput").ap()
    sin_in = dt("c_sin", [128, T], F32, kind="ExternalInput").ap()
    cos_in = dt("c_cos", [128, T], F32, kind="ExternalInput").ap()
    masks_in = dt("c_masks", [5, 128, 512], BF16, kind="ExternalInput").ap()

    o_out = dt("o", [T, D], F32, kind="ExternalOutput").ap()
    xnsp = dt("xnspill", [128, DT, T], BF16, kind="Internal").ap()
    w1t_sc = dt("w1t_sc", [128, DT, 4096], BF16, kind="Internal").ap()
    w2t_sc = dt("w2t_sc", [128, 32, D], BF16, kind="Internal").ap()

    dbg_specs = {"xnT": ([128, DT, T], BF16), "srow": ([1, T], F32),
                 "acc1": ([128, NT, D], F32), "acc2": ([128, NT, D], F32),
                 "G": ([128, NT, 8], F32),
                 "qrot": ([128, 8, T], BF16), "krot": ([128, 8, T + VKV], BF16),
                 "vext": ([128, (NT + 1) * 32 * 65], BF16)}
    dbg_aps = {nm: dt("dbg_" + nm, s, ty, kind="ExternalOutput").ap()
               for nm, (s, ty) in dbg_specs.items() if nm in debug}

    es = ExitStack()
    st_acc = ExitStack()
    st_xn = ExitStack()

    _uniq = [0]

    def sbl(st, name, shape, ty):
        _uniq[0] += 1
        return st.enter_context(nc.sbuf_tensor(f"{name}_{_uniq[0]}", list(shape), ty))

    xnT = st_xn.enter_context(nc.sbuf_tensor("xnT", [128, DT, T], BF16, side="right"))
    acc = st_acc.enter_context(nc.sbuf_tensor("acc", [128, NT, D], F32, side="right"))

    consts = {nm: es.enter_context(nc.sbuf_tensor("s_" + nm, [p_, f_], ty))
              for nm, (p_, f_, ty) in CSHAPES.items()}
    gamma_c = es.enter_context(nc.sbuf_tensor("gamma_c", [128, DT], F32))
    wq_b = es.enter_context(nc.sbuf_tensor("wq_b", [128, DT, 128], BF16))
    wkv_b = es.enter_context(nc.sbuf_tensor("wkv_b", [128, DT, 256], BF16))
    wr_b = es.enter_context(nc.sbuf_tensor("wr_b", [128, DT, 8], BF16))
    br_b = es.enter_context(nc.sbuf_tensor("br_b", [1, 8], BF16))
    b1c0 = es.enter_context(nc.sbuf_tensor("b1c0", [128, 8, 32], F32))
    bq_c = es.enter_context(nc.sbuf_tensor("bq_c", [128, 8], F32))
    bk_c = es.enter_context(nc.sbuf_tensor("bk_c", [128, 8], F32))
    b1c1 = es.enter_context(nc.sbuf_tensor("b1c1", [128, 32], F32))
    G = es.enter_context(nc.sbuf_tensor("G", [128, NT, 8], F32))

    pA = es.enter_context(nc.psum_tensor("pA", [128, 1024], F32))
    pB = es.enter_context(nc.psum_tensor("pB", [128, 1024], F32))
    pC = es.enter_context(nc.psum_tensor("pC", [128, 1024], F32))
    pD = es.enter_context(nc.psum_tensor("pD", [128, 1024], F32))
    PS = [pA[:, 0:512], pA[:, 512:1024], pB[:, 0:512], pB[:, 512:1024],
          pC[:, 0:512], pC[:, 512:1024], pD[:, 0:512], pD[:, 512:1024]]
    bank_free = [None] * 8

    sd = Sem(es, nc, "sd", dma=True)
    gd = Sem(es, nc, "gd", dma=True)
    gw = Sem(es, nc, "gw", dma=True)
    moes = [Sem(es, nc, "moe0", dma=True), Sem(es, nc, "moe1", dma=True)]
    tls = [Sem(es, nc, "tl0", dma=True), Sem(es, nc, "tl1", dma=True)]
    accs = [Sem(es, nc, "acc0", dma=True), Sem(es, nc, "acc1", dma=True)]
    ffws = [Sem(es, nc, "ffw1", dma=True), Sem(es, nc, "ffw2", dma=True),
            Sem(es, nc, "ffw3", dma=True)]
    b2ss = [Sem(es, nc, "b20", dma=True), Sem(es, nc, "b21", dma=True)]
    tst = Sem(es, nc, "tst", dma=True)
    mm = Sem(es, nc, "mm")
    vv = Sem(es, nc, "vv")
    sc = Sem(es, nc, "sc")
    ALLSEMS = (sd, gd, gw, moes[0], moes[1], tls[0], tls[1], b2ss[0], b2ss[1],
               accs[0], accs[1], ffws[0], ffws[1], ffws[2], tst, mm, vv, sc)

    ident32 = consts["ident32"]
    ident16 = consts["ident16"]
    ones_row_bf = consts["ones_row_bf"]
    ones_row_f32 = consts["ones_row_f32"]
    tk = {}
    final_outs = []

    with es, nc.Block() as block:
        @block.sync
        def _(sync_unused):
            TE = Eng(nc.tensor)
            VE = Eng(nc.vector)
            SE = Eng(nc.scalar)
            GE = Eng(nc.gpsimd)
            SY = Eng(nc.sync)

            def dma(q, sem, out, in_, deps=(), **kw):
                q.wait(*deps)
                return sem.mark(q.raw.dma_start(out=out, in_=in_, **kw), 16)

            def use_bank(bk):
                TE.wait(bank_free[bk])

            def barrier():
                VE.raw.drain()
                toks = [s.tok() for s in ALLSEMS]
                for E in (TE, VE, SE, GE, SY):
                    E.wait(*toks)

            def small_transpose(src_ap, dst_ap, bk, p_, f_, deps=()):
                use_bank(bk)
                TE.wait(tk.get("consts"), *deps)
                tr = mm.mark(TE.raw.matmul(PS[bk][0:f_, 0:p_], lhsT=src_ap,
                                           rhs=ident32[0:p_, 0:p_], start=True, stop=True))
                VE.wait(tr)
                t = vv.mark(VE.raw.tensor_copy(dst_ap, PS[bk][0:f_, 0:p_]))
                bank_free[bk] = t
                return t

            for nm in CSHAPES:
                tk["consts"] = dma(SY, sd, consts[nm][:], ci[nm])
            tk["x"] = dma(SY, sd, acc[:], x_in.rearrange("(i p) d -> p i d", p=128))

            with ExitStack() as st0:
                graw = sbl(st0, "graw", (8, 128), F32)
                wqf = sbl(st0, "wqf", (128, DT, 384), F32)
                wrT = sbl(st0, "wrT", (8, DT, 128), F32)
                b1row = sbl(st0, "b1row", (32, 8, 128), F32)
                bqrow = sbl(st0, "bqrow", (8, 2, 128), F32)
                b1frow = sbl(st0, "b1frow", (32, 128), F32)

                t_g = dma(GE, gd, graw[:], g0)
                t_wq = dma(GE, gd, wqf[:, :, 0:128], wq0.rearrange("(a p) m -> p a m", p=128))
                t_wkv = dma(GE, gd, wqf[:, :, 128:384], wkv0.rearrange("(a p) m -> p a m", p=128))
                t_wr = dma(GE, gd, wrT[:], wr0T.rearrange("e (a p) -> e a p", p=128))
                t_br = dma(GE, gd, br_b[:], br0)
                t_b1r = dma(GE, gd, b1row[:], b1_0.rearrange("e (c p) -> c e p", p=128),
                            allow_slow_non_contiguous=True)
                t_bq = dma(GE, gd, bqrow[:, 0, :], bq1)
                t_bk = dma(GE, gd, bqrow[:, 1, :], bk1)
                t_b1f = dma(GE, gd, b1frow[:], b1_1)

                small_transpose(graw[:], gamma_c[:], 6, 8, 128, deps=(t_g,))
                t_gc = vv.tok()
                VE.wait(t_wq, t_wkv, t_gc)
                VE.raw.drain()
                for a in range(DT):
                    VE.raw.tensor_scalar(out=wq_b[:, a, :], in0=wqf[:, a, 0:128],
                                         scalar1=gamma_c[:, a:a + 1], scalar2=QSC,
                                         op0=OP.mult, op1=OP.mult)
                    tk["wfold"] = vv.mark(VE.raw.tensor_scalar(
                        out=wkv_b[:, a, :], in0=wqf[:, a, 128:384],
                        scalar1=gamma_c[:, a:a + 1], scalar2=None, op0=OP.mult))
                with ExitStack() as st00:
                    wrtmp = sbl(st00, "wrtmp", (128, DT, 8), F32)
                    for a in range(DT):
                        small_transpose(wrT[:, a, :], wrtmp[:, a, :], a % 2, 8, 128,
                                        deps=(t_wr,))
                    VE.raw.drain()
                    tk["wr"] = vv.mark(VE.raw.tensor_copy(wr_b[:], wrtmp[:]))
                for e in range(8):
                    small_transpose(b1row[:, e, :], b1c0[:, e, :], 2 + e % 2, 32, 128,
                                    deps=(t_b1r,))
                small_transpose(bqrow[:, 0, :], bq_c[:], 4, 8, 128, deps=(t_bq,))
                small_transpose(bqrow[:, 1, :], bk_c[:], 5, 8, 128, deps=(t_bk,))
                small_transpose(b1frow[:], b1c1[:], 6, 32, 128, deps=(t_b1f,))
                tk["smallpar"] = vv.tok()
                barrier()

            def layernorm_phase(want_s, S_full):
                with ExitStack() as st:
                    xtmp = sbl(st, "ln_xtmp", (128, 2, D), F32)
                    sq_t = sbl(st, "ln_sq", (128, 2, D), F32)
                    cols = sbl(st, "ln_cols", (128, 8, NT), F32)
                    s_tok = sbl(st, "ln_stok", (128, NT), F32)
                    s_row = sbl(st, "ln_srow", (1, T), F32)
                    t_sq_rd = [None, None]
                    for i in range(NT):
                        VE.raw.tensor_reduce(cols[:, 0, i:i + 1], acc[:, i, :], AX, OP.add)
                        sl = i % 2
                        SE.wait(t_sq_rd[sl])
                        t_sqw = sc.mark(SE.raw.activation(sq_t[:, sl, :], acc[:, i, :],
                                                          ACT.Square))
                        VE.wait(t_sqw)
                        t_sq_rd[sl] = vv.mark(VE.raw.tensor_reduce(
                            cols[:, 1, i:i + 1], sq_t[:, sl, :], AX, OP.add))
                    VE.raw.drain()
                    VE.raw.tensor_scalar(out=cols[:, 2, :], in0=cols[:, 0, :],
                                         scalar1=-1.0 / D, scalar2=None, op0=OP.mult)
                    VE.raw.tensor_scalar(out=cols[:, 3, :], in0=cols[:, 1, :],
                                         scalar1=1.0 / D, scalar2=None, op0=OP.mult)
                    VE.raw.drain()
                    VE.raw.tensor_tensor(out=cols[:, 4, :], in0=cols[:, 2, :],
                                         in1=cols[:, 2, :], op=OP.mult)
                    VE.raw.drain()
                    VE.raw.tensor_tensor(out=cols[:, 4, :], in0=cols[:, 3, :],
                                         in1=cols[:, 4, :], op=OP.subtract)
                    VE.raw.drain()
                    VE.raw.tensor_scalar(out=cols[:, 5, :], in0=cols[:, 4, :],
                                         scalar1=1e-5, scalar2=None, op0=OP.add)
                    VE.raw.drain()
                    t_rp = vv.mark(VE.raw.reciprocal(cols[:, 6, :], cols[:, 5, :]))
                    SE.wait(t_rp)
                    t_rstd = sc.mark(SE.raw.activation(cols[:, 7, :], cols[:, 6, :],
                                                       ACT.Sqrt))
                    if want_s:
                        VE.wait(t_rstd)
                        VE.raw.tensor_tensor(out=cols[:, 5, :], in0=cols[:, 4, :],
                                             in1=cols[:, 6, :], op=OP.mult)
                        VE.raw.drain()
                        VE.raw.tensor_scalar(out=cols[:, 5, :], in0=cols[:, 5, :],
                                             scalar1=9.765625e-28, scalar2=None, op0=OP.max)
                        VE.raw.drain()
                        t_si = vv.mark(VE.raw.reciprocal(cols[:, 5, :], cols[:, 5, :]))
                        SE.wait(t_si)
                        t_sv = sc.mark(SE.raw.activation(s_tok[:], cols[:, 5, :], ACT.Sqrt))
                    t_xuse = [None, None]
                    for i in range(NT):
                        sl = i % 2
                        VE.wait(t_rstd, t_xuse[sl])
                        VE.raw.drain()
                        t_xn = vv.mark(VE.raw.tensor_scalar(
                            out=xtmp[:, sl, :], in0=acc[:, i, :],
                            scalar1=cols[:, 2, i:i + 1],
                            scalar2=cols[:, 7, i:i + 1], op0=OP.add, op1=OP.mult))
                        TE.wait(t_xn, tk["consts"])
                        for a in range(DT):
                            bk = a % 2
                            use_bank(bk)
                            tr = mm.mark(TE.raw.transpose(
                                PS[bk][:, 0:128], xtmp[:, sl, a * 128:(a + 1) * 128],
                                ident32[:]))
                            VE.wait(tr)
                            bank_free[bk] = vv.mark(VE.raw.tensor_copy(
                                xnT[:, a, i * 128:(i + 1) * 128], PS[bk][:, 0:128]))
                        t_xuse[sl] = tr
                    if want_s:
                        t_stok = sc.tok()
                        for i in range(NT):
                            bk = 2 + i % 2
                            use_bank(bk)
                            TE.wait(t_stok, tk["consts"])
                            tr = mm.mark(TE.raw.matmul(
                                PS[bk][0:1, 0:128], lhsT=s_tok[:, i:i + 1],
                                rhs=ident32[:], start=True, stop=True))
                            SE.wait(tr)
                            bank_free[bk] = sc.mark(SE.raw.copy(
                                s_row[0:1, i * 128:(i + 1) * 128], PS[bk][0:1, 0:128]))
                        t_srow = sc.tok()
                        if "srow" in dbg_aps:
                            final_outs.append(dma(SY, sd, dbg_aps["srow"], s_row[:],
                                                  deps=(t_srow,)))
                        for cch in range(NCH):
                            bk = 4 + cch % 2
                            use_bank(bk)
                            TE.wait(t_srow, tk["consts"])
                            tr = mm.mark(TE.raw.matmul(
                                PS[bk], lhsT=ones_row_f32[:],
                                rhs=s_row[0:1, cch * 512:(cch + 1) * 512],
                                start=True, stop=True))
                            VE.wait(tr)
                            bank_free[bk] = vv.mark(VE.raw.tensor_copy(
                                S_full[:, cch * 512:(cch + 1) * 512], PS[bk]))
                    barrier()

            def l0_attn(S_full, st_S):
                st_l0 = ExitStack()
                qT = sbl(st_l0, "qT", (128, T), BF16)
                kT = sbl(st_l0, "kT", (128, T), BF16)
                vT = sbl(st_l0, "vT", (128, T), BF16)
                q_aug = sbl(st_l0, "q_aug", (17, 8, T), BF16)
                q_tok = sbl(st_l0, "q_tok", (128, NT, 128), BF16)
                with st_l0:
                    t_ev = {0: None, 1: None, 2: None}
                    for cch in range(NCH):
                        for wi in range(3):
                            buf = (qT, kT, vT)[wi]
                            bk = (cch * 3 + wi) % 4
                            use_bank(bk)
                            TE.wait(tk["wfold"])
                            for a in range(DT):
                                lhsT = (wq_b[:, a, :] if wi == 0
                                        else wkv_b[:, a, (wi - 1) * 128:wi * 128])
                                t_mm = mm.mark(TE.raw.matmul(
                                    PS[bk], lhsT=lhsT, rhs=xnT[:, a, cch * 512:(cch + 1) * 512],
                                    start=(a == 0), stop=(a == DT - 1)))
                            VE.wait(t_mm)
                            t_c = vv.mark(VE.raw.tensor_tensor(
                                out=buf[:, cch * 512:(cch + 1) * 512], in0=PS[bk],
                                in1=S_full[:, cch * 512:(cch + 1) * 512], op=OP.mult))
                            bank_free[bk] = t_c
                            t_ev[wi] = t_c
                    barrier()
                    st_S.close()
                    if l0sub < 2:
                        return

                    kvA = sbl(st_l0, "kvA", (17, 8, 17), BF16)
                    kvB = sbl(st_l0, "kvB", (128, 8, 17), BF16)
                    kvC = sbl(st_l0, "kvC", (128, 8, 17), BF16)
                    with ExitStack() as st:
                        k_tok = sbl(st, "k_tok", (128, NT, 8, 17), BF16)
                        v_tok = sbl(st, "v_tok", (128, NT, 8, 17), BF16)
                        ksc = sbl(st, "ksc", (128, 8, 16), BF16)
                        kf2 = sbl(st, "kf2", (128, 2, 8, 256), BF16)
                        kvS = sbl(st, "kvS", (17, 8, 273), BF16)

                        t_qaug = [vv.mark(VE.raw.memset(q_aug[0:1, :, :], 1.0))]
                        for h in range(8):
                            t_qaug.append(dma(SY, sd, q_aug[1:17, h, :],
                                              qT[h * 16:(h + 1) * 16, :], deps=(t_ev[0],)))
                        t_kones = vv.mark(VE.raw.memset(k_tok[:, :, :, 0:1], 1.0))
                        VE.raw.drain()
                        t_vones = vv.mark(VE.raw.memset(v_tok[:, :, :, 0:1], 1.0))

                        t_tok = [[None] * 3 for _ in range(NT)]
                        for i in range(NT):
                            for wi, src in enumerate((qT, kT, vT)):
                                bk = (i * 3 + wi) % 4
                                use_bank(bk)
                                TE.wait(t_ev[wi], tk["consts"])
                                tr = mm.mark(TE.raw.matmul(
                                    PS[bk][:, 0:128], lhsT=src[:, i * 128:(i + 1) * 128],
                                    rhs=ident16[:], start=True, stop=True))
                                VE.wait(tr)
                                if wi == 0:
                                    t_tok[i][wi] = vv.mark(VE.raw.tensor_copy(
                                        q_tok[:, i, :], PS[bk][:, 0:128]))
                                else:
                                    dst = (k_tok if wi == 1 else v_tok)[:, i, :, 1:17]
                                    t_tok[i][wi] = vv.mark(VE.raw.tensor_copy(
                                        dst, PS[bk][:, 0:128].rearrange("p (h d) -> p h d", h=8)))
                                bank_free[bk] = t_tok[i][wi]

                        if l0sub < 3:
                            barrier()
                            return
                        kvps = [pA[0:17, 0:273], pA[0:17, 512:785],
                                pB[0:17, 0:273], pB[0:17, 512:785],
                                pC[0:17, 0:273], pC[0:17, 512:785],
                                pD[0:17, 0:273], pD[0:17, 512:785]]
                        t_kf2_used = None
                        for i in range(NT):
                            par = i % 2
                            VE.wait(t_tok[i][1], t_tok[i][2], t_kones, t_vones, t_kf2_used)
                            VE.raw.drain()
                            VE.raw.tensor_scalar(out=ksc[:], in0=k_tok[:, i, :, 1:17],
                                                 scalar1=TAYSC, scalar2=None, op0=OP.mult)
                            VE.raw.drain()
                            for j in range(16):
                                t_kf2 = vv.mark(VE.raw.tensor_tensor(
                                    out=kf2[:, par, :, j * 16:(j + 1) * 16], in0=ksc[:],
                                    in1=ksc[:, :, j:j + 1].broadcast_to([128, 8, 16]),
                                    op=OP.mult))
                            TE.wait(t_kf2)
                            if i == 0:
                                for bk in range(8):
                                    use_bank(bk)
                            for h in range(8):
                                TE.raw.matmul(kvps[h][:, 0:17], lhsT=v_tok[:, i, h, :],
                                              rhs=k_tok[:, i, h, :],
                                              start=(i == 0), stop=False)
                                t_kv = mm.mark(TE.raw.matmul(
                                    kvps[h][:, 17:273], lhsT=v_tok[:, i, h, :],
                                    rhs=kf2[:, par, h, :],
                                    start=False, stop=(i == NT - 1)))
                            if i >= 1:
                                t_kf2_used = t_kv

                        t_kvS = []
                        for h in range(8):
                            SE.wait(t_kv)
                            t_kvS.append(sc.mark(SE.raw.copy(kvS[:, h, :], kvps[h])))
                            bank_free[h] = t_kvS[-1]
                        for h in range(8):
                            for pi, (lo, hi, dst) in enumerate(((0, 17, kvA), (17, 145, kvB),
                                                                (145, 273, kvC))):
                                bk = (h * 3 + pi) % 2
                                use_bank(bk)
                                TE.wait(t_kvS[h])
                                tr = mm.mark(TE.raw.matmul(
                                    PS[bk][0:(hi - lo), 0:17], lhsT=kvS[:, h, lo:hi],
                                    rhs=ident16[0:17, 0:17], start=True, stop=True))
                                VE.wait(tr)
                                bank_free[bk] = vv.mark(VE.raw.tensor_copy(
                                    dst[:, h, :], PS[bk][0:(hi - lo), 0:17]))
                        t_kvT = vv.tok()
                        barrier()

                    if l0sub < 4:
                        return
                    with ExitStack() as st:
                        qsc = sbl(st, "qsc", (128, 8, 16), BF16)
                        qf2 = sbl(st, "qf2", (128, 8, 256), BF16)
                        qf2T = sbl(st, "qf2T", (128, 2, 8, 512), BF16)
                        outS = sbl(st, "outS", (17, 8, 512), BF16)
                        recS = sbl(st, "recS", (1, 2, 512), F32)
                        wout2 = sbl(st, "wout2", (17, 8, D), BF16)
                        t_wo2z = vv.mark(VE.raw.memset(wout2[0:1, :, :], 0.0))
                        t_wo2 = dma(GE, gd, wout2[1:17, :, :],
                                    wo0.rearrange("(h e) m -> e h m", e=16),
                                    allow_slow_non_contiguous=True)
                        for cch in range(NCH):
                            t_qf2T = []
                            t_qf2_used = mm.tok()
                            for jj in range(4):
                                i = 4 * cch + jj
                                VE.wait(t_tok[i][0], t_qf2_used)
                                VE.raw.drain()
                                VE.raw.tensor_scalar(
                                    out=qsc[:], in0=q_tok[:, i, :].rearrange(
                                        "p (h d) -> p h d", h=8),
                                    scalar1=TAYSC, scalar2=None, op0=OP.mult)
                                VE.raw.drain()
                                for j in range(16):
                                    t_qf2 = vv.mark(VE.raw.tensor_tensor(
                                        out=qf2[:, :, j * 16:(j + 1) * 16], in0=qsc[:],
                                        in1=qsc[:, :, j:j + 1].broadcast_to([128, 8, 16]),
                                        op=OP.mult))
                                TE.wait(t_qf2)
                                for h in range(8):
                                    for piece in range(2):
                                        bk = (h * 2 + piece) % 2
                                        use_bank(bk)
                                        tr = mm.mark(TE.raw.matmul(
                                            PS[bk][:, 0:128],
                                            lhsT=qf2[:, h, piece * 128:(piece + 1) * 128],
                                            rhs=ident16[:], start=True, stop=True))
                                        VE.wait(tr)
                                        bank_free[bk] = vv.mark(VE.raw.tensor_copy(
                                            qf2T[:, piece, h, jj * 128:(jj + 1) * 128],
                                            PS[bk][:, 0:128]))
                                        t_qf2T.append(bank_free[bk])
                                t_qf2_used = mm.tok()
                            if l0sub < 5:
                                continue
                            t_sc_h = []
                            for h in range(8):
                                bko = 2 + h % 2
                                use_bank(bko)
                                ps17 = PS[bko][0:17, :]
                                TE.wait(t_qaug, t_qf2T[-1], t_kvT)
                                TE.raw.matmul(ps17, lhsT=kvA[:, h, :],
                                              rhs=q_aug[:, h, cch * 512:(cch + 1) * 512],
                                              start=True, stop=False)
                                TE.raw.matmul(ps17, lhsT=kvB[:, h, :], rhs=qf2T[:, 0, h, :],
                                              start=False, stop=False)
                                t3 = mm.mark(TE.raw.matmul(ps17, lhsT=kvC[:, h, :],
                                                           rhs=qf2T[:, 1, h, :],
                                                           start=False, stop=True))
                                SE.wait(t3)
                                t_outS = sc.mark(SE.raw.copy(outS[:, h, :], ps17))
                                if l0sub == 50:
                                    bank_free[bko] = t_outS
                                    t_sc_h.append(t_outS)
                                    continue
                                t_den = sc.mark(SE.raw.copy(recS[0:1, 0, :], ps17[0:1, :]))
                                bank_free[bko] = t_den
                                VE.wait(t_den)
                                VE.raw.drain()
                                VE.raw.tensor_scalar(out=recS[0:1, 0, :],
                                                     in0=recS[0:1, 0, :],
                                                     scalar1=1e-5, scalar2=None, op0=OP.max)
                                VE.raw.drain()
                                t_rec = vv.mark(VE.raw.reciprocal(recS[0:1, 1, :],
                                                                  recS[0:1, 0, :]))
                                if l0sub == 51:
                                    t_sc_h.append(t_rec)
                                    continue
                                bk4 = 4 + h % 2
                                use_bank(bk4)
                                TE.wait(t_rec)
                                mmb = mm.mark(TE.raw.matmul(PS[bk4][0:17, :],
                                                            lhsT=ones_row_f32[0:1, 0:17],
                                                            rhs=recS[0:1, 1, :],
                                                            start=True, stop=True))
                                if l0sub == 52:
                                    bank_free[bk4] = mmb
                                    t_sc_h.append(mmb)
                                    continue
                                VE.wait(mmb, t_outS)
                                t_sch = vv.mark(VE.raw.tensor_tensor(
                                    out=outS[:, h, :], in0=outS[:, h, :],
                                    in1=PS[bk4][0:17, :], op=OP.mult))
                                bank_free[bk4] = t_sch
                                t_sc_h.append(t_sch)
                            if l0sub < 6 or l0sub in (50, 51, 52):
                                continue
                            for tt_ in range(4):
                                for dch in range(2):
                                    bk = 6 + (tt_ * 2 + dch) % 2
                                    use_bank(bk)
                                    TE.wait(t_sc_h, t_wo2z, t_wo2)
                                    for h in range(8):
                                        t_wo = mm.mark(TE.raw.matmul(
                                            PS[bk], lhsT=outS[:, h, tt_ * 128:(tt_ + 1) * 128],
                                            rhs=wout2[:, h, dch * 512:(dch + 1) * 512],
                                            start=(h == 0), stop=(h == 7)))
                                    VE.wait(t_wo)
                                    aslc = acc[:, 4 * cch + tt_, dch * 512:(dch + 1) * 512]
                                    bank_free[bk] = vv.mark(VE.raw.tensor_tensor(
                                        out=aslc, in0=aslc, in1=PS[bk], op=OP.add))
                    barrier()

            def l0_ffn(tern_gen):
                with ExitStack() as st:
                    W1q = sbl(st, "W1q", (128, 2, DT, 1024), BF16)
                    W2q = sbl(st, "W2q", (128, 2, 8, 1024), BF16)
                    h1 = sbl(st, "h1", (128, 8, 512), BF16)
                    b2s = sbl(st, "b2s", (1, 2, D), BF16)
                    lgt = sbl(st, "lgt", (128, 8), F32)
                    lgt2 = sbl(st, "lgt2", (128, 8), F32)
                    colsr = sbl(st, "colsr", (128, 6), F32)
                    for i in range(NT):
                        bk = 6 + i % 2
                        use_bank(bk)
                        TE.wait(tk["wr"], tk.get("smallpar"))
                        psl = PS[bk][:, 0:8]
                        for a in range(DT):
                            TE.raw.matmul(psl, lhsT=xnT[:, a, i * 128:(i + 1) * 128],
                                          rhs=wr_b[:, a, :], start=(a == 0), stop=False)
                        t_lg = mm.mark(TE.raw.matmul(psl, lhsT=ones_row_bf[:],
                                                     rhs=br_b[:], start=False, stop=True))
                        VE.wait(t_lg)
                        VE.raw.tensor_reduce(colsr[:, 0:1], psl, AX, OP.max)
                        VE.raw.drain()
                        t_mx = vv.mark(VE.raw.tensor_scalar(
                            out=colsr[:, 1:2], in0=colsr[:, 0:1],
                            scalar1=-1.0, scalar2=None, op0=OP.mult))
                        SE.wait(t_mx)
                        t_e = sc.mark(SE.raw.activation(lgt[:], psl, ACT.Exp,
                                                        bias=colsr[:, 1:2]))
                        VE.wait(t_e)
                        VE.raw.tensor_reduce(colsr[:, 2:3], lgt[:], AX, OP.add)
                        VE.raw.drain()
                        VE.raw.reciprocal(colsr[:, 3:4], colsr[:, 2:3])
                        VE.raw.tensor_scalar(out=lgt2[:], in0=psl, scalar1=colsr[:, 0:1],
                                             scalar2=-1e6, op0=OP.is_ge, op1=OP.mult)
                        VE.raw.drain()
                        bank_free[bk] = vv.mark(VE.raw.tensor_tensor(
                            out=lgt2[:], in0=lgt2[:], in1=consts["iota8big"][:], op=OP.add))
                        VE.raw.drain()
                        VE.raw.tensor_reduce(colsr[:, 4:5], lgt2[:], AX, OP.min)
                        VE.raw.drain()
                        VE.raw.tensor_scalar(out=lgt2[:], in0=consts["iota8"][:],
                                             scalar1=colsr[:, 4:5], scalar2=None,
                                             op0=OP.is_equal)
                        VE.raw.drain()
                        t_G = vv.mark(VE.raw.tensor_scalar(
                            out=G[:, i, :], in0=lgt2[:], scalar1=colsr[:, 3:4],
                            scalar2=None, op0=OP.mult))
                    if "G" in dbg_aps:
                        final_outs.append(dma(SY, sd, dbg_aps["G"], G[:], deps=(t_G,)))

                    t_w1use = [None, None]
                    t_w2use = [None, None]
                    t_b2use = [None, None]
                    t_h1use = mm.tok()
                    for e in range(8):
                        sl = e % 2
                        t_b2 = dma(GE, b2ss[sl], b2s[0:1, sl:sl + 1, :],
                                   b2_0[e:e + 1, :].rearrange("(o e) d -> o e d", o=1),
                                   deps=(t_b2use[sl],))
                        for qq in range(4):
                            s = (e * 4 + qq) % 2
                            t_w1 = dma(GE, moes[s], W1q[:, s, :, :],
                                       w1_0[e].rearrange("(a p) f -> p a f", p=128)
                                       [:, :, qq * 1024:(qq + 1) * 1024],
                                       deps=(t_w1use[s],))
                            t_w2 = dma(GE, moes[s], W2q[:, s, :, :],
                                       w2_0[e].rearrange("(b p) d -> p b d", p=128)
                                       [:, qq * 8:(qq + 1) * 8, :],
                                       deps=(t_w2use[s],))
                            for tq in range(max(1, NCH)):
                                for ft in range(8):
                                    bk = ft % 2
                                    use_bank(bk)
                                    TE.wait(t_w1)
                                    for a in range(DT):
                                        t_p1 = mm.mark(TE.raw.matmul(
                                            PS[bk], lhsT=W1q[:, s, a, ft * 128:(ft + 1) * 128],
                                            rhs=xnT[:, a, tq * 512:(tq + 1) * 512],
                                            start=(a == 0), stop=(a == DT - 1)))
                                    SE.wait(t_p1, t_h1use)
                                    bank_free[bk] = sc.mark(SE.raw.activation(
                                        h1[:, ft, :], PS[bk], ACT.Relu,
                                        bias=b1c0[:, e, qq * 8 + ft:qq * 8 + ft + 1]))
                                t_h1 = sc.tok()
                                VE.raw.drain()
                                for dch in range(2):
                                    for ts_ in range(4):
                                        bk = 2 + (dch * 4 + ts_) % 4
                                        use_bank(bk)
                                        TE.wait(t_h1, t_w2, t_G, t_b2)
                                        for ft in range(8):
                                            ins = TE.raw.matmul(
                                                PS[bk],
                                                lhsT=h1[:, ft, ts_ * 128:(ts_ + 1) * 128],
                                                rhs=W2q[:, s, ft, dch * 512:(dch + 1) * 512],
                                                start=(ft == 0),
                                                stop=(ft == 7 and qq != 0))
                                            if ft == 7 and qq == 0:
                                                ins = TE.raw.matmul(
                                                    PS[bk], lhsT=ones_row_bf[:],
                                                    rhs=b2s[0:1, sl, dch * 512:(dch + 1) * 512],
                                                    start=False, stop=True)
                                        t_p2 = mm.mark(ins)
                                        VE.wait(t_p2)
                                        aslc = acc[:, tq * 4 + ts_, dch * 512:(dch + 1) * 512]
                                        bank_free[bk] = vv.mark(VE.raw.scalar_tensor_tensor(
                                            out=aslc, in0=PS[bk],
                                            scalar=G[:, tq * 4 + ts_, e:e + 1],
                                            in1=aslc, op0=OP.mult, op1=OP.add))
                                t_h1use = mm.tok()
                            t_w1use[s] = mm.tok()
                            t_w2use[s] = t_w1use[s]
                            if tern_gen is not None:
                                for _ in range(2 if e < 6 else 5):
                                    if next(tern_gen, "END") == "END":
                                        tern_gen = None
                                        break
                        t_b2use[sl] = mm.tok()
                    if tern_gen is not None:
                        for _ in tern_gen:
                            pass
                    barrier()

            def tern_steps():
                st = ExitStack()
                wbuf = sbl(st, "wbuf", (128, 2, 2048), F32)
                redc = sbl(st, "redc", (128, 52), F32)
                thr = sbl(st, "thr", (1, 4), F32)
                thrc = sbl(st, "thrc", (128, 4), F32)
                ttmp = sbl(st, "ttmp", (128, 4, 128), BF16)
                buf1 = sbl(st, "buf1", (128, 8, 256), BF16)
                buf2 = sbl(st, "buf2", (128, 4, 512), BF16)
                slot_used = [None, None]
                t_thrc = None
                tern_ctr = [0]

                def tern_group4(bk, srcs, dsts, thc, deps):
                    use_bank(bk)
                    TE.wait(tk["consts"], *deps)
                    for i4, s_ap in enumerate(srcs):
                        tr = mm.mark(TE.raw.transpose(
                            PS[bk][:, i4 * 128:(i4 + 1) * 128], s_ap, ident32[:]))
                    VE.wait(tr, t_thrc)
                    VE.raw.drain()
                    for i4 in range(4):
                        VE.raw.tensor_scalar(out=ttmp[:, i4, :],
                                             in0=PS[bk][:, i4 * 128:(i4 + 1) * 128],
                                             scalar1=thrc[:, thc + 1:thc + 2],
                                             scalar2=None, op0=OP.is_lt)
                    VE.raw.drain()
                    for i4 in range(4):
                        tt_ = vv.mark(VE.raw.scalar_tensor_tensor(
                            out=dsts[i4], in0=PS[bk][:, i4 * 128:(i4 + 1) * 128],
                            scalar=thrc[:, thc:thc + 1],
                            in1=ttmp[:, i4, :], op0=OP.is_gt, op1=OP.subtract))
                    bank_free[bk] = tt_
                    return tt_

                def p1_src(t):
                    if t < 32:
                        return w1_1[t * 128:(t + 1) * 128, :], redc[:, t:t + 1], 1024
                    tt = t - 32
                    return (w2_1[(tt // 2) * 128:(tt // 2 + 1) * 128,
                                 (tt % 2) * 2048:(tt % 2 + 1) * 2048],
                            redc[:, 32 + tt:33 + tt], 2048)

                t_l = [None, None]
                srcap, _, rl = p1_src(0)
                t_l[0] = dma(SY, tls[0], wbuf[:, 0, 0:rl], srcap, deps=(None,))
                for t in range(48):
                    s = t % 2
                    if t + 1 < 48:
                        srcap, _, rl1 = p1_src(t + 1)
                        t_l[(t + 1) % 2] = dma(SY, tls[(t + 1) % 2],
                                               wbuf[:, (t + 1) % 2, 0:rl1], srcap,
                                               deps=(slot_used[(t + 1) % 2],))
                    _, cdst, rl0 = p1_src(t)
                    VE.wait(t_l[s])
                    slot_used[s] = vv.mark(VE.raw.tensor_reduce(
                        cdst, wbuf[:, s, 0:rl0], AX, OP.add,
                        apply_absolute_value=True))
                    if t % 2 == 1:
                        yield
                for wi in range(2):
                    VE.raw.drain()
                    csrc = redc[:, 0:32] if wi == 0 else redc[:, 32:48]
                    rcol = redc[:, 48 + wi:49 + wi]
                    t_red = vv.mark(VE.raw.tensor_reduce(rcol, csrc, AX, OP.add))
                    use_bank(6)
                    TE.wait(t_red)
                    t_ps = mm.mark(TE.raw.matmul(PS[6][0:1, 0:1], lhsT=rcol,
                                                 rhs=consts["ones_col_f32"][:],
                                                 start=True, stop=True))
                    VE.wait(t_ps)
                    nelem = float(1024 * 4096)
                    VE.raw.tensor_scalar(out=thr[0:1, 0:1], in0=PS[6][0:1, 0:1],
                                         scalar1=1.0 / nelem, scalar2=1e-8,
                                         op0=OP.mult, op1=OP.max)
                    VE.raw.drain()
                    VE.raw.tensor_scalar(out=thr[0:1, 1:2], in0=thr[0:1, 0:1],
                                         scalar1=0.5, scalar2=None, op0=OP.mult)
                    VE.raw.drain()
                    t_thr = vv.mark(VE.raw.tensor_scalar(
                        out=thr[0:1, 2:3], in0=thr[0:1, 1:2],
                        scalar1=-1.0, scalar2=None, op0=OP.mult))
                    bank_free[6] = t_thr
                    use_bank(6)
                    TE.wait(t_thr)
                    t_bc = mm.mark(TE.raw.matmul(PS[6][:, 0:2],
                                                 lhsT=ones_row_f32[:],
                                                 rhs=thr[0:1, 1:3], start=True, stop=True))
                    VE.wait(t_bc)
                    bank_free[6] = vv.mark(VE.raw.tensor_copy(
                        thrc[:, 2 * wi:2 * wi + 2], PS[6][:, 0:2]))
                    t_thrc = vv.tok()
                yield

                t_st = None
                w1r = w1_1.rearrange("(t p) d -> p t d", p=128)
                t_l[0] = dma(SY, tls[0],
                             wbuf[:, 0, 0:2048].rearrange("p (t d) -> p t d", t=2),
                             w1r[:, 0:2, :], deps=(slot_used[0],))
                for fb in range(16):
                    s = fb % 2
                    if fb + 1 < 16:
                        s1 = (fb + 1) % 2
                        t_l[s1] = dma(SY, tls[s1],
                                      wbuf[:, s1, 0:2048].rearrange("p (t d) -> p t d", t=2),
                                      w1r[:, (fb + 1) * 2:(fb + 2) * 2, :],
                                      deps=(slot_used[s1],))
                    for g4 in range(4):
                        ftl, a0 = g4 // 2, (g4 % 2) * 4
                        srcs = [wbuf[:, s, ftl * 1024 + (a0 + j) * 128:
                                     ftl * 1024 + (a0 + j + 1) * 128] for j in range(4)]
                        dsts = [buf1[:, a0 + j, ftl * 128:(ftl + 1) * 128]
                                for j in range(4)]
                        tt_ = tern_group4(6 + g4 % 2, srcs, dsts, 0, (t_l[s], t_st))
                    slot_used[s] = tt_
                    t_st = dma(SY, tst, w1t_sc[:, :, fb * 256:(fb + 1) * 256],
                               buf1[:], deps=(tt_,))
                    yield
                w2r = w2_1.rearrange("(dc p) f -> p dc f", p=128)

                def p2b_ap(u):
                    ftb, dc = u // 8, u % 8
                    return w2r[:, dc, ftb * 512:(ftb + 1) * 512]

                t_l[0] = dma(SY, tls[0], wbuf[:, 0, 0:512], p2b_ap(0),
                             deps=(slot_used[0],))
                for u in range(64):
                    ftb, dc = u // 8, u % 8
                    dh, dc2 = dc // 4, dc % 4
                    s = u % 2
                    if u + 1 < 64:
                        s1 = (u + 1) % 2
                        t_l[s1] = dma(SY, tls[s1], wbuf[:, s1, 0:512], p2b_ap(u + 1),
                                      deps=(slot_used[s1],))
                    srcs = [wbuf[:, s, fl * 128:(fl + 1) * 128] for fl in range(4)]
                    dsts = [buf2[:, fl, dc2 * 128:(dc2 + 1) * 128] for fl in range(4)]
                    tt_ = tern_group4(6 + u % 2, srcs, dsts, 2, (t_l[s], t_st))
                    slot_used[s] = tt_
                    if dc2 == 3:
                        t_st = dma(SY, tst,
                                   w2t_sc[:, ftb * 4:(ftb + 1) * 4,
                                          dh * 512:(dh + 1) * 512],
                                   buf2[:], deps=(tt_,))
                    if u % 2 == 1:
                        yield
                tk["tern"] = t_st
                st.close()

            def l1_attn():
                st_l1 = ExitStack()
                qrotT = sbl(st_l1, "qrotT", (128, 8, T), BF16)
                krotT = sbl(st_l1, "krotT", (128, 8, T + VKV), BF16)
                v_ext = sbl(st_l1, "v_ext", (128, NT + 1, 32, 65), BF16)
                with st_l1:
                    with ExitStack() as st:
                        sincos = sbl(st, "sincos", (128, 2, T), BF16)
                        Wb = sbl(st, "Wqk", (128, DT, D), BF16)
                        qb = sbl(st, "qb", (128, 512), BF16)
                        rt1 = sbl(st, "rt1", (128, 512), BF16)
                        rt2 = sbl(st, "rt2", (128, 512), BF16)
                        tk["sc"] = dma(GE, gd, sincos[:, 0, :], sin_in)
                        tk["sc"] = dma(GE, gd, sincos[:, 1, :], cos_in)
                        t_W = None
                        t_Wuse = None
                        t_qb_used = None
                        for which in range(2):
                            t_W = dma(GE, gw, Wb[:], (wqf1 if which == 0 else wkf1)
                                      .rearrange("(a p) m -> p a m", p=128), deps=(t_Wuse,))
                            dstT = qrotT if which == 0 else krotT
                            for a2 in range(8):
                                for cch in range(NCH):
                                    bk = (a2 * NCH + cch) % 2
                                    use_bank(bk)
                                    TE.wait(t_W)
                                    for a in range(DT):
                                        t_pq = mm.mark(TE.raw.matmul(
                                            PS[bk], lhsT=Wb[:, a, a2 * 128:(a2 + 1) * 128],
                                            rhs=xnT[:, a, cch * 512:(cch + 1) * 512],
                                            start=(a == 0), stop=(a == DT - 1)))
                                    VE.wait(t_pq, t_qb_used)
                                    VE.raw.drain()
                                    if which == 0:
                                        t_evq = vv.mark(VE.raw.tensor_scalar(
                                            out=qb[:], in0=PS[bk],
                                            scalar1=bq_c[:, a2:a2 + 1], scalar2=L1QSC,
                                            op0=OP.add, op1=OP.mult))
                                    else:
                                        t_evq = vv.mark(VE.raw.tensor_scalar(
                                            out=qb[:], in0=PS[bk],
                                            scalar1=bk_c[:, a2:a2 + 1], scalar2=None,
                                            op0=OP.add))
                                    bank_free[bk] = t_evq
                                    bk2 = 2 + (a2 * NCH + cch) % 2
                                    use_bank(bk2)
                                    TE.wait(t_evq, tk["consts"])
                                    t_fl = mm.mark(TE.raw.matmul(
                                        PS[bk2], lhsT=consts["pflip"][:], rhs=qb[:],
                                        start=True, stop=True))
                                    VE.wait(t_fl, tk["sc"])
                                    VE.raw.drain()
                                    VE.raw.tensor_tensor(
                                        out=rt1[:], in0=qb[:],
                                        in1=sincos[:, 1, cch * 512:(cch + 1) * 512], op=OP.mult)
                                    VE.raw.tensor_tensor(
                                        out=rt2[:], in0=PS[bk2],
                                        in1=sincos[:, 0, cch * 512:(cch + 1) * 512], op=OP.mult)
                                    VE.raw.drain()
                                    t_rot = vv.mark(VE.raw.tensor_tensor(
                                        out=dstT[:, a2, cch * 512:(cch + 1) * 512],
                                        in0=rt1[:], in1=rt2[:], op=OP.add))
                                    bank_free[bk2] = t_rot
                                    t_qb_used = t_rot
                            t_Wuse = mm.tok()
                        tk["rotdone"] = vv.tok()
                        tk["vk"] = dma(GE, gd, krotT[:, :, T:T + VKV],
                                       vk1.rearrange("(a p) m -> p a m", p=128),
                                       allow_slow_non_contiguous=True)
                        barrier()

                    with ExitStack() as st:
                        WVb = sbl(st, "WVb", (128, DT, 1024), BF16)
                        bv_r = sbl(st, "bv_r", (1, 2048), BF16)
                        t_bv = dma(GE, gd, bv_r[:], bv1)
                        t_vz = vv.mark(VE.raw.memset(v_ext[:], 0.0))
                        VE.raw.drain()
                        t_vo = vv.mark(VE.raw.memset(v_ext[:, :, :, 64:65], 1.0))
                        tk["vv"] = dma(GE, gd, v_ext[0:VKV, NT, :, 0:64],
                                       vv1_in.rearrange("v (h r) -> v h r", h=32),
                                       deps=(t_vz, t_vo))
                        t_WVuse = None
                        for half in range(2):
                            t_WV = dma(GE, gw, WVb[:],
                                       wvf1.rearrange("(a p) m -> p a m", p=128)
                                       [:, :, half * 1024:(half + 1) * 1024],
                                       deps=(t_WVuse,))
                            for kvt in range(NT):
                                for hc2 in range(2):
                                    hc = half * 2 + hc2
                                    bk = 4 + (kvt * 2 + hc2) % 4
                                    use_bank(bk)
                                    TE.wait(t_WV, t_bv)
                                    for a in range(DT):
                                        TE.raw.matmul(
                                            PS[bk], lhsT=xnT[:, a, kvt * 128:(kvt + 1) * 128],
                                            rhs=WVb[:, a, hc2 * 512:(hc2 + 1) * 512],
                                            start=(a == 0), stop=False)
                                    t_pv = mm.mark(TE.raw.matmul(
                                        PS[bk], lhsT=ones_row_bf[:],
                                        rhs=bv_r[0:1, hc * 512:(hc + 1) * 512],
                                        start=False, stop=True))
                                    VE.wait(t_pv, t_vo)
                                    bank_free[bk] = vv.mark(VE.raw.tensor_copy(
                                        v_ext[:, kvt, hc * 8:(hc + 1) * 8, 0:64],
                                        PS[bk].rearrange("p (h r) -> p h r", h=8)))
                            t_WVuse = mm.tok()
                        barrier()

                    if "qrot" in dbg_aps:
                        final_outs.append(dma(SY, sd, dbg_aps["qrot"], qrotT[:]))
                    if "krot" in dbg_aps:
                        final_outs.append(dma(SY, sd, dbg_aps["krot"], krotT[:]))
                    if "vext" in dbg_aps:
                        final_outs.append(dma(SY, sd, dbg_aps["vext"],
                                              v_ext[:].rearrange("p a b c -> p (a b c)")))

                    tk["xnspill"] = dma(SY, sd, xnsp, xnT[:])
                    barrier()
                    st_xn.close()

                    with ExitStack() as st:
                        zT = sbl(st, "zT", (128, 16, 512), BF16)
                        expT = sbl(st, "expT", (128, 8, 512), BF16)
                        denM = sbl(st, "denM", (32, 512), F32)
                        recMb = sbl(st, "recMb", (32, 512), BF16)
                        den_stg = sbl(st, "den_stg", (1, 2, 512), F32)
                        sel_sb = sbl(st, "sel_sb", (32, 2048), BF16)
                        WOb = sbl(st, "WOb", (128, 16, 512), BF16)
                        stg = sbl(st, "stg", (128, 2, 512), F32)
                        shcol = sbl(st, "shcol", (128, 1), F32)
                        s_masks = sbl(st, "s_masks", (128, 5, 512), BF16)
                        t_sh = vv.mark(VE.raw.memset(shcol[:], EXP_SHIFT))
                        SE.wait(t_sh)
                        t_mskld = dma(SY, sd, s_masks[:], masks_in.rearrange("m p q -> p m q"))
                        t_selld = dma(SY, sd, sel_sb[:], sel_in)
                        wof_r = wof1.rearrange("(vc p) d -> p vc d", p=128)
                        t_wob_used = None
                        t_stg_used = [None, None]
                        t_zT_used = None
                        t_dstg_used = [None, None]
                        t_exp_used = [[mm.tok(), mm.tok()] for _ in range(4)]
                        for cch in range(NCH):
                            full = [kvt for kvt in range(NT) if kvt * 128 <= cch * 512 + 527]
                            has_virt = T <= cch * 512 + 527
                            nsteps = len(full) + (1 if has_virt else 0)
                            t_zT = []
                            t_den_dmas = []
                            t_wob = dma(GE, gw, WOb[:], wof_r[:, :, 0:512],
                                        deps=(t_wob_used,))
                            for a2 in range(8):
                                t_e = [[None, None] for _ in range(4)]
                                t_z = [None] * 4
                                for hh in range(4):
                                    use_bank(4 + hh)
                                for p in range(nsteps + 1):
                                    if p < nsteps:
                                        virt = p >= len(full)
                                        kvt = full[p] if not virt else NT
                                        rows = 128 if not virt else VKV
                                        delta = (kvt * 128 if not virt else T) - cch * 512
                                        sl = p % 2
                                        for hh in range(4):
                                            hb = 32 * hh
                                            use_bank(hh)
                                            TE.wait(tk["rotdone"], tk["vk"])
                                            t_s = mm.mark(TE.raw.matmul(
                                                PS[hh][0:rows, :],
                                                lhsT=krotT[hb:hb + 32, a2,
                                                           kvt * 128:kvt * 128 + rows],
                                                rhs=qrotT[hb:hb + 32, a2,
                                                          cch * 512:(cch + 1) * 512],
                                                start=True, stop=True,
                                                tile_position=(hb, 0)))
                                            SE.wait(t_s, t_exp_used[hh][sl])
                                            t_ex = sc.mark(SE.raw.activation(
                                                expT[0:rows, 2 * hh + sl, :],
                                                PS[hh][0:rows, :],
                                                ACT.Exp, bias=shcol[0:rows, :]))
                                            bank_free[hh] = t_ex
                                            if 0 <= delta <= 512:
                                                VE.wait(t_ex, t_mskld)
                                                t_ex = vv.mark(VE.raw.tensor_tensor(
                                                    out=expT[0:rows, 2 * hh + sl, :],
                                                    in0=expT[0:rows, 2 * hh + sl, :],
                                                    in1=s_masks[0:rows, delta // 128, :],
                                                    op=OP.mult))
                                            t_e[hh][sl] = t_ex
                                    if p >= 1:
                                        pz = p - 1
                                        virt = pz >= len(full)
                                        kvt = full[pz] if not virt else NT
                                        rows = 128 if not virt else VKV
                                        sl = pz % 2
                                        for hh in range(4):
                                            h = a2 * 4 + hh
                                            TE.wait(t_e[hh][sl], tk["vv"], t_zT_used)
                                            t_z[hh] = mm.mark(TE.raw.matmul(
                                                PS[4 + hh][0:65, :],
                                                lhsT=v_ext[0:rows, kvt, h, :],
                                                rhs=expT[0:rows, 2 * hh + sl, :],
                                                start=(pz == 0), stop=(pz == nsteps - 1)))
                                            t_exp_used[hh][sl] = t_z[hh]
                                for hh in range(4):
                                    h = a2 * 4 + hh
                                    zoff, vc = 64 * (h % 2), h // 2
                                    sl = hh % 2
                                    VE.wait(t_z[hh], t_dstg_used[sl])
                                    VE.raw.drain()
                                    VE.raw.tensor_copy(den_stg[0:1, sl, :],
                                                       PS[4 + hh][64:65, :])
                                    VE.raw.drain()
                                    t_zc = vv.mark(VE.raw.tensor_copy(
                                        zT[zoff:zoff + 64, vc, :], PS[4 + hh][0:64, :]))
                                    bank_free[4 + hh] = t_zc
                                    t_dd = dma(SY, sd, denM[h:h + 1, :],
                                               den_stg[0:1, sl, :], deps=(t_zc,))
                                    t_dstg_used[sl] = t_dd
                                    t_den_dmas.append(t_dd)
                            VE.wait(*t_den_dmas)
                            VE.raw.drain()
                            VE.raw.reciprocal(denM[:], denM[:])
                            VE.raw.drain()
                            t_rcb = vv.mark(VE.raw.tensor_copy(recMb[:], denM[:]))
                            for h in range(32):
                                zoff, vc = 64 * (h % 2), h // 2
                                bk = h % 8
                                use_bank(bk)
                                TE.wait(t_rcb, t_selld)
                                t_rb = mm.mark(TE.raw.matmul(
                                    PS[bk][0:64, :],
                                    lhsT=sel_sb[:, h * 64:(h + 1) * 64],
                                    rhs=recMb[:], start=True, stop=True))
                                VE.wait(t_rb)
                                t_zn = vv.mark(VE.raw.tensor_tensor(
                                    out=zT[zoff:zoff + 64, vc, :],
                                    in0=zT[zoff:zoff + 64, vc, :],
                                    in1=PS[bk][0:64, :], op=OP.mult))
                                bank_free[bk] = t_zn
                                t_zT.append(t_zn)
                            for dch in range(2):
                                if dch == 1:
                                    t_wob = dma(GE, gw, WOb[:], wof_r[:, :, 512:1024],
                                                deps=(t_wob_used,))
                                for qt in range(4):
                                    bk = (qt + dch * 2) % 4
                                    use_bank(bk)
                                    TE.wait(t_zT, t_wob)
                                    for vc in range(16):
                                        t_wo = mm.mark(TE.raw.matmul(
                                            PS[bk], lhsT=zT[:, vc, qt * 128:(qt + 1) * 128],
                                            rhs=WOb[:, vc, :],
                                            start=(vc == 0), stop=(vc == 15)))
                                    sp = qt % 2
                                    SE.wait(t_wo, t_stg_used[sp])
                                    t_sg = sc.mark(SE.raw.copy(stg[:, sp, :], PS[bk]))
                                    bank_free[bk] = t_sg
                                    r0 = cch * 512 + qt * 128
                                    t_acd = dma(GE, accs[sp],
                                                o_out[r0:r0 + 128, dch * 512:(dch + 1) * 512],
                                                stg[:, sp, :], deps=(t_sg, tk["spill"]),
                                                accum_op=OP.add)
                                    t_stg_used[sp] = t_acd
                                    final_outs.append(t_acd)
                                t_wob_used = mm.tok()
                            t_zT_used = sc.tok()
                    barrier()

            def l1_ffn():
                with ExitStack() as st:
                    xnb = sbl(st, "xnb", (128, DT, T), BF16)
                    b2r1 = sbl(st, "b2r1", (1, D), BF16)
                    t_b2f = dma(GE, gd, b2r1[:], b2_1)
                    w1tb = sbl(st, "w1tb", (128, DT, 4096), BF16)
                    w2tb = sbl(st, "w2tb", (128, 2, 32, 512), BF16)
                    hbuf = sbl(st, "hbuf", (128, 32, 512), BF16)
                    stg = sbl(st, "stg", (128, 2, 512), F32)
                    t_xn = dma(SY, sd, xnb[:], xnsp, deps=(tk["xnspill"],))
                    t_w1t = dma(GE, ffws[0], w1tb[:], w1t_sc, deps=(tk["tern"],))
                    t_w2use = [None, None]
                    t_stg_used = [None, None]
                    t_huse = mm.tok()
                    for tq in range(max(1, NCH)):
                        for dch in range(2):
                            ws = dch
                            t_w2t = dma(GE, ffws[1 + dch], w2tb[:, ws, :, :],
                                        w2t_sc[:, :, dch * 512:(dch + 1) * 512],
                                        deps=(t_w2use[ws], tk["tern"]))
                            if dch == 0:
                                for fc in range(32):
                                    bk = fc % 2
                                    use_bank(bk)
                                    TE.wait(t_w1t, t_xn)
                                    for a in range(DT):
                                        t_p1 = mm.mark(TE.raw.matmul(
                                            PS[bk], lhsT=w1tb[:, a, fc * 128:(fc + 1) * 128],
                                            rhs=xnb[:, a, tq * 512:(tq + 1) * 512],
                                            start=(a == 0), stop=(a == DT - 1)))
                                    SE.wait(t_p1, t_huse)
                                    bank_free[bk] = sc.mark(SE.raw.activation(
                                        hbuf[:, fc, :], PS[bk], ACT.Gelu,
                                        bias=b1c1[:, fc:fc + 1]))
                                t_h = sc.tok()
                            for qt in range(4):
                                bk = 2 + (qt % 4)
                                use_bank(bk)
                                TE.wait(t_h, t_w2t, t_b2f)
                                for fc in range(32):
                                    TE.raw.matmul(
                                        PS[bk], lhsT=hbuf[:, fc, qt * 128:(qt + 1) * 128],
                                        rhs=w2tb[:, ws, fc, :],
                                        start=(fc == 0), stop=False)
                                t_p2 = mm.mark(TE.raw.matmul(
                                    PS[bk], lhsT=ones_row_bf[:],
                                    rhs=b2r1[0:1, dch * 512:(dch + 1) * 512],
                                    start=False, stop=True))
                                sp = qt % 2
                                SE.wait(t_p2, t_stg_used[sp])
                                t_sg = sc.mark(SE.raw.copy(stg[:, sp, :], PS[bk]))
                                bank_free[bk] = t_sg
                                r0 = tq * 512 + qt * 128
                                t_acd = dma(GE, accs[sp],
                                            o_out[r0:r0 + 128, dch * 512:(dch + 1) * 512],
                                            stg[:, sp, :], deps=(t_sg, tk["spill"]),
                                            accum_op=OP.add)
                                t_stg_used[sp] = t_acd
                                final_outs.append(t_acd)
                            t_w2use[ws] = mm.tok()
                        t_huse = mm.tok()
                    barrier()

            st_S = ExitStack()
            S_full = st_S.enter_context(
                nc.sbuf_tensor("S_full", [128, T], F32, side="right"))
            layernorm_phase(True, S_full)
            if "xnT" in dbg_aps and upto < 3:
                final_outs.append(dma(SY, sd, dbg_aps["xnT"], xnT[:]))
            if upto >= 2:
                l0_attn(S_full, st_S)
            else:
                st_S.close()
            if "acc1" in dbg_aps:
                final_outs.append(dma(SY, sd, dbg_aps["acc1"], acc[:],
                                      deps=(vv.tok(), mm.tok(), sc.tok())))
            if upto >= 3:
                l0_ffn(tern_steps() if upto >= 4 else None)
            if "acc2" in dbg_aps:
                final_outs.append(dma(SY, sd, dbg_aps["acc2"], acc[:],
                                      deps=(vv.tok(), mm.tok(), sc.tok())))
            if upto >= 5:
                layernorm_phase(False, None)
                if "xnT" in dbg_aps:
                    final_outs.append(dma(SY, sd, dbg_aps["xnT"], xnT[:]))
                tk["spill"] = dma(SY, sd, o_out.rearrange("(i p) d -> p i d", p=128),
                                  acc[:], deps=(vv.tok(), mm.tok()))
                final_outs.append(tk["spill"])
                barrier()
                st_acc.close()
            if upto >= 6:
                l1_attn()
            if upto >= 7:
                l1_ffn()
            if upto < 5:
                tk["spill"] = dma(SY, sd, o_out.rearrange("(i p) d -> p i d", p=128),
                                  acc[:], deps=(vv.tok(), mm.tok(), sc.tok()))
                final_outs.append(tk["spill"])
            SY.wait(*[s.tok() for s in ALLSEMS])
            SY.wait(*final_outs)

    return nc


def host_inputs(inputs, T=2048):
    f32 = np.float32
    c = host_constants(T)
    WQ = np.ascontiguousarray(
        inputs["L1_attn_W_Q"].reshape(8, 4, 1024, 32).transpose(2, 0, 1, 3).reshape(1024, 1024))
    WK = np.ascontiguousarray(
        inputs["L1_attn_W_K"].reshape(8, 4, 1024, 32).transpose(2, 0, 1, 3).reshape(1024, 1024))
    WV = np.ascontiguousarray(inputs["L1_attn_W_V"][:, :, 0].T)
    WO = np.ascontiguousarray(inputs["L1_attn_W_O"][:, 0, :])
    bQ = np.ascontiguousarray(inputs["L1_attn_b_Q"].reshape(8, 4, 32).reshape(8, 128))
    bK = np.ascontiguousarray(inputs["L1_attn_b_K"].reshape(8, 4, 32).reshape(8, 128))
    bV = np.ascontiguousarray(inputs["L1_attn_b_V"][:, 0][None, :])
    vk = np.ascontiguousarray(
        inputs["L1_attn_virtual_k"].reshape(16, 8, 4, 32).transpose(1, 2, 3, 0).reshape(1024, 16))
    vvv = np.ascontiguousarray(inputs["L1_attn_virtual_v"][:, :, 0])
    base = {
        "L0_gamma": np.ascontiguousarray(inputs["L0_attn_gamma"].reshape(8, 128), dtype=f32),
        "L0_w_q": np.ascontiguousarray(inputs["L0_attn_w_q"], dtype=f32),
        "L0_w_kv": np.ascontiguousarray(inputs["L0_attn_w_kv"], dtype=f32),
        "L0_w_out": np.ascontiguousarray(inputs["L0_attn_w_out"], dtype=f32),
        "L0_WrT": np.ascontiguousarray(inputs["L0_ffn_Wr"].T, dtype=f32),
        "L0_br": np.ascontiguousarray(inputs["L0_ffn_br"][None, :], dtype=f32),
        "L0_W1": np.ascontiguousarray(inputs["L0_ffn_W1"], dtype=f32),
        "L0_b1": np.ascontiguousarray(inputs["L0_ffn_b1"], dtype=f32),
        "L0_W2": np.ascontiguousarray(inputs["L0_ffn_W2"], dtype=f32),
        "L0_b2": np.ascontiguousarray(inputs["L0_ffn_b2"], dtype=f32),
        "L1_WQ": WQ.astype(f32), "L1_WK": WK.astype(f32),
        "L1_WV": WV.astype(f32), "L1_WO": WO.astype(f32),
        "L1_bQ": bQ.astype(f32), "L1_bK": bK.astype(f32), "L1_bV": bV.astype(f32),
        "L1_vk": vk.astype(f32), "L1_vv": vvv.astype(f32),
        "L1_W1": np.ascontiguousarray(inputs["L1_ffn_W1"], dtype=f32),
        "L1_b1": np.ascontiguousarray(inputs["L1_ffn_b1"].reshape(32, 128), dtype=f32),
        "L1_W2": np.ascontiguousarray(inputs["L1_ffn_W2"], dtype=f32),
        "L1_b2": np.ascontiguousarray(inputs["L1_ffn_b2"][None, :], dtype=f32),
        "c_sin": c["sin"][:, :T], "c_cos": c["cos"][:, :T],
        "c_masks": np.ascontiguousarray(c["masks"]),
    }
    for nm in CSHAPES:
        base["c_" + nm] = np.ascontiguousarray(c[nm])
    base["c_sel32"] = np.ascontiguousarray(c["sel32"])
    x = np.asarray(inputs["L0_attn_x"], dtype=f32)
    in_maps = []
    for i in range(8):
        m = dict(base)
        m["x"] = np.ascontiguousarray(x[i, :T, :])
        in_maps.append(m)
    return in_maps


def kernel(**inputs):
    T = 2048
    nc = build(T=T)
    in_maps = host_inputs(inputs, T=T)
    res = run_bass_kernel_spmd(nc, in_maps, core_ids=list(range(8)))
    out = np.stack([np.asarray(r["o"], dtype=np.float32) for r in res.results], axis=0)
    return out


# version 12
# speedup vs baseline: 1.0668x; 1.0158x over previous
import numpy as np
import ml_dtypes
from contextlib import ExitStack

import concourse.bass as bass
import concourse.mybir as mybir
from concourse.bass_utils import run_bass_kernel_spmd

F32 = mybir.dt.float32
BF16 = mybir.dt.bfloat16
AX = mybir.AxisListType.X
OP = mybir.AluOpType
ACT = mybir.ActivationFunctionType

D = 1024
DT = 8
VKV = 16
ROT_BASE = 10000.0
EXP_SHIFT = -20.0
QSC = 0.25
TAYSC = 2.0 ** -0.25
L1QSC = 32.0 ** -0.5
bfdt = ml_dtypes.bfloat16


def host_constants(T):
    c = {}
    c["ident32"] = np.eye(128, dtype=np.float32)
    c["ident16"] = np.eye(128, dtype=np.float32).astype(bfdt)
    c["ones_row_bf"] = np.ones((1, 128), dtype=np.float32).astype(bfdt)
    c["ones_row_f32"] = np.ones((1, 128), dtype=np.float32)
    c["ones_col_f32"] = np.ones((128, 1), dtype=np.float32)
    pos = np.arange(T, dtype=np.float64)
    freq = ROT_BASE ** (np.arange(16, dtype=np.float64) / 16.0)
    ang = pos[None, :] / freq[(np.arange(128) % 16)][:, None]
    c["sin"] = np.sin(ang).astype(np.float32)
    c["cos"] = np.cos(ang).astype(np.float32)
    pf = np.zeros((128, 128), dtype=np.float32)
    for m in range(128):
        if m % 32 < 16:
            pf[m + 16, m] = -1.0
        else:
            pf[m - 16, m] = 1.0
    c["pflip"] = pf.astype(bfdt)
    kvl = np.arange(128)[:, None]
    ql = np.arange(512)[None, :]
    c["masks"] = np.stack(
        [(kvl - ql <= 16 - 128 * dd).astype(np.float32) for dd in range(5)]
    ).astype(bfdt)
    sel8 = np.zeros((8, 136), dtype=np.float32)
    for h in range(8):
        sel8[h, h * 17:(h + 1) * 17] = 1.0
    c["sel8"] = sel8.astype(bfdt)
    sel = np.zeros((32, 2048), dtype=np.float32)
    for h in range(32):
        sel[h, h * 64:(h + 1) * 64] = 1.0
    c["sel32"] = sel.astype(bfdt)
    c["iota8"] = np.broadcast_to(np.arange(8, dtype=np.float32), (128, 8)).copy()
    c["iota8big"] = (c["iota8"] + 1e6).astype(np.float32)
    return c


CSHAPES = dict(ident32=(128, 128, F32), ident16=(128, 128, BF16),
               sel8=(8, 136, BF16),
               ones_row_bf=(1, 128, BF16), ones_row_f32=(1, 128, F32),
               ones_col_f32=(128, 1, F32), pflip=(128, 128, BF16),
               iota8=(128, 8, F32), iota8big=(128, 8, F32))


class Sem:
    def __init__(self, es, nc, name, dma=False):
        self.h = es.enter_context(nc.semaphore(name))
        self.v = 0
        self.name = name
        self.dma = dma

    def mark(self, instr, n=1):
        instr.then_inc(self.h, n)
        self.v += n
        return (self, self.v)

    def tok(self):
        return (self, self.v)


class Eng:
    def __init__(self, raw):
        self.raw = raw
        self.seen = {}

    def wait(self, *deps):
        for d in deps:
            if d is None:
                continue
            if isinstance(d, (list, tuple)) and d and isinstance(d[0], (list, tuple, type(None), Sem)) \
                    and not (len(d) == 2 and isinstance(d[0], Sem)):
                self.wait(*d)
                continue
            sem, val = d
            if sem.dma:
                val = sem.v
            if val > self.seen.get(sem, 0):
                self.raw.wait_ge(sem.h, val)
                self.seen[sem] = val


def build(T=2048, upto=99, debug=(), l0sub=9):
    NT = T // 128
    NCH = T // 512
    nc = bass.Bass("TRN2", target_bir_lowering=False)
    dt = nc.dram_tensor

    x_in = dt("x", [T, D], F32, kind="ExternalInput").ap()
    g0 = dt("L0_gamma", [8, 128], F32, kind="ExternalInput").ap()
    wq0 = dt("L0_w_q", [D, 128], F32, kind="ExternalInput").ap()
    wkv0 = dt("L0_w_kv", [D, 256], F32, kind="ExternalInput").ap()
    wo0 = dt("L0_w_out", [128, D], F32, kind="ExternalInput").ap()
    wr0T = dt("L0_WrT", [8, D], F32, kind="ExternalInput").ap()
    br0 = dt("L0_br", [1, 8], F32, kind="ExternalInput").ap()
    w1_0 = dt("L0_W1", [8, D, 4096], F32, kind="ExternalInput").ap()
    b1_0 = dt("L0_b1", [8, 4096], F32, kind="ExternalInput").ap()
    w2_0 = dt("L0_W2", [8, 4096, D], F32, kind="ExternalInput").ap()
    b2_0 = dt("L0_b2", [8, D], F32, kind="ExternalInput").ap()
    wqf1 = dt("L1_WQ", [D, D], F32, kind="ExternalInput").ap()
    wkf1 = dt("L1_WK", [D, D], F32, kind="ExternalInput").ap()
    wvf1 = dt("L1_WV", [D, 2048], F32, kind="ExternalInput").ap()
    wof1 = dt("L1_WO", [2048, D], F32, kind="ExternalInput").ap()
    bq1 = dt("L1_bQ", [8, 128], F32, kind="ExternalInput").ap()
    bk1 = dt("L1_bK", [8, 128], F32, kind="ExternalInput").ap()
    bv1 = dt("L1_bV", [1, 2048], F32, kind="ExternalInput").ap()
    vk1 = dt("L1_vk", [D, VKV], F32, kind="ExternalInput").ap()
    vv1_in = dt("L1_vv", [VKV, 2048], F32, kind="ExternalInput").ap()
    w1_1 = dt("L1_W1", [4096, D], F32, kind="ExternalInput").ap()
    b1_1 = dt("L1_b1", [32, 128], F32, kind="ExternalInput").ap()
    w2_1 = dt("L1_W2", [D, 4096], F32, kind="ExternalInput").ap()
    b2_1 = dt("L1_b2", [1, D], F32, kind="ExternalInput").ap()
    ci = {nm: dt("c_" + nm, [p_, f_], ty, kind="ExternalInput").ap()
          for nm, (p_, f_, ty) in CSHAPES.items()}
    sel_in = dt("c_sel32", [32, 2048], BF16, kind="ExternalInput").ap()
    sin_in = dt("c_sin", [128, T], F32, kind="ExternalInput").ap()
    cos_in = dt("c_cos", [128, T], F32, kind="ExternalInput").ap()
    masks_in = dt("c_masks", [5, 128, 512], BF16, kind="ExternalInput").ap()

    o_out = dt("o", [T, D], F32, kind="ExternalOutput").ap()
    xnsp = dt("xnspill", [128, DT, T], BF16, kind="Internal").ap()
    w1t_sc = dt("w1t_sc", [128, DT, 4096], BF16, kind="Internal").ap()
    w2t_sc = dt("w2t_sc", [128, 32, D], BF16, kind="Internal").ap()

    dbg_specs = {"xnT": ([128, DT, T], BF16), "srow": ([1, T], F32),
                 "acc1": ([128, NT, D], F32), "acc2": ([128, NT, D], F32),
                 "G": ([128, NT, 8], F32),
                 "qrot": ([128, 8, T], BF16), "krot": ([128, 8, T + VKV], BF16),
                 "vext": ([128, (NT + 1) * 32 * 65], BF16)}
    dbg_aps = {nm: dt("dbg_" + nm, s, ty, kind="ExternalOutput").ap()
               for nm, (s, ty) in dbg_specs.items() if nm in debug}

    es = ExitStack()
    st_acc = ExitStack()
    st_xn = ExitStack()

    _uniq = [0]

    def sbl(st, name, shape, ty):
        _uniq[0] += 1
        return st.enter_context(nc.sbuf_tensor(f"{name}_{_uniq[0]}", list(shape), ty))

    xnT = st_xn.enter_context(nc.sbuf_tensor("xnT", [128, DT, T], BF16, side="right"))
    acc = st_acc.enter_context(nc.sbuf_tensor("acc", [128, NT, D], F32, side="right"))

    consts = {nm: es.enter_context(nc.sbuf_tensor("s_" + nm, [p_, f_], ty))
              for nm, (p_, f_, ty) in CSHAPES.items()}
    gamma_c = es.enter_context(nc.sbuf_tensor("gamma_c", [128, DT], F32))
    wq_b = es.enter_context(nc.sbuf_tensor("wq_b", [128, DT, 128], BF16))
    wkv_b = es.enter_context(nc.sbuf_tensor("wkv_b", [128, DT, 256], BF16))
    wr_b = es.enter_context(nc.sbuf_tensor("wr_b", [128, DT, 8], BF16))
    br_b = es.enter_context(nc.sbuf_tensor("br_b", [1, 8], BF16))
    b1c0 = es.enter_context(nc.sbuf_tensor("b1c0", [128, 8, 32], F32))
    bq_c = es.enter_context(nc.sbuf_tensor("bq_c", [128, 8], F32))
    bk_c = es.enter_context(nc.sbuf_tensor("bk_c", [128, 8], F32))
    b1c1 = es.enter_context(nc.sbuf_tensor("b1c1", [128, 32], F32))

    pA = es.enter_context(nc.psum_tensor("pA", [128, 1024], F32))
    pB = es.enter_context(nc.psum_tensor("pB", [128, 1024], F32))
    pC = es.enter_context(nc.psum_tensor("pC", [128, 1024], F32))
    pD = es.enter_context(nc.psum_tensor("pD", [128, 1024], F32))
    PS = [pA[:, 0:512], pA[:, 512:1024], pB[:, 0:512], pB[:, 512:1024],
          pC[:, 0:512], pC[:, 512:1024], pD[:, 0:512], pD[:, 512:1024]]
    bank_free = [None] * 8

    sd = Sem(es, nc, "sd", dma=True)
    gd = Sem(es, nc, "gd", dma=True)
    gw = Sem(es, nc, "gw", dma=True)
    moes = [Sem(es, nc, "moe0", dma=True), Sem(es, nc, "moe1", dma=True)]
    tls = [Sem(es, nc, "tl0", dma=True), Sem(es, nc, "tl1", dma=True)]
    accs = [Sem(es, nc, "acc0", dma=True), Sem(es, nc, "acc1", dma=True)]
    ffws = [Sem(es, nc, "ffw1", dma=True), Sem(es, nc, "ffw2", dma=True),
            Sem(es, nc, "ffw3", dma=True)]
    b2ss = [Sem(es, nc, "b20", dma=True), Sem(es, nc, "b21", dma=True)]
    tst = Sem(es, nc, "tst", dma=True)
    mm = Sem(es, nc, "mm")
    vv = Sem(es, nc, "vv")
    sc = Sem(es, nc, "sc")
    ALLSEMS = (sd, gd, gw, moes[0], moes[1], tls[0], tls[1], b2ss[0], b2ss[1],
               accs[0], accs[1], ffws[0], ffws[1], ffws[2], tst, mm, vv, sc)

    ident32 = consts["ident32"]
    ident16 = consts["ident16"]
    ones_row_bf = consts["ones_row_bf"]
    ones_row_f32 = consts["ones_row_f32"]
    tk = {}
    final_outs = []

    with es, nc.Block() as block:
        @block.sync
        def _(sync_unused):
            TE = Eng(nc.tensor)
            VE = Eng(nc.vector)
            SE = Eng(nc.scalar)
            GE = Eng(nc.gpsimd)
            SY = Eng(nc.sync)

            def dma(q, sem, out, in_, deps=(), **kw):
                q.wait(*deps)
                return sem.mark(q.raw.dma_start(out=out, in_=in_, **kw), 16)

            def use_bank(bk):
                TE.wait(bank_free[bk])

            def barrier():
                VE.raw.drain()
                toks = [s.tok() for s in ALLSEMS]
                for E in (TE, VE, SE, GE, SY):
                    E.wait(*toks)

            def small_transpose(src_ap, dst_ap, bk, p_, f_, deps=()):
                use_bank(bk)
                TE.wait(tk.get("consts"), *deps)
                tr = mm.mark(TE.raw.matmul(PS[bk][0:f_, 0:p_], lhsT=src_ap,
                                           rhs=ident32[0:p_, 0:p_], start=True, stop=True))
                VE.wait(tr)
                t = vv.mark(VE.raw.tensor_copy(dst_ap, PS[bk][0:f_, 0:p_]))
                bank_free[bk] = t
                return t

            for nm in CSHAPES:
                tk["consts"] = dma(SY, sd, consts[nm][:], ci[nm])
            tk["x"] = dma(SY, sd, acc[:], x_in.rearrange("(i p) d -> p i d", p=128))

            with ExitStack() as st0:
                graw = sbl(st0, "graw", (8, 128), F32)
                wqf = sbl(st0, "wqf", (128, DT, 384), F32)
                wrT = sbl(st0, "wrT", (8, DT, 128), F32)
                b1row = sbl(st0, "b1row", (32, 8, 128), F32)
                bqrow = sbl(st0, "bqrow", (8, 2, 128), F32)
                b1frow = sbl(st0, "b1frow", (32, 128), F32)

                t_g = dma(GE, gd, graw[:], g0)
                t_wq = dma(GE, gd, wqf[:, :, 0:128], wq0.rearrange("(a p) m -> p a m", p=128))
                t_wkv = dma(GE, gd, wqf[:, :, 128:384], wkv0.rearrange("(a p) m -> p a m", p=128))
                t_wr = dma(GE, gd, wrT[:], wr0T.rearrange("e (a p) -> e a p", p=128))
                t_br = dma(GE, gd, br_b[:], br0)
                t_b1r = dma(GE, gd, b1row[:], b1_0.rearrange("e (c p) -> c e p", p=128),
                            allow_slow_non_contiguous=True)
                t_bq = dma(GE, gd, bqrow[:, 0, :], bq1)
                t_bk = dma(GE, gd, bqrow[:, 1, :], bk1)
                t_b1f = dma(GE, gd, b1frow[:], b1_1)

                small_transpose(graw[:], gamma_c[:], 6, 8, 128, deps=(t_g,))
                t_gc = vv.tok()
                VE.wait(t_wq, t_wkv, t_gc)
                VE.raw.drain()
                for a in range(DT):
                    VE.raw.tensor_scalar(out=wq_b[:, a, :], in0=wqf[:, a, 0:128],
                                         scalar1=gamma_c[:, a:a + 1], scalar2=QSC,
                                         op0=OP.mult, op1=OP.mult)
                    tk["wfold"] = vv.mark(VE.raw.tensor_scalar(
                        out=wkv_b[:, a, :], in0=wqf[:, a, 128:384],
                        scalar1=gamma_c[:, a:a + 1], scalar2=None, op0=OP.mult))
                with ExitStack() as st00:
                    wrtmp = sbl(st00, "wrtmp", (128, DT, 8), F32)
                    for a in range(DT):
                        small_transpose(wrT[:, a, :], wrtmp[:, a, :], a % 2, 8, 128,
                                        deps=(t_wr,))
                    VE.raw.drain()
                    tk["wr"] = vv.mark(VE.raw.tensor_copy(wr_b[:], wrtmp[:]))
                for e in range(8):
                    small_transpose(b1row[:, e, :], b1c0[:, e, :], 2 + e % 2, 32, 128,
                                    deps=(t_b1r,))
                small_transpose(bqrow[:, 0, :], bq_c[:], 4, 8, 128, deps=(t_bq,))
                small_transpose(bqrow[:, 1, :], bk_c[:], 5, 8, 128, deps=(t_bk,))
                small_transpose(b1frow[:], b1c1[:], 6, 32, 128, deps=(t_b1f,))
                tk["smallpar"] = vv.tok()
                barrier()

            def layernorm_phase(want_s, S_full):
                with ExitStack() as st:
                    xtmp = sbl(st, "ln_xtmp", (128, 2, D), F32)
                    sq_t = sbl(st, "ln_sq", (128, 2, D), F32)
                    cols = sbl(st, "ln_cols", (128, 8, NT), F32)
                    s_tok = sbl(st, "ln_stok", (128, NT), F32)
                    s_row = sbl(st, "ln_srow", (1, T), F32)
                    t_sq_rd = [None, None]
                    for i in range(NT):
                        VE.raw.tensor_reduce(cols[:, 0, i:i + 1], acc[:, i, :], AX, OP.add)
                        sl = i % 2
                        SE.wait(t_sq_rd[sl])
                        t_sqw = sc.mark(SE.raw.activation(sq_t[:, sl, :], acc[:, i, :],
                                                          ACT.Square))
                        VE.wait(t_sqw)
                        t_sq_rd[sl] = vv.mark(VE.raw.tensor_reduce(
                            cols[:, 1, i:i + 1], sq_t[:, sl, :], AX, OP.add))
                    VE.raw.drain()
                    VE.raw.tensor_scalar(out=cols[:, 2, :], in0=cols[:, 0, :],
                                         scalar1=-1.0 / D, scalar2=None, op0=OP.mult)
                    VE.raw.tensor_scalar(out=cols[:, 3, :], in0=cols[:, 1, :],
                                         scalar1=1.0 / D, scalar2=None, op0=OP.mult)
                    VE.raw.drain()
                    VE.raw.tensor_tensor(out=cols[:, 4, :], in0=cols[:, 2, :],
                                         in1=cols[:, 2, :], op=OP.mult)
                    VE.raw.drain()
                    VE.raw.tensor_tensor(out=cols[:, 4, :], in0=cols[:, 3, :],
                                         in1=cols[:, 4, :], op=OP.subtract)
                    VE.raw.drain()
                    VE.raw.tensor_scalar(out=cols[:, 5, :], in0=cols[:, 4, :],
                                         scalar1=1e-5, scalar2=None, op0=OP.add)
                    VE.raw.drain()
                    t_rp = vv.mark(VE.raw.reciprocal(cols[:, 6, :], cols[:, 5, :]))
                    SE.wait(t_rp)
                    t_rstd = sc.mark(SE.raw.activation(cols[:, 7, :], cols[:, 6, :],
                                                       ACT.Sqrt))
                    if want_s:
                        VE.wait(t_rstd)
                        VE.raw.tensor_tensor(out=cols[:, 5, :], in0=cols[:, 4, :],
                                             in1=cols[:, 6, :], op=OP.mult)
                        VE.raw.drain()
                        VE.raw.tensor_scalar(out=cols[:, 5, :], in0=cols[:, 5, :],
                                             scalar1=9.765625e-28, scalar2=None, op0=OP.max)
                        VE.raw.drain()
                        t_si = vv.mark(VE.raw.reciprocal(cols[:, 5, :], cols[:, 5, :]))
                        SE.wait(t_si)
                        t_sv = sc.mark(SE.raw.activation(s_tok[:], cols[:, 5, :], ACT.Sqrt))
                    t_xuse = [None, None]
                    for i in range(NT):
                        sl = i % 2
                        VE.wait(t_rstd, t_xuse[sl])
                        VE.raw.drain()
                        t_xn = vv.mark(VE.raw.tensor_scalar(
                            out=xtmp[:, sl, :], in0=acc[:, i, :],
                            scalar1=cols[:, 2, i:i + 1],
                            scalar2=cols[:, 7, i:i + 1], op0=OP.add, op1=OP.mult))
                        TE.wait(t_xn, tk["consts"])
                        for a in range(DT):
                            bk = a % 2
                            use_bank(bk)
                            tr = mm.mark(TE.raw.transpose(
                                PS[bk][:, 0:128], xtmp[:, sl, a * 128:(a + 1) * 128],
                                ident32[:]))
                            VE.wait(tr)
                            bank_free[bk] = vv.mark(VE.raw.tensor_copy(
                                xnT[:, a, i * 128:(i + 1) * 128], PS[bk][:, 0:128]))
                        t_xuse[sl] = tr
                    if want_s:
                        t_stok = sc.tok()
                        for i in range(NT):
                            bk = 2 + i % 2
                            use_bank(bk)
                            TE.wait(t_stok, tk["consts"])
                            tr = mm.mark(TE.raw.matmul(
                                PS[bk][0:1, 0:128], lhsT=s_tok[:, i:i + 1],
                                rhs=ident32[:], start=True, stop=True))
                            SE.wait(tr)
                            bank_free[bk] = sc.mark(SE.raw.copy(
                                s_row[0:1, i * 128:(i + 1) * 128], PS[bk][0:1, 0:128]))
                        t_srow = sc.tok()
                        if "srow" in dbg_aps:
                            final_outs.append(dma(SY, sd, dbg_aps["srow"], s_row[:],
                                                  deps=(t_srow,)))
                        for cch in range(NCH):
                            bk = 4 + cch % 2
                            use_bank(bk)
                            TE.wait(t_srow, tk["consts"])
                            tr = mm.mark(TE.raw.matmul(
                                PS[bk], lhsT=ones_row_f32[:],
                                rhs=s_row[0:1, cch * 512:(cch + 1) * 512],
                                start=True, stop=True))
                            VE.wait(tr)
                            bank_free[bk] = vv.mark(VE.raw.tensor_copy(
                                S_full[:, cch * 512:(cch + 1) * 512], PS[bk]))
                    barrier()

            def l0_attn(S_full, st_S):
                st_l0 = ExitStack()
                qT = sbl(st_l0, "qT", (128, T), BF16)
                kT = sbl(st_l0, "kT", (128, T), BF16)
                vT = sbl(st_l0, "vT", (128, T), BF16)
                q_aug = sbl(st_l0, "q_aug", (17, 8, T), BF16)
                q_tok = sbl(st_l0, "q_tok", (128, NT, 128), BF16)
                with st_l0:
                    t_ev = {0: None, 1: None, 2: None}
                    for cch in range(NCH):
                        for wi in range(3):
                            buf = (qT, kT, vT)[wi]
                            bk = (cch * 3 + wi) % 4
                            use_bank(bk)
                            TE.wait(tk["wfold"])
                            for a in range(DT):
                                lhsT = (wq_b[:, a, :] if wi == 0
                                        else wkv_b[:, a, (wi - 1) * 128:wi * 128])
                                t_mm = mm.mark(TE.raw.matmul(
                                    PS[bk], lhsT=lhsT, rhs=xnT[:, a, cch * 512:(cch + 1) * 512],
                                    start=(a == 0), stop=(a == DT - 1)))
                            VE.wait(t_mm)
                            t_c = vv.mark(VE.raw.tensor_tensor(
                                out=buf[:, cch * 512:(cch + 1) * 512], in0=PS[bk],
                                in1=S_full[:, cch * 512:(cch + 1) * 512], op=OP.mult))
                            bank_free[bk] = t_c
                            t_ev[wi] = t_c
                    barrier()
                    st_S.close()
                    if l0sub < 2:
                        return

                    kvA = sbl(st_l0, "kvA", (17, 8, 17), BF16)
                    kvB = sbl(st_l0, "kvB", (128, 8, 17), BF16)
                    kvC = sbl(st_l0, "kvC", (128, 8, 17), BF16)
                    with ExitStack() as st:
                        k_tok = sbl(st, "k_tok", (128, NT, 8, 17), BF16)
                        v_tok = sbl(st, "v_tok", (128, NT, 8, 17), BF16)
                        ksc = sbl(st, "ksc", (128, 8, 16), BF16)
                        kf2 = sbl(st, "kf2", (128, 2, 8, 256), BF16)
                        kvS = sbl(st, "kvS", (17, 8, 273), BF16)

                        t_qaug = [vv.mark(VE.raw.memset(q_aug[0:1, :, :], 1.0))]
                        for h in range(8):
                            t_qaug.append(dma(SY, sd, q_aug[1:17, h, :],
                                              qT[h * 16:(h + 1) * 16, :], deps=(t_ev[0],)))
                        t_kones = vv.mark(VE.raw.memset(k_tok[:, :, :, 0:1], 1.0))
                        VE.raw.drain()
                        t_vones = vv.mark(VE.raw.memset(v_tok[:, :, :, 0:1], 1.0))

                        t_tok = [[None] * 3 for _ in range(NT)]
                        for i in range(NT):
                            for wi, src in enumerate((qT, kT, vT)):
                                bk = (i * 3 + wi) % 4
                                use_bank(bk)
                                TE.wait(t_ev[wi], tk["consts"])
                                tr = mm.mark(TE.raw.matmul(
                                    PS[bk][:, 0:128], lhsT=src[:, i * 128:(i + 1) * 128],
                                    rhs=ident16[:], start=True, stop=True))
                                VE.wait(tr)
                                if wi == 0:
                                    t_tok[i][wi] = vv.mark(VE.raw.tensor_copy(
                                        q_tok[:, i, :], PS[bk][:, 0:128]))
                                else:
                                    dst = (k_tok if wi == 1 else v_tok)[:, i, :, 1:17]
                                    t_tok[i][wi] = vv.mark(VE.raw.tensor_copy(
                                        dst, PS[bk][:, 0:128].rearrange("p (h d) -> p h d", h=8)))
                                bank_free[bk] = t_tok[i][wi]

                        if l0sub < 3:
                            barrier()
                            return
                        kvps = [pA[0:17, 0:273], pA[0:17, 512:785],
                                pB[0:17, 0:273], pB[0:17, 512:785],
                                pC[0:17, 0:273], pC[0:17, 512:785],
                                pD[0:17, 0:273], pD[0:17, 512:785]]
                        t_kf2_used = None
                        for i in range(NT):
                            par = i % 2
                            VE.wait(t_tok[i][1], t_tok[i][2], t_kones, t_vones, t_kf2_used)
                            VE.raw.drain()
                            VE.raw.tensor_scalar(out=ksc[:], in0=k_tok[:, i, :, 1:17],
                                                 scalar1=TAYSC, scalar2=None, op0=OP.mult)
                            VE.raw.drain()
                            for j in range(16):
                                t_kf2 = vv.mark(VE.raw.tensor_tensor(
                                    out=kf2[:, par, :, j * 16:(j + 1) * 16], in0=ksc[:],
                                    in1=ksc[:, :, j:j + 1].broadcast_to([128, 8, 16]),
                                    op=OP.mult))
                            TE.wait(t_kf2)
                            if i == 0:
                                for bk in range(8):
                                    use_bank(bk)
                            for h in range(8):
                                TE.raw.matmul(kvps[h][:, 0:17], lhsT=v_tok[:, i, h, :],
                                              rhs=k_tok[:, i, h, :],
                                              start=(i == 0), stop=False)
                                t_kv = mm.mark(TE.raw.matmul(
                                    kvps[h][:, 17:273], lhsT=v_tok[:, i, h, :],
                                    rhs=kf2[:, par, h, :],
                                    start=False, stop=(i == NT - 1)))
                            if i >= 1:
                                t_kf2_used = t_kv

                        t_kvS = []
                        for h in range(8):
                            SE.wait(t_kv)
                            t_kvS.append(sc.mark(SE.raw.copy(kvS[:, h, :], kvps[h])))
                            bank_free[h] = t_kvS[-1]
                        for h in range(8):
                            for pi, (lo, hi, dst) in enumerate(((0, 17, kvA), (17, 145, kvB),
                                                                (145, 273, kvC))):
                                bk = (h * 3 + pi) % 2
                                use_bank(bk)
                                TE.wait(t_kvS[h])
                                tr = mm.mark(TE.raw.matmul(
                                    PS[bk][0:(hi - lo), 0:17], lhsT=kvS[:, h, lo:hi],
                                    rhs=ident16[0:17, 0:17], start=True, stop=True))
                                VE.wait(tr)
                                bank_free[bk] = vv.mark(VE.raw.tensor_copy(
                                    dst[:, h, :], PS[bk][0:(hi - lo), 0:17]))
                        t_kvT = vv.tok()
                        barrier()

                    if l0sub < 4:
                        return
                    with ExitStack() as st:
                        qsc = sbl(st, "qsc", (128, 8, 16), BF16)
                        qf2 = sbl(st, "qf2", (128, 8, 256), BF16)
                        qf2T = sbl(st, "qf2T", (128, 2, 8, 512), BF16)
                        outS = sbl(st, "outS", (17, 8, 512), BF16)
                        recS = sbl(st, "recS", (1, 2, 512), F32)
                        denL = sbl(st, "denL", (8, 512), F32)
                        recLb = sbl(st, "recLb", (8, 512), BF16)
                        wout2 = sbl(st, "wout2", (17, 8, D), BF16)
                        t_wo2z = vv.mark(VE.raw.memset(wout2[0:1, :, :], 0.0))
                        t_wo2 = dma(GE, gd, wout2[1:17, :, :],
                                    wo0.rearrange("(h e) m -> e h m", e=16),
                                    allow_slow_non_contiguous=True)
                        t_dstg2 = [None, None]
                        for cch in range(NCH):
                            t_qf2T = []
                            t_dens = []
                            t_qf2_used = mm.tok()
                            for jj in range(4):
                                i = 4 * cch + jj
                                VE.wait(t_tok[i][0], t_qf2_used)
                                VE.raw.drain()
                                VE.raw.tensor_scalar(
                                    out=qsc[:], in0=q_tok[:, i, :].rearrange(
                                        "p (h d) -> p h d", h=8),
                                    scalar1=TAYSC, scalar2=None, op0=OP.mult)
                                VE.raw.drain()
                                for j in range(16):
                                    t_qf2 = vv.mark(VE.raw.tensor_tensor(
                                        out=qf2[:, :, j * 16:(j + 1) * 16], in0=qsc[:],
                                        in1=qsc[:, :, j:j + 1].broadcast_to([128, 8, 16]),
                                        op=OP.mult))
                                TE.wait(t_qf2)
                                for h in range(8):
                                    for piece in range(2):
                                        bk = (h * 2 + piece) % 2
                                        use_bank(bk)
                                        tr = mm.mark(TE.raw.matmul(
                                            PS[bk][:, 0:128],
                                            lhsT=qf2[:, h, piece * 128:(piece + 1) * 128],
                                            rhs=ident16[:], start=True, stop=True))
                                        VE.wait(tr)
                                        bank_free[bk] = vv.mark(VE.raw.tensor_copy(
                                            qf2T[:, piece, h, jj * 128:(jj + 1) * 128],
                                            PS[bk][:, 0:128]))
                                        t_qf2T.append(bank_free[bk])
                                t_qf2_used = mm.tok()
                            if l0sub < 5:
                                continue
                            t_sc_h = []
                            for h in range(8):
                                bko = 2 + h % 2
                                use_bank(bko)
                                ps17 = PS[bko][0:17, :]
                                TE.wait(t_qaug, t_qf2T[-1], t_kvT)
                                TE.raw.matmul(ps17, lhsT=kvA[:, h, :],
                                              rhs=q_aug[:, h, cch * 512:(cch + 1) * 512],
                                              start=True, stop=False)
                                TE.raw.matmul(ps17, lhsT=kvB[:, h, :], rhs=qf2T[:, 0, h, :],
                                              start=False, stop=False)
                                t3 = mm.mark(TE.raw.matmul(ps17, lhsT=kvC[:, h, :],
                                                           rhs=qf2T[:, 1, h, :],
                                                           start=False, stop=True))
                                SE.wait(t3, t_dstg2[h % 2])
                                t_outS = sc.mark(SE.raw.copy(outS[:, h, :], ps17))
                                t_den = sc.mark(SE.raw.copy(recS[0:1, h % 2, :],
                                                            ps17[0:1, :]))
                                bank_free[bko] = t_den
                                t_dd = dma(SY, sd, denL[h:h + 1, :],
                                           recS[0:1, h % 2, :], deps=(t_den,))
                                t_dstg2[h % 2] = t_dd
                                t_dens.append(t_dd)
                            VE.wait(*t_dens)
                            VE.raw.drain()
                            VE.raw.tensor_scalar(out=denL[:], in0=denL[:],
                                                 scalar1=1e-5, scalar2=None, op0=OP.max)
                            VE.raw.drain()
                            VE.raw.reciprocal(denL[:], denL[:])
                            VE.raw.drain()
                            t_rcb = vv.mark(VE.raw.tensor_copy(recLb[:], denL[:]))
                            t_outS_all = sc.tok()
                            for h in range(8):
                                bk4 = 4 + h % 2
                                use_bank(bk4)
                                TE.wait(t_rcb, tk["consts"])
                                mmb = mm.mark(TE.raw.matmul(
                                    PS[bk4][0:17, :],
                                    lhsT=consts["sel8"][:, h * 17:(h + 1) * 17],
                                    rhs=recLb[:], start=True, stop=True))
                                VE.wait(mmb, t_outS_all)
                                t_sch = vv.mark(VE.raw.tensor_tensor(
                                    out=outS[:, h, :], in0=outS[:, h, :],
                                    in1=PS[bk4][0:17, :], op=OP.mult))
                                bank_free[bk4] = t_sch
                                t_sc_h.append(t_sch)
                            if l0sub < 6:
                                continue
                            for tt_ in range(4):
                                for dch in range(2):
                                    bk = 6 + (tt_ * 2 + dch) % 2
                                    use_bank(bk)
                                    TE.wait(t_sc_h, t_wo2z, t_wo2)
                                    for h in range(8):
                                        t_wo = mm.mark(TE.raw.matmul(
                                            PS[bk], lhsT=outS[:, h, tt_ * 128:(tt_ + 1) * 128],
                                            rhs=wout2[:, h, dch * 512:(dch + 1) * 512],
                                            start=(h == 0), stop=(h == 7)))
                                    VE.wait(t_wo)
                                    aslc = acc[:, 4 * cch + tt_, dch * 512:(dch + 1) * 512]
                                    bank_free[bk] = vv.mark(VE.raw.tensor_tensor(
                                        out=aslc, in0=aslc, in1=PS[bk], op=OP.add))
                    barrier()

            def l0_ffn(tern_gen):
                with ExitStack() as st:
                    W1q = sbl(st, "W1q", (128, 2, DT, 1024), BF16)
                    W2q = sbl(st, "W2q", (128, 2, 8, 1024), BF16)
                    h1 = sbl(st, "h1", (128, 8, 512), BF16)
                    b2s = sbl(st, "b2s", (1, 2, D), BF16)
                    G = sbl(st, "G", (128, NT, 8), F32)
                    lgt = sbl(st, "lgt", (128, 8), F32)
                    lgt2 = sbl(st, "lgt2", (128, 8), F32)
                    colsr = sbl(st, "colsr", (128, 6), F32)
                    for i in range(NT):
                        bk = 6 + i % 2
                        use_bank(bk)
                        TE.wait(tk["wr"], tk.get("smallpar"))
                        psl = PS[bk][:, 0:8]
                        for a in range(DT):
                            TE.raw.matmul(psl, lhsT=xnT[:, a, i * 128:(i + 1) * 128],
                                          rhs=wr_b[:, a, :], start=(a == 0), stop=False)
                        t_lg = mm.mark(TE.raw.matmul(psl, lhsT=ones_row_bf[:],
                                                     rhs=br_b[:], start=False, stop=True))
                        VE.wait(t_lg)
                        VE.raw.tensor_reduce(colsr[:, 0:1], psl, AX, OP.max)
                        VE.raw.drain()
                        t_mx = vv.mark(VE.raw.tensor_scalar(
                            out=colsr[:, 1:2], in0=colsr[:, 0:1],
                            scalar1=-1.0, scalar2=None, op0=OP.mult))
                        SE.wait(t_mx)
                        t_e = sc.mark(SE.raw.activation(lgt[:], psl, ACT.Exp,
                                                        bias=colsr[:, 1:2]))
                        VE.wait(t_e)
                        VE.raw.tensor_reduce(colsr[:, 2:3], lgt[:], AX, OP.add)
                        VE.raw.drain()
                        VE.raw.reciprocal(colsr[:, 3:4], colsr[:, 2:3])
                        VE.raw.tensor_scalar(out=lgt2[:], in0=psl, scalar1=colsr[:, 0:1],
                                             scalar2=-1e6, op0=OP.is_ge, op1=OP.mult)
                        VE.raw.drain()
                        bank_free[bk] = vv.mark(VE.raw.tensor_tensor(
                            out=lgt2[:], in0=lgt2[:], in1=consts["iota8big"][:], op=OP.add))
                        VE.raw.drain()
                        VE.raw.tensor_reduce(colsr[:, 4:5], lgt2[:], AX, OP.min)
                        VE.raw.drain()
                        VE.raw.tensor_scalar(out=lgt2[:], in0=consts["iota8"][:],
                                             scalar1=colsr[:, 4:5], scalar2=None,
                                             op0=OP.is_equal)
                        VE.raw.drain()
                        t_G = vv.mark(VE.raw.tensor_scalar(
                            out=G[:, i, :], in0=lgt2[:], scalar1=colsr[:, 3:4],
                            scalar2=None, op0=OP.mult))
                    if "G" in dbg_aps:
                        final_outs.append(dma(SY, sd, dbg_aps["G"], G[:], deps=(t_G,)))

                    t_w1use = [None, None]
                    t_w2use = [None, None]
                    t_b2use = [None, None]
                    t_h1use = mm.tok()
                    for e in range(8):
                        sl = e % 2
                        t_b2 = dma(GE, b2ss[sl], b2s[0:1, sl:sl + 1, :],
                                   b2_0[e:e + 1, :].rearrange("(o e) d -> o e d", o=1),
                                   deps=(t_b2use[sl],))
                        for qq in range(4):
                            s = (e * 4 + qq) % 2
                            t_w1 = dma(GE, moes[s], W1q[:, s, :, :],
                                       w1_0[e].rearrange("(a p) f -> p a f", p=128)
                                       [:, :, qq * 1024:(qq + 1) * 1024],
                                       deps=(t_w1use[s],))
                            t_w2 = dma(GE, moes[s], W2q[:, s, :, :],
                                       w2_0[e].rearrange("(b p) d -> p b d", p=128)
                                       [:, qq * 8:(qq + 1) * 8, :],
                                       deps=(t_w2use[s],))
                            for tq in range(max(1, NCH)):
                                for ft in range(8):
                                    bk = ft % 2
                                    use_bank(bk)
                                    TE.wait(t_w1)
                                    for a in range(DT):
                                        t_p1 = mm.mark(TE.raw.matmul(
                                            PS[bk], lhsT=W1q[:, s, a, ft * 128:(ft + 1) * 128],
                                            rhs=xnT[:, a, tq * 512:(tq + 1) * 512],
                                            start=(a == 0), stop=(a == DT - 1)))
                                    SE.wait(t_p1, t_h1use)
                                    bank_free[bk] = sc.mark(SE.raw.activation(
                                        h1[:, ft, :], PS[bk], ACT.Relu,
                                        bias=b1c0[:, e, qq * 8 + ft:qq * 8 + ft + 1]))
                                t_h1 = sc.tok()
                                VE.raw.drain()
                                for dch in range(2):
                                    for ts_ in range(4):
                                        bk = 2 + (dch * 4 + ts_) % 4
                                        use_bank(bk)
                                        TE.wait(t_h1, t_w2, t_G, t_b2)
                                        for ft in range(8):
                                            ins = TE.raw.matmul(
                                                PS[bk],
                                                lhsT=h1[:, ft, ts_ * 128:(ts_ + 1) * 128],
                                                rhs=W2q[:, s, ft, dch * 512:(dch + 1) * 512],
                                                start=(ft == 0),
                                                stop=(ft == 7 and qq != 0))
                                            if ft == 7 and qq == 0:
                                                ins = TE.raw.matmul(
                                                    PS[bk], lhsT=ones_row_bf[:],
                                                    rhs=b2s[0:1, sl, dch * 512:(dch + 1) * 512],
                                                    start=False, stop=True)
                                        t_p2 = mm.mark(ins)
                                        VE.wait(t_p2)
                                        aslc = acc[:, tq * 4 + ts_, dch * 512:(dch + 1) * 512]
                                        bank_free[bk] = vv.mark(VE.raw.scalar_tensor_tensor(
                                            out=aslc, in0=PS[bk],
                                            scalar=G[:, tq * 4 + ts_, e:e + 1],
                                            in1=aslc, op0=OP.mult, op1=OP.add))
                                t_h1use = mm.tok()
                            t_w1use[s] = mm.tok()
                            t_w2use[s] = t_w1use[s]
                            if tern_gen is not None:
                                for _ in range(2 if e < 6 else 5):
                                    if next(tern_gen, "END") == "END":
                                        tern_gen = None
                                        break
                        t_b2use[sl] = mm.tok()
                    if tern_gen is not None:
                        for _ in tern_gen:
                            pass
                    barrier()

            def tern_steps():
                st = ExitStack()
                wbuf = sbl(st, "wbuf", (128, 2, 2048), F32)
                redc = sbl(st, "redc", (128, 52), F32)
                thr = sbl(st, "thr", (1, 4), F32)
                thrc = sbl(st, "thrc", (128, 4), F32)
                ttmp = sbl(st, "ttmp", (128, 4, 128), BF16)
                buf1 = sbl(st, "buf1", (128, 8, 256), BF16)
                buf2 = sbl(st, "buf2", (128, 4, 512), BF16)
                slot_used = [None, None]
                t_thrc = None
                tern_ctr = [0]

                def tern_group4(bk, srcs, dsts, thc, deps):
                    use_bank(bk)
                    TE.wait(tk["consts"], *deps)
                    for i4, s_ap in enumerate(srcs):
                        tr = mm.mark(TE.raw.transpose(
                            PS[bk][:, i4 * 128:(i4 + 1) * 128], s_ap, ident32[:]))
                    VE.wait(tr, t_thrc)
                    VE.raw.drain()
                    for i4 in range(4):
                        VE.raw.tensor_scalar(out=ttmp[:, i4, :],
                                             in0=PS[bk][:, i4 * 128:(i4 + 1) * 128],
                                             scalar1=thrc[:, thc + 1:thc + 2],
                                             scalar2=None, op0=OP.is_lt)
                    VE.raw.drain()
                    for i4 in range(4):
                        tt_ = vv.mark(VE.raw.scalar_tensor_tensor(
                            out=dsts[i4], in0=PS[bk][:, i4 * 128:(i4 + 1) * 128],
                            scalar=thrc[:, thc:thc + 1],
                            in1=ttmp[:, i4, :], op0=OP.is_gt, op1=OP.subtract))
                    bank_free[bk] = tt_
                    return tt_

                def p1_src(t):
                    if t < 32:
                        return w1_1[t * 128:(t + 1) * 128, :], redc[:, t:t + 1], 1024
                    tt = t - 32
                    return (w2_1[(tt // 2) * 128:(tt // 2 + 1) * 128,
                                 (tt % 2) * 2048:(tt % 2 + 1) * 2048],
                            redc[:, 32 + tt:33 + tt], 2048)

                t_l = [None, None]
                srcap, _, rl = p1_src(0)
                t_l[0] = dma(SY, tls[0], wbuf[:, 0, 0:rl], srcap, deps=(None,))
                for t in range(48):
                    s = t % 2
                    if t + 1 < 48:
                        srcap, _, rl1 = p1_src(t + 1)
                        t_l[(t + 1) % 2] = dma(SY, tls[(t + 1) % 2],
                                               wbuf[:, (t + 1) % 2, 0:rl1], srcap,
                                               deps=(slot_used[(t + 1) % 2],))
                    _, cdst, rl0 = p1_src(t)
                    VE.wait(t_l[s])
                    slot_used[s] = vv.mark(VE.raw.tensor_reduce(
                        cdst, wbuf[:, s, 0:rl0], AX, OP.add,
                        apply_absolute_value=True))
                    if t % 2 == 1:
                        yield
                for wi in range(2):
                    VE.raw.drain()
                    csrc = redc[:, 0:32] if wi == 0 else redc[:, 32:48]
                    rcol = redc[:, 48 + wi:49 + wi]
                    t_red = vv.mark(VE.raw.tensor_reduce(rcol, csrc, AX, OP.add))
                    use_bank(6)
                    TE.wait(t_red)
                    t_ps = mm.mark(TE.raw.matmul(PS[6][0:1, 0:1], lhsT=rcol,
                                                 rhs=consts["ones_col_f32"][:],
                                                 start=True, stop=True))
                    VE.wait(t_ps)
                    nelem = float(1024 * 4096)
                    VE.raw.tensor_scalar(out=thr[0:1, 0:1], in0=PS[6][0:1, 0:1],
                                         scalar1=1.0 / nelem, scalar2=1e-8,
                                         op0=OP.mult, op1=OP.max)
                    VE.raw.drain()
                    VE.raw.tensor_scalar(out=thr[0:1, 1:2], in0=thr[0:1, 0:1],
                                         scalar1=0.5, scalar2=None, op0=OP.mult)
                    VE.raw.drain()
                    t_thr = vv.mark(VE.raw.tensor_scalar(
                        out=thr[0:1, 2:3], in0=thr[0:1, 1:2],
                        scalar1=-1.0, scalar2=None, op0=OP.mult))
                    bank_free[6] = t_thr
                    use_bank(6)
                    TE.wait(t_thr)
                    t_bc = mm.mark(TE.raw.matmul(PS[6][:, 0:2],
                                                 lhsT=ones_row_f32[:],
                                                 rhs=thr[0:1, 1:3], start=True, stop=True))
                    VE.wait(t_bc)
                    bank_free[6] = vv.mark(VE.raw.tensor_copy(
                        thrc[:, 2 * wi:2 * wi + 2], PS[6][:, 0:2]))
                    t_thrc = vv.tok()
                yield

                t_st = None
                w1r = w1_1.rearrange("(t p) d -> p t d", p=128)
                t_l[0] = dma(SY, tls[0],
                             wbuf[:, 0, 0:2048].rearrange("p (t d) -> p t d", t=2),
                             w1r[:, 0:2, :], deps=(slot_used[0],))
                for fb in range(16):
                    s = fb % 2
                    if fb + 1 < 16:
                        s1 = (fb + 1) % 2
                        t_l[s1] = dma(SY, tls[s1],
                                      wbuf[:, s1, 0:2048].rearrange("p (t d) -> p t d", t=2),
                                      w1r[:, (fb + 1) * 2:(fb + 2) * 2, :],
                                      deps=(slot_used[s1],))
                    for g4 in range(4):
                        ftl, a0 = g4 // 2, (g4 % 2) * 4
                        srcs = [wbuf[:, s, ftl * 1024 + (a0 + j) * 128:
                                     ftl * 1024 + (a0 + j + 1) * 128] for j in range(4)]
                        dsts = [buf1[:, a0 + j, ftl * 128:(ftl + 1) * 128]
                                for j in range(4)]
                        tt_ = tern_group4(6 + g4 % 2, srcs, dsts, 0, (t_l[s], t_st))
                    slot_used[s] = tt_
                    t_st = dma(SY, tst, w1t_sc[:, :, fb * 256:(fb + 1) * 256],
                               buf1[:], deps=(tt_,))
                    yield
                w2r = w2_1.rearrange("(dc p) f -> p dc f", p=128)

                def p2b_ap(u):
                    ftb, dc = u // 8, u % 8
                    return w2r[:, dc, ftb * 512:(ftb + 1) * 512]

                t_l[0] = dma(SY, tls[0], wbuf[:, 0, 0:512], p2b_ap(0),
                             deps=(slot_used[0],))
                for u in range(64):
                    ftb, dc = u // 8, u % 8
                    dh, dc2 = dc // 4, dc % 4
                    s = u % 2
                    if u + 1 < 64:
                        s1 = (u + 1) % 2
                        t_l[s1] = dma(SY, tls[s1], wbuf[:, s1, 0:512], p2b_ap(u + 1),
                                      deps=(slot_used[s1],))
                    srcs = [wbuf[:, s, fl * 128:(fl + 1) * 128] for fl in range(4)]
                    dsts = [buf2[:, fl, dc2 * 128:(dc2 + 1) * 128] for fl in range(4)]
                    tt_ = tern_group4(6 + u % 2, srcs, dsts, 2, (t_l[s], t_st))
                    slot_used[s] = tt_
                    if dc2 == 3:
                        t_st = dma(SY, tst,
                                   w2t_sc[:, ftb * 4:(ftb + 1) * 4,
                                          dh * 512:(dh + 1) * 512],
                                   buf2[:], deps=(tt_,))
                    if u % 2 == 1:
                        yield
                tk["tern"] = t_st
                st.close()

            def l1_attn():
                st_l1 = ExitStack()
                qrotT = sbl(st_l1, "qrotT", (128, 8, T), BF16)
                krotT = sbl(st_l1, "krotT", (128, 8, T + VKV), BF16)
                v_ext = sbl(st_l1, "v_ext", (128, NT + 1, 32, 65), BF16)
                with st_l1:
                    with ExitStack() as st:
                        sincos = sbl(st, "sincos", (128, 2, T), BF16)
                        Wb = sbl(st, "Wqk", (128, DT, D), BF16)
                        qb = sbl(st, "qb", (128, 512), BF16)
                        rt1 = sbl(st, "rt1", (128, 512), BF16)
                        rt2 = sbl(st, "rt2", (128, 512), BF16)
                        tk["sc"] = dma(GE, gd, sincos[:, 0, :], sin_in)
                        tk["sc"] = dma(GE, gd, sincos[:, 1, :], cos_in)
                        t_W = None
                        t_Wuse = None
                        t_qb_used = None
                        for which in range(2):
                            t_W = dma(GE, gw, Wb[:], (wqf1 if which == 0 else wkf1)
                                      .rearrange("(a p) m -> p a m", p=128), deps=(t_Wuse,))
                            dstT = qrotT if which == 0 else krotT
                            for a2 in range(8):
                                for cch in range(NCH):
                                    bk = (a2 * NCH + cch) % 2
                                    use_bank(bk)
                                    TE.wait(t_W)
                                    for a in range(DT):
                                        t_pq = mm.mark(TE.raw.matmul(
                                            PS[bk], lhsT=Wb[:, a, a2 * 128:(a2 + 1) * 128],
                                            rhs=xnT[:, a, cch * 512:(cch + 1) * 512],
                                            start=(a == 0), stop=(a == DT - 1)))
                                    VE.wait(t_pq, t_qb_used)
                                    VE.raw.drain()
                                    if which == 0:
                                        t_evq = vv.mark(VE.raw.tensor_scalar(
                                            out=qb[:], in0=PS[bk],
                                            scalar1=bq_c[:, a2:a2 + 1], scalar2=L1QSC,
                                            op0=OP.add, op1=OP.mult))
                                    else:
                                        t_evq = vv.mark(VE.raw.tensor_scalar(
                                            out=qb[:], in0=PS[bk],
                                            scalar1=bk_c[:, a2:a2 + 1], scalar2=None,
                                            op0=OP.add))
                                    bank_free[bk] = t_evq
                                    bk2 = 2 + (a2 * NCH + cch) % 2
                                    use_bank(bk2)
                                    TE.wait(t_evq, tk["consts"])
                                    t_fl = mm.mark(TE.raw.matmul(
                                        PS[bk2], lhsT=consts["pflip"][:], rhs=qb[:],
                                        start=True, stop=True))
                                    VE.wait(t_fl, tk["sc"])
                                    VE.raw.drain()
                                    VE.raw.tensor_tensor(
                                        out=rt1[:], in0=qb[:],
                                        in1=sincos[:, 1, cch * 512:(cch + 1) * 512], op=OP.mult)
                                    VE.raw.tensor_tensor(
                                        out=rt2[:], in0=PS[bk2],
                                        in1=sincos[:, 0, cch * 512:(cch + 1) * 512], op=OP.mult)
                                    VE.raw.drain()
                                    t_rot = vv.mark(VE.raw.tensor_tensor(
                                        out=dstT[:, a2, cch * 512:(cch + 1) * 512],
                                        in0=rt1[:], in1=rt2[:], op=OP.add))
                                    bank_free[bk2] = t_rot
                                    t_qb_used = t_rot
                            t_Wuse = mm.tok()
                        tk["rotdone"] = vv.tok()
                        tk["vk"] = dma(GE, gd, krotT[:, :, T:T + VKV],
                                       vk1.rearrange("(a p) m -> p a m", p=128),
                                       allow_slow_non_contiguous=True)
                        barrier()

                    with ExitStack() as st:
                        WVb = sbl(st, "WVb", (128, DT, 1024), BF16)
                        bv_r = sbl(st, "bv_r", (1, 2048), BF16)
                        t_bv = dma(GE, gd, bv_r[:], bv1)
                        t_vz = vv.mark(VE.raw.memset(v_ext[:], 0.0))
                        VE.raw.drain()
                        t_vo = vv.mark(VE.raw.memset(v_ext[:, :, :, 64:65], 1.0))
                        tk["vv"] = dma(GE, gd, v_ext[0:VKV, NT, :, 0:64],
                                       vv1_in.rearrange("v (h r) -> v h r", h=32),
                                       deps=(t_vz, t_vo))
                        t_WVuse = None
                        for half in range(2):
                            t_WV = dma(GE, gw, WVb[:],
                                       wvf1.rearrange("(a p) m -> p a m", p=128)
                                       [:, :, half * 1024:(half + 1) * 1024],
                                       deps=(t_WVuse,))
                            for kvt in range(NT):
                                for hc2 in range(2):
                                    hc = half * 2 + hc2
                                    bk = 4 + (kvt * 2 + hc2) % 4
                                    use_bank(bk)
                                    TE.wait(t_WV, t_bv)
                                    for a in range(DT):
                                        TE.raw.matmul(
                                            PS[bk], lhsT=xnT[:, a, kvt * 128:(kvt + 1) * 128],
                                            rhs=WVb[:, a, hc2 * 512:(hc2 + 1) * 512],
                                            start=(a == 0), stop=False)
                                    t_pv = mm.mark(TE.raw.matmul(
                                        PS[bk], lhsT=ones_row_bf[:],
                                        rhs=bv_r[0:1, hc * 512:(hc + 1) * 512],
                                        start=False, stop=True))
                                    VE.wait(t_pv, t_vo)
                                    bank_free[bk] = vv.mark(VE.raw.tensor_copy(
                                        v_ext[:, kvt, hc * 8:(hc + 1) * 8, 0:64],
                                        PS[bk].rearrange("p (h r) -> p h r", h=8)))
                            t_WVuse = mm.tok()
                        barrier()

                    if "qrot" in dbg_aps:
                        final_outs.append(dma(SY, sd, dbg_aps["qrot"], qrotT[:]))
                    if "krot" in dbg_aps:
                        final_outs.append(dma(SY, sd, dbg_aps["krot"], krotT[:]))
                    if "vext" in dbg_aps:
                        final_outs.append(dma(SY, sd, dbg_aps["vext"],
                                              v_ext[:].rearrange("p a b c -> p (a b c)")))

                    tk["xnspill"] = dma(SY, sd, xnsp, xnT[:])
                    barrier()
                    st_xn.close()

                    with ExitStack() as st:
                        zT = sbl(st, "zT", (128, 16, 512), BF16)
                        expT = sbl(st, "expT", (128, 8, 512), BF16)
                        denM = sbl(st, "denM", (32, 512), F32)
                        recMb = sbl(st, "recMb", (32, 512), BF16)
                        den_stg = sbl(st, "den_stg", (1, 2, 512), F32)
                        sel_sb = sbl(st, "sel_sb", (32, 2048), BF16)
                        WOb = sbl(st, "WOb", (128, 16, 512), BF16)
                        stg = sbl(st, "stg", (128, 2, 512), F32)
                        shcol = sbl(st, "shcol", (128, 1), F32)
                        s_masks = sbl(st, "s_masks", (128, 5, 512), BF16)
                        t_sh = vv.mark(VE.raw.memset(shcol[:], EXP_SHIFT))
                        SE.wait(t_sh)
                        t_mskld = dma(SY, sd, s_masks[:], masks_in.rearrange("m p q -> p m q"))
                        t_selld = dma(SY, sd, sel_sb[:], sel_in)
                        wof_r = wof1.rearrange("(vc p) d -> p vc d", p=128)
                        t_wob_used = None
                        t_stg_used = [None, None]
                        t_zT_used = None
                        t_dstg_used = [None, None]
                        t_exp_used = [[mm.tok(), mm.tok()] for _ in range(4)]
                        for cch in range(NCH):
                            full = [kvt for kvt in range(NT) if kvt * 128 <= cch * 512 + 527]
                            has_virt = T <= cch * 512 + 527
                            nsteps = len(full) + (1 if has_virt else 0)
                            t_zT = []
                            t_den_dmas = []
                            t_wob = dma(GE, gw, WOb[:], wof_r[:, :, 0:512],
                                        deps=(t_wob_used,))
                            for a2 in range(8):
                                t_e = [[None, None] for _ in range(4)]
                                t_z = [None] * 4
                                for hh in range(4):
                                    use_bank(4 + hh)
                                for p in range(nsteps + 1):
                                    if p < nsteps:
                                        virt = p >= len(full)
                                        kvt = full[p] if not virt else NT
                                        rows = 128 if not virt else VKV
                                        delta = (kvt * 128 if not virt else T) - cch * 512
                                        sl = p % 2
                                        for hh in range(4):
                                            hb = 32 * hh
                                            use_bank(hh)
                                            TE.wait(tk["rotdone"], tk["vk"])
                                            t_s = mm.mark(TE.raw.matmul(
                                                PS[hh][0:rows, :],
                                                lhsT=krotT[hb:hb + 32, a2,
                                                           kvt * 128:kvt * 128 + rows],
                                                rhs=qrotT[hb:hb + 32, a2,
                                                          cch * 512:(cch + 1) * 512],
                                                start=True, stop=True,
                                                tile_position=(hb, 0)))
                                            SE.wait(t_s, t_exp_used[hh][sl])
                                            t_ex = sc.mark(SE.raw.activation(
                                                expT[0:rows, 2 * hh + sl, :],
                                                PS[hh][0:rows, :],
                                                ACT.Exp, bias=shcol[0:rows, :]))
                                            bank_free[hh] = t_ex
                                            if 0 <= delta <= 512:
                                                VE.wait(t_ex, t_mskld)
                                                t_ex = vv.mark(VE.raw.tensor_tensor(
                                                    out=expT[0:rows, 2 * hh + sl, :],
                                                    in0=expT[0:rows, 2 * hh + sl, :],
                                                    in1=s_masks[0:rows, delta // 128, :],
                                                    op=OP.mult))
                                            t_e[hh][sl] = t_ex
                                    if p >= 1:
                                        pz = p - 1
                                        virt = pz >= len(full)
                                        kvt = full[pz] if not virt else NT
                                        rows = 128 if not virt else VKV
                                        sl = pz % 2
                                        for hh in range(4):
                                            h = a2 * 4 + hh
                                            TE.wait(t_e[hh][sl], tk["vv"], t_zT_used)
                                            t_z[hh] = mm.mark(TE.raw.matmul(
                                                PS[4 + hh][0:65, :],
                                                lhsT=v_ext[0:rows, kvt, h, :],
                                                rhs=expT[0:rows, 2 * hh + sl, :],
                                                start=(pz == 0), stop=(pz == nsteps - 1)))
                                            t_exp_used[hh][sl] = t_z[hh]
                                for hh in range(4):
                                    h = a2 * 4 + hh
                                    zoff, vc = 64 * (h % 2), h // 2
                                    sl = hh % 2
                                    VE.wait(t_z[hh], t_dstg_used[sl])
                                    VE.raw.drain()
                                    VE.raw.tensor_copy(den_stg[0:1, sl, :],
                                                       PS[4 + hh][64:65, :])
                                    VE.raw.drain()
                                    t_zc = vv.mark(VE.raw.tensor_copy(
                                        zT[zoff:zoff + 64, vc, :], PS[4 + hh][0:64, :]))
                                    bank_free[4 + hh] = t_zc
                                    t_dd = dma(SY, sd, denM[h:h + 1, :],
                                               den_stg[0:1, sl, :], deps=(t_zc,))
                                    t_dstg_used[sl] = t_dd
                                    t_den_dmas.append(t_dd)
                            VE.wait(*t_den_dmas)
                            VE.raw.drain()
                            VE.raw.reciprocal(denM[:], denM[:])
                            VE.raw.drain()
                            t_rcb = vv.mark(VE.raw.tensor_copy(recMb[:], denM[:]))
                            for h in range(32):
                                zoff, vc = 64 * (h % 2), h // 2
                                bk = h % 8
                                use_bank(bk)
                                TE.wait(t_rcb, t_selld)
                                t_rb = mm.mark(TE.raw.matmul(
                                    PS[bk][0:64, :],
                                    lhsT=sel_sb[:, h * 64:(h + 1) * 64],
                                    rhs=recMb[:], start=True, stop=True))
                                VE.wait(t_rb)
                                t_zn = vv.mark(VE.raw.tensor_tensor(
                                    out=zT[zoff:zoff + 64, vc, :],
                                    in0=zT[zoff:zoff + 64, vc, :],
                                    in1=PS[bk][0:64, :], op=OP.mult))
                                bank_free[bk] = t_zn
                                t_zT.append(t_zn)
                            for dch in range(2):
                                if dch == 1:
                                    t_wob = dma(GE, gw, WOb[:], wof_r[:, :, 512:1024],
                                                deps=(t_wob_used,))
                                for qt in range(4):
                                    bk = (qt + dch * 2) % 4
                                    use_bank(bk)
                                    TE.wait(t_zT, t_wob)
                                    for vc in range(16):
                                        t_wo = mm.mark(TE.raw.matmul(
                                            PS[bk], lhsT=zT[:, vc, qt * 128:(qt + 1) * 128],
                                            rhs=WOb[:, vc, :],
                                            start=(vc == 0), stop=(vc == 15)))
                                    sp = qt % 2
                                    SE.wait(t_wo, t_stg_used[sp])
                                    t_sg = sc.mark(SE.raw.copy(stg[:, sp, :], PS[bk]))
                                    bank_free[bk] = t_sg
                                    r0 = cch * 512 + qt * 128
                                    t_acd = dma(GE, accs[sp],
                                                o_out[r0:r0 + 128, dch * 512:(dch + 1) * 512],
                                                stg[:, sp, :], deps=(t_sg, tk["spill"]),
                                                accum_op=OP.add)
                                    t_stg_used[sp] = t_acd
                                    final_outs.append(t_acd)
                                t_wob_used = mm.tok()
                            t_zT_used = sc.tok()
                    barrier()

            def l1_ffn():
                with ExitStack() as st:
                    xnb = sbl(st, "xnb", (128, DT, T), BF16)
                    b2r1 = sbl(st, "b2r1", (1, D), BF16)
                    t_b2f = dma(GE, gd, b2r1[:], b2_1)
                    w1tb = sbl(st, "w1tb", (128, DT, 4096), BF16)
                    w2tb = sbl(st, "w2tb", (128, 2, 32, 512), BF16)
                    hbuf = sbl(st, "hbuf", (128, 32, 512), BF16)
                    stg = sbl(st, "stg", (128, 2, 512), F32)
                    t_xn = dma(SY, sd, xnb[:], xnsp, deps=(tk["xnspill"],))
                    t_w1t = dma(GE, ffws[0], w1tb[:], w1t_sc, deps=(tk["tern"],))
                    t_w2use = [None, None]
                    t_stg_used = [None, None]
                    t_huse = mm.tok()
                    for tq in range(max(1, NCH)):
                        for dch in range(2):
                            ws = dch
                            t_w2t = dma(GE, ffws[1 + dch], w2tb[:, ws, :, :],
                                        w2t_sc[:, :, dch * 512:(dch + 1) * 512],
                                        deps=(t_w2use[ws], tk["tern"]))
                            if dch == 0:
                                for fc in range(32):
                                    bk = fc % 2
                                    use_bank(bk)
                                    TE.wait(t_w1t, t_xn)
                                    for a in range(DT):
                                        t_p1 = mm.mark(TE.raw.matmul(
                                            PS[bk], lhsT=w1tb[:, a, fc * 128:(fc + 1) * 128],
                                            rhs=xnb[:, a, tq * 512:(tq + 1) * 512],
                                            start=(a == 0), stop=(a == DT - 1)))
                                    SE.wait(t_p1, t_huse)
                                    bank_free[bk] = sc.mark(SE.raw.activation(
                                        hbuf[:, fc, :], PS[bk], ACT.Gelu,
                                        bias=b1c1[:, fc:fc + 1]))
                                t_h = sc.tok()
                            for qt in range(4):
                                bk = 2 + (qt % 4)
                                use_bank(bk)
                                TE.wait(t_h, t_w2t, t_b2f)
                                for fc in range(32):
                                    TE.raw.matmul(
                                        PS[bk], lhsT=hbuf[:, fc, qt * 128:(qt + 1) * 128],
                                        rhs=w2tb[:, ws, fc, :],
                                        start=(fc == 0), stop=False)
                                t_p2 = mm.mark(TE.raw.matmul(
                                    PS[bk], lhsT=ones_row_bf[:],
                                    rhs=b2r1[0:1, dch * 512:(dch + 1) * 512],
                                    start=False, stop=True))
                                sp = qt % 2
                                SE.wait(t_p2, t_stg_used[sp])
                                t_sg = sc.mark(SE.raw.copy(stg[:, sp, :], PS[bk]))
                                bank_free[bk] = t_sg
                                r0 = tq * 512 + qt * 128
                                t_acd = dma(GE, accs[sp],
                                            o_out[r0:r0 + 128, dch * 512:(dch + 1) * 512],
                                            stg[:, sp, :], deps=(t_sg, tk["spill"]),
                                            accum_op=OP.add)
                                t_stg_used[sp] = t_acd
                                final_outs.append(t_acd)
                            t_w2use[ws] = mm.tok()
                        t_huse = mm.tok()
                    barrier()

            st_S = ExitStack()
            S_full = st_S.enter_context(
                nc.sbuf_tensor("S_full", [128, T], F32, side="right"))
            layernorm_phase(True, S_full)
            if "xnT" in dbg_aps and upto < 3:
                final_outs.append(dma(SY, sd, dbg_aps["xnT"], xnT[:]))
            if upto >= 2:
                l0_attn(S_full, st_S)
            else:
                st_S.close()
            if "acc1" in dbg_aps:
                final_outs.append(dma(SY, sd, dbg_aps["acc1"], acc[:],
                                      deps=(vv.tok(), mm.tok(), sc.tok())))
            if upto >= 3:
                l0_ffn(tern_steps() if upto >= 4 else None)
            if "acc2" in dbg_aps:
                final_outs.append(dma(SY, sd, dbg_aps["acc2"], acc[:],
                                      deps=(vv.tok(), mm.tok(), sc.tok())))
            if upto >= 5:
                layernorm_phase(False, None)
                if "xnT" in dbg_aps:
                    final_outs.append(dma(SY, sd, dbg_aps["xnT"], xnT[:]))
                tk["spill"] = dma(SY, sd, o_out.rearrange("(i p) d -> p i d", p=128),
                                  acc[:], deps=(vv.tok(), mm.tok()))
                final_outs.append(tk["spill"])
                barrier()
                st_acc.close()
            if upto >= 6:
                l1_attn()
            if upto >= 7:
                l1_ffn()
            if upto < 5:
                tk["spill"] = dma(SY, sd, o_out.rearrange("(i p) d -> p i d", p=128),
                                  acc[:], deps=(vv.tok(), mm.tok(), sc.tok()))
                final_outs.append(tk["spill"])
            SY.wait(*[s.tok() for s in ALLSEMS])
            SY.wait(*final_outs)

    return nc


def host_inputs(inputs, T=2048):
    f32 = np.float32
    c = host_constants(T)
    WQ = np.ascontiguousarray(
        inputs["L1_attn_W_Q"].reshape(8, 4, 1024, 32).transpose(2, 0, 1, 3).reshape(1024, 1024))
    WK = np.ascontiguousarray(
        inputs["L1_attn_W_K"].reshape(8, 4, 1024, 32).transpose(2, 0, 1, 3).reshape(1024, 1024))
    WV = np.ascontiguousarray(inputs["L1_attn_W_V"][:, :, 0].T)
    WO = np.ascontiguousarray(inputs["L1_attn_W_O"][:, 0, :])
    bQ = np.ascontiguousarray(inputs["L1_attn_b_Q"].reshape(8, 4, 32).reshape(8, 128))
    bK = np.ascontiguousarray(inputs["L1_attn_b_K"].reshape(8, 4, 32).reshape(8, 128))
    bV = np.ascontiguousarray(inputs["L1_attn_b_V"][:, 0][None, :])
    vk = np.ascontiguousarray(
        inputs["L1_attn_virtual_k"].reshape(16, 8, 4, 32).transpose(1, 2, 3, 0).reshape(1024, 16))
    vvv = np.ascontiguousarray(inputs["L1_attn_virtual_v"][:, :, 0])
    base = {
        "L0_gamma": np.ascontiguousarray(inputs["L0_attn_gamma"].reshape(8, 128), dtype=f32),
        "L0_w_q": np.ascontiguousarray(inputs["L0_attn_w_q"], dtype=f32),
        "L0_w_kv": np.ascontiguousarray(inputs["L0_attn_w_kv"], dtype=f32),
        "L0_w_out": np.ascontiguousarray(inputs["L0_attn_w_out"], dtype=f32),
        "L0_WrT": np.ascontiguousarray(inputs["L0_ffn_Wr"].T, dtype=f32),
        "L0_br": np.ascontiguousarray(inputs["L0_ffn_br"][None, :], dtype=f32),
        "L0_W1": np.ascontiguousarray(inputs["L0_ffn_W1"], dtype=f32),
        "L0_b1": np.ascontiguousarray(inputs["L0_ffn_b1"], dtype=f32),
        "L0_W2": np.ascontiguousarray(inputs["L0_ffn_W2"], dtype=f32),
        "L0_b2": np.ascontiguousarray(inputs["L0_ffn_b2"], dtype=f32),
        "L1_WQ": WQ.astype(f32), "L1_WK": WK.astype(f32),
        "L1_WV": WV.astype(f32), "L1_WO": WO.astype(f32),
        "L1_bQ": bQ.astype(f32), "L1_bK": bK.astype(f32), "L1_bV": bV.astype(f32),
        "L1_vk": vk.astype(f32), "L1_vv": vvv.astype(f32),
        "L1_W1": np.ascontiguousarray(inputs["L1_ffn_W1"], dtype=f32),
        "L1_b1": np.ascontiguousarray(inputs["L1_ffn_b1"].reshape(32, 128), dtype=f32),
        "L1_W2": np.ascontiguousarray(inputs["L1_ffn_W2"], dtype=f32),
        "L1_b2": np.ascontiguousarray(inputs["L1_ffn_b2"][None, :], dtype=f32),
        "c_sin": c["sin"][:, :T], "c_cos": c["cos"][:, :T],
        "c_masks": np.ascontiguousarray(c["masks"]),
    }
    for nm in CSHAPES:
        base["c_" + nm] = np.ascontiguousarray(c[nm])
    base["c_sel32"] = np.ascontiguousarray(c["sel32"])
    x = np.asarray(inputs["L0_attn_x"], dtype=f32)
    in_maps = []
    for i in range(8):
        m = dict(base)
        m["x"] = np.ascontiguousarray(x[i, :T, :])
        in_maps.append(m)
    return in_maps


def kernel(**inputs):
    T = 2048
    nc = build(T=T)
    in_maps = host_inputs(inputs, T=T)
    res = run_bass_kernel_spmd(nc, in_maps, core_ids=list(range(8)))
    out = np.stack([np.asarray(r["o"], dtype=np.float32) for r in res.results], axis=0)
    return out


# version 13
# speedup vs baseline: 1.0755x; 1.0081x over previous
import numpy as np
import ml_dtypes
from contextlib import ExitStack

import concourse.bass as bass
import concourse.mybir as mybir
from concourse.bass_utils import run_bass_kernel_spmd

F32 = mybir.dt.float32
BF16 = mybir.dt.bfloat16
AX = mybir.AxisListType.X
OP = mybir.AluOpType
ACT = mybir.ActivationFunctionType

D = 1024
DT = 8
VKV = 16
ROT_BASE = 10000.0
EXP_SHIFT = -20.0
QSC = 0.25
TAYSC = 2.0 ** -0.25
L1QSC = 32.0 ** -0.5
bfdt = ml_dtypes.bfloat16


def host_constants(T):
    c = {}
    c["ident32"] = np.eye(128, dtype=np.float32)
    c["ident16"] = np.eye(128, dtype=np.float32).astype(bfdt)
    c["ones_row_bf"] = np.ones((1, 128), dtype=np.float32).astype(bfdt)
    c["ones_row_f32"] = np.ones((1, 128), dtype=np.float32)
    c["ones_col_f32"] = np.ones((128, 1), dtype=np.float32)
    pos = np.arange(T, dtype=np.float64)
    freq = ROT_BASE ** (np.arange(16, dtype=np.float64) / 16.0)
    ang = pos[None, :] / freq[(np.arange(128) % 16)][:, None]
    c["sin"] = np.sin(ang).astype(np.float32)
    c["cos"] = np.cos(ang).astype(np.float32)
    pf = np.zeros((128, 128), dtype=np.float32)
    for m in range(128):
        if m % 32 < 16:
            pf[m + 16, m] = -1.0
        else:
            pf[m - 16, m] = 1.0
    c["pflip"] = pf.astype(bfdt)
    kvl = np.arange(128)[:, None]
    ql = np.arange(512)[None, :]
    c["masks"] = np.stack(
        [(kvl - ql <= 16 - 128 * dd).astype(np.float32) for dd in range(5)]
    ).astype(bfdt)
    sel8 = np.zeros((8, 136), dtype=np.float32)
    for h in range(8):
        sel8[h, h * 17:(h + 1) * 17] = 1.0
    c["sel8"] = sel8.astype(bfdt)
    sel = np.zeros((32, 2048), dtype=np.float32)
    for h in range(32):
        sel[h, h * 64:(h + 1) * 64] = 1.0
    c["sel32"] = sel.astype(bfdt)
    c["iota8"] = np.broadcast_to(np.arange(8, dtype=np.float32), (128, 8)).copy()
    c["iota8big"] = (c["iota8"] + 1e6).astype(np.float32)
    return c


CSHAPES = dict(ident32=(128, 128, F32), ident16=(128, 128, BF16),
               sel8=(8, 136, BF16),
               ones_row_bf=(1, 128, BF16), ones_row_f32=(1, 128, F32),
               ones_col_f32=(128, 1, F32), pflip=(128, 128, BF16),
               iota8=(128, 8, F32), iota8big=(128, 8, F32))


class Sem:
    def __init__(self, es, nc, name, dma=False):
        self.h = es.enter_context(nc.semaphore(name))
        self.v = 0
        self.name = name
        self.dma = dma

    def mark(self, instr, n=1):
        instr.then_inc(self.h, n)
        self.v += n
        return (self, self.v)

    def tok(self):
        return (self, self.v)


class Eng:
    def __init__(self, raw):
        self.raw = raw
        self.seen = {}

    def wait(self, *deps):
        for d in deps:
            if d is None:
                continue
            if isinstance(d, (list, tuple)) and d and isinstance(d[0], (list, tuple, type(None), Sem)) \
                    and not (len(d) == 2 and isinstance(d[0], Sem)):
                self.wait(*d)
                continue
            sem, val = d
            if sem.dma:
                val = sem.v
            if val > self.seen.get(sem, 0):
                self.raw.wait_ge(sem.h, val)
                self.seen[sem] = val


def build(T=2048, upto=99, debug=(), l0sub=9):
    NT = T // 128
    NCH = T // 512
    nc = bass.Bass("TRN2", target_bir_lowering=False)
    dt = nc.dram_tensor

    x_in = dt("x", [T, D], F32, kind="ExternalInput").ap()
    g0 = dt("L0_gamma", [8, 128], F32, kind="ExternalInput").ap()
    wq0 = dt("L0_w_q", [D, 128], F32, kind="ExternalInput").ap()
    wkv0 = dt("L0_w_kv", [D, 256], F32, kind="ExternalInput").ap()
    wo0 = dt("L0_w_out", [128, D], F32, kind="ExternalInput").ap()
    wr0T = dt("L0_WrT", [8, D], F32, kind="ExternalInput").ap()
    br0 = dt("L0_br", [1, 8], F32, kind="ExternalInput").ap()
    w1_0 = dt("L0_W1", [8, D, 4096], F32, kind="ExternalInput").ap()
    b1_0 = dt("L0_b1", [8, 4096], F32, kind="ExternalInput").ap()
    w2_0 = dt("L0_W2", [8, 4096, D], F32, kind="ExternalInput").ap()
    b2_0 = dt("L0_b2", [8, D], F32, kind="ExternalInput").ap()
    wqf1 = dt("L1_WQ", [D, D], F32, kind="ExternalInput").ap()
    wkf1 = dt("L1_WK", [D, D], F32, kind="ExternalInput").ap()
    wvf1 = dt("L1_WV", [D, 2048], F32, kind="ExternalInput").ap()
    wof1 = dt("L1_WO", [2048, D], F32, kind="ExternalInput").ap()
    bq1 = dt("L1_bQ", [8, 128], F32, kind="ExternalInput").ap()
    bk1 = dt("L1_bK", [8, 128], F32, kind="ExternalInput").ap()
    bv1 = dt("L1_bV", [1, 2048], F32, kind="ExternalInput").ap()
    vk1 = dt("L1_vk", [D, VKV], F32, kind="ExternalInput").ap()
    vv1_in = dt("L1_vv", [VKV, 2048], F32, kind="ExternalInput").ap()
    w1_1 = dt("L1_W1", [4096, D], F32, kind="ExternalInput").ap()
    b1_1 = dt("L1_b1", [32, 128], F32, kind="ExternalInput").ap()
    w2_1 = dt("L1_W2", [D, 4096], F32, kind="ExternalInput").ap()
    b2_1 = dt("L1_b2", [1, D], F32, kind="ExternalInput").ap()
    ci = {nm: dt("c_" + nm, [p_, f_], ty, kind="ExternalInput").ap()
          for nm, (p_, f_, ty) in CSHAPES.items()}
    sel_in = dt("c_sel32", [32, 2048], BF16, kind="ExternalInput").ap()
    sin_in = dt("c_sin", [128, T], F32, kind="ExternalInput").ap()
    cos_in = dt("c_cos", [128, T], F32, kind="ExternalInput").ap()
    masks_in = dt("c_masks", [5, 128, 512], BF16, kind="ExternalInput").ap()

    o_out = dt("o", [T, D], F32, kind="ExternalOutput").ap()
    xnsp = dt("xnspill", [128, DT, T], BF16, kind="Internal").ap()
    w1t_sc = dt("w1t_sc", [128, DT, 4096], BF16, kind="Internal").ap()
    w2t_sc = dt("w2t_sc", [128, 32, D], BF16, kind="Internal").ap()

    dbg_specs = {"xnT": ([128, DT, T], BF16), "srow": ([1, T], F32),
                 "acc1": ([128, NT, D], F32), "acc2": ([128, NT, D], F32),
                 "G": ([128, NT, 8], F32),
                 "qrot": ([128, 8, T], BF16), "krot": ([128, 8, T + VKV], BF16),
                 "vext": ([128, (NT + 1) * 32 * 65], BF16)}
    dbg_aps = {nm: dt("dbg_" + nm, s, ty, kind="ExternalOutput").ap()
               for nm, (s, ty) in dbg_specs.items() if nm in debug}

    es = ExitStack()
    st_acc = ExitStack()
    st_xn = ExitStack()

    _uniq = [0]

    def sbl(st, name, shape, ty):
        _uniq[0] += 1
        return st.enter_context(nc.sbuf_tensor(f"{name}_{_uniq[0]}", list(shape), ty))

    xnT = st_xn.enter_context(nc.sbuf_tensor("xnT", [128, DT, T], BF16, side="right"))
    acc = st_acc.enter_context(nc.sbuf_tensor("acc", [128, NT, D], F32, side="right"))

    consts = {nm: es.enter_context(nc.sbuf_tensor("s_" + nm, [p_, f_], ty))
              for nm, (p_, f_, ty) in CSHAPES.items()}
    gamma_c = es.enter_context(nc.sbuf_tensor("gamma_c", [128, DT], F32))
    wq_b = es.enter_context(nc.sbuf_tensor("wq_b", [128, DT, 128], BF16))
    wkv_b = es.enter_context(nc.sbuf_tensor("wkv_b", [128, DT, 256], BF16))
    wr_b = es.enter_context(nc.sbuf_tensor("wr_b", [128, DT, 8], BF16))
    br_b = es.enter_context(nc.sbuf_tensor("br_b", [1, 8], BF16))
    b1c0 = es.enter_context(nc.sbuf_tensor("b1c0", [128, 8, 32], F32))
    bq_c = es.enter_context(nc.sbuf_tensor("bq_c", [128, 8], F32))
    bk_c = es.enter_context(nc.sbuf_tensor("bk_c", [128, 8], F32))
    b1c1 = es.enter_context(nc.sbuf_tensor("b1c1", [128, 32], F32))

    pA = es.enter_context(nc.psum_tensor("pA", [128, 1024], F32))
    pB = es.enter_context(nc.psum_tensor("pB", [128, 1024], F32))
    pC = es.enter_context(nc.psum_tensor("pC", [128, 1024], F32))
    pD = es.enter_context(nc.psum_tensor("pD", [128, 1024], F32))
    PS = [pA[:, 0:512], pA[:, 512:1024], pB[:, 0:512], pB[:, 512:1024],
          pC[:, 0:512], pC[:, 512:1024], pD[:, 0:512], pD[:, 512:1024]]
    bank_free = [None] * 8

    sd = Sem(es, nc, "sd", dma=True)
    gd = Sem(es, nc, "gd", dma=True)
    gw = Sem(es, nc, "gw", dma=True)
    moes = [Sem(es, nc, "moe0", dma=True), Sem(es, nc, "moe1", dma=True)]
    tls = [Sem(es, nc, "tl0", dma=True), Sem(es, nc, "tl1", dma=True)]
    accs = [Sem(es, nc, "acc0", dma=True), Sem(es, nc, "acc1", dma=True)]
    ffws = [Sem(es, nc, "ffw1", dma=True), Sem(es, nc, "ffw2", dma=True),
            Sem(es, nc, "ffw3", dma=True)]
    b2ss = [Sem(es, nc, "b20", dma=True), Sem(es, nc, "b21", dma=True)]
    tst = Sem(es, nc, "tst", dma=True)
    mm = Sem(es, nc, "mm")
    vv = Sem(es, nc, "vv")
    sc = Sem(es, nc, "sc")
    ALLSEMS = (sd, gd, gw, moes[0], moes[1], tls[0], tls[1], b2ss[0], b2ss[1],
               accs[0], accs[1], ffws[0], ffws[1], ffws[2], tst, mm, vv, sc)

    ident32 = consts["ident32"]
    ident16 = consts["ident16"]
    ones_row_bf = consts["ones_row_bf"]
    ones_row_f32 = consts["ones_row_f32"]
    tk = {}
    final_outs = []

    with es, nc.Block() as block:
        @block.sync
        def _(sync_unused):
            TE = Eng(nc.tensor)
            VE = Eng(nc.vector)
            SE = Eng(nc.scalar)
            GE = Eng(nc.gpsimd)
            SY = Eng(nc.sync)

            def dma(q, sem, out, in_, deps=(), **kw):
                q.wait(*deps)
                return sem.mark(q.raw.dma_start(out=out, in_=in_, **kw), 16)

            def use_bank(bk):
                TE.wait(bank_free[bk])

            def barrier():
                VE.raw.drain()
                toks = [s.tok() for s in ALLSEMS]
                for E in (TE, VE, SE, GE, SY):
                    E.wait(*toks)

            def small_transpose(src_ap, dst_ap, bk, p_, f_, deps=()):
                use_bank(bk)
                TE.wait(tk.get("consts"), *deps)
                tr = mm.mark(TE.raw.matmul(PS[bk][0:f_, 0:p_], lhsT=src_ap,
                                           rhs=ident32[0:p_, 0:p_], start=True, stop=True))
                VE.wait(tr)
                t = vv.mark(VE.raw.tensor_copy(dst_ap, PS[bk][0:f_, 0:p_]))
                bank_free[bk] = t
                return t

            for nm in CSHAPES:
                tk["consts"] = dma(SY, sd, consts[nm][:], ci[nm])
            tk["x"] = dma(SY, sd, acc[:], x_in.rearrange("(i p) d -> p i d", p=128))

            with ExitStack() as st0:
                graw = sbl(st0, "graw", (8, 128), F32)
                wqf = sbl(st0, "wqf", (128, DT, 384), F32)
                wrT = sbl(st0, "wrT", (8, DT, 128), F32)
                b1row = sbl(st0, "b1row", (32, 8, 128), F32)
                bqrow = sbl(st0, "bqrow", (8, 2, 128), F32)
                b1frow = sbl(st0, "b1frow", (32, 128), F32)

                t_g = dma(GE, gd, graw[:], g0)
                t_wq = dma(GE, gd, wqf[:, :, 0:128], wq0.rearrange("(a p) m -> p a m", p=128))
                t_wkv = dma(GE, gd, wqf[:, :, 128:384], wkv0.rearrange("(a p) m -> p a m", p=128))
                t_wr = dma(GE, gd, wrT[:], wr0T.rearrange("e (a p) -> e a p", p=128))
                t_br = dma(GE, gd, br_b[:], br0)
                t_b1r = dma(GE, gd, b1row[:], b1_0.rearrange("e (c p) -> c e p", p=128),
                            allow_slow_non_contiguous=True)
                t_bq = dma(GE, gd, bqrow[:, 0, :], bq1)
                t_bk = dma(GE, gd, bqrow[:, 1, :], bk1)
                t_b1f = dma(GE, gd, b1frow[:], b1_1)

                small_transpose(graw[:], gamma_c[:], 6, 8, 128, deps=(t_g,))
                t_gc = vv.tok()
                VE.wait(t_wq, t_wkv, t_gc)
                VE.raw.drain()
                for a in range(DT):
                    VE.raw.tensor_scalar(out=wq_b[:, a, :], in0=wqf[:, a, 0:128],
                                         scalar1=gamma_c[:, a:a + 1], scalar2=QSC,
                                         op0=OP.mult, op1=OP.mult)
                    tk["wfold"] = vv.mark(VE.raw.tensor_scalar(
                        out=wkv_b[:, a, :], in0=wqf[:, a, 128:384],
                        scalar1=gamma_c[:, a:a + 1], scalar2=None, op0=OP.mult))
                with ExitStack() as st00:
                    wrtmp = sbl(st00, "wrtmp", (128, DT, 8), F32)
                    for a in range(DT):
                        small_transpose(wrT[:, a, :], wrtmp[:, a, :], a % 2, 8, 128,
                                        deps=(t_wr,))
                    VE.raw.drain()
                    tk["wr"] = vv.mark(VE.raw.tensor_copy(wr_b[:], wrtmp[:]))
                for e in range(8):
                    small_transpose(b1row[:, e, :], b1c0[:, e, :], 2 + e % 2, 32, 128,
                                    deps=(t_b1r,))
                small_transpose(bqrow[:, 0, :], bq_c[:], 4, 8, 128, deps=(t_bq,))
                small_transpose(bqrow[:, 1, :], bk_c[:], 5, 8, 128, deps=(t_bk,))
                small_transpose(b1frow[:], b1c1[:], 6, 32, 128, deps=(t_b1f,))
                tk["smallpar"] = vv.tok()
                barrier()

            def layernorm_phase(want_s, S_full):
                with ExitStack() as st:
                    xtmp = sbl(st, "ln_xtmp", (128, 2, D), F32)
                    sq_t = sbl(st, "ln_sq", (128, 2, D), F32)
                    cols = sbl(st, "ln_cols", (128, 8, NT), F32)
                    s_tok = sbl(st, "ln_stok", (128, NT), F32)
                    s_row = sbl(st, "ln_srow", (1, T), F32)
                    t_sq_rd = [None, None]
                    for i in range(NT):
                        VE.raw.tensor_reduce(cols[:, 0, i:i + 1], acc[:, i, :], AX, OP.add)
                        sl = i % 2
                        SE.wait(t_sq_rd[sl])
                        t_sqw = sc.mark(SE.raw.activation(sq_t[:, sl, :], acc[:, i, :],
                                                          ACT.Square))
                        VE.wait(t_sqw)
                        t_sq_rd[sl] = vv.mark(VE.raw.tensor_reduce(
                            cols[:, 1, i:i + 1], sq_t[:, sl, :], AX, OP.add))
                    VE.raw.drain()
                    VE.raw.tensor_scalar(out=cols[:, 2, :], in0=cols[:, 0, :],
                                         scalar1=-1.0 / D, scalar2=None, op0=OP.mult)
                    VE.raw.tensor_scalar(out=cols[:, 3, :], in0=cols[:, 1, :],
                                         scalar1=1.0 / D, scalar2=None, op0=OP.mult)
                    VE.raw.drain()
                    VE.raw.tensor_tensor(out=cols[:, 4, :], in0=cols[:, 2, :],
                                         in1=cols[:, 2, :], op=OP.mult)
                    VE.raw.drain()
                    VE.raw.tensor_tensor(out=cols[:, 4, :], in0=cols[:, 3, :],
                                         in1=cols[:, 4, :], op=OP.subtract)
                    VE.raw.drain()
                    VE.raw.tensor_scalar(out=cols[:, 5, :], in0=cols[:, 4, :],
                                         scalar1=1e-5, scalar2=None, op0=OP.add)
                    VE.raw.drain()
                    t_rp = vv.mark(VE.raw.reciprocal(cols[:, 6, :], cols[:, 5, :]))
                    SE.wait(t_rp)
                    t_rstd = sc.mark(SE.raw.activation(cols[:, 7, :], cols[:, 6, :],
                                                       ACT.Sqrt))
                    if want_s:
                        VE.wait(t_rstd)
                        VE.raw.tensor_tensor(out=cols[:, 5, :], in0=cols[:, 4, :],
                                             in1=cols[:, 6, :], op=OP.mult)
                        VE.raw.drain()
                        VE.raw.tensor_scalar(out=cols[:, 5, :], in0=cols[:, 5, :],
                                             scalar1=9.765625e-28, scalar2=None, op0=OP.max)
                        VE.raw.drain()
                        t_si = vv.mark(VE.raw.reciprocal(cols[:, 5, :], cols[:, 5, :]))
                        SE.wait(t_si)
                        t_sv = sc.mark(SE.raw.activation(s_tok[:], cols[:, 5, :], ACT.Sqrt))
                    t_xuse = [None, None]
                    for i in range(NT):
                        sl = i % 2
                        VE.wait(t_rstd, t_xuse[sl])
                        VE.raw.drain()
                        t_xn = vv.mark(VE.raw.tensor_scalar(
                            out=xtmp[:, sl, :], in0=acc[:, i, :],
                            scalar1=cols[:, 2, i:i + 1],
                            scalar2=cols[:, 7, i:i + 1], op0=OP.add, op1=OP.mult))
                        TE.wait(t_xn, tk["consts"])
                        for a in range(DT):
                            bk = a % 2
                            use_bank(bk)
                            tr = mm.mark(TE.raw.transpose(
                                PS[bk][:, 0:128], xtmp[:, sl, a * 128:(a + 1) * 128],
                                ident32[:]))
                            VE.wait(tr)
                            bank_free[bk] = vv.mark(VE.raw.tensor_copy(
                                xnT[:, a, i * 128:(i + 1) * 128], PS[bk][:, 0:128]))
                        t_xuse[sl] = tr
                    if want_s:
                        t_stok = sc.tok()
                        for i in range(NT):
                            bk = 2 + i % 2
                            use_bank(bk)
                            TE.wait(t_stok, tk["consts"])
                            tr = mm.mark(TE.raw.matmul(
                                PS[bk][0:1, 0:128], lhsT=s_tok[:, i:i + 1],
                                rhs=ident32[:], start=True, stop=True))
                            SE.wait(tr)
                            bank_free[bk] = sc.mark(SE.raw.copy(
                                s_row[0:1, i * 128:(i + 1) * 128], PS[bk][0:1, 0:128]))
                        t_srow = sc.tok()
                        if "srow" in dbg_aps:
                            final_outs.append(dma(SY, sd, dbg_aps["srow"], s_row[:],
                                                  deps=(t_srow,)))
                        for cch in range(NCH):
                            bk = 4 + cch % 2
                            use_bank(bk)
                            TE.wait(t_srow, tk["consts"])
                            tr = mm.mark(TE.raw.matmul(
                                PS[bk], lhsT=ones_row_f32[:],
                                rhs=s_row[0:1, cch * 512:(cch + 1) * 512],
                                start=True, stop=True))
                            VE.wait(tr)
                            bank_free[bk] = vv.mark(VE.raw.tensor_copy(
                                S_full[:, cch * 512:(cch + 1) * 512], PS[bk]))
                    barrier()

            def l0_attn(S_full, st_S):
                st_l0 = ExitStack()
                qT = sbl(st_l0, "qT", (128, T), BF16)
                kT = sbl(st_l0, "kT", (128, T), BF16)
                vT = sbl(st_l0, "vT", (128, T), BF16)
                q_aug = sbl(st_l0, "q_aug", (17, 8, T), BF16)
                q_tok = sbl(st_l0, "q_tok", (128, NT, 128), BF16)
                with st_l0:
                    t_ev = {0: None, 1: None, 2: None}
                    for cch in range(NCH):
                        for wi in range(3):
                            buf = (qT, kT, vT)[wi]
                            bk = (cch * 3 + wi) % 4
                            use_bank(bk)
                            TE.wait(tk["wfold"])
                            for a in range(DT):
                                lhsT = (wq_b[:, a, :] if wi == 0
                                        else wkv_b[:, a, (wi - 1) * 128:wi * 128])
                                t_mm = mm.mark(TE.raw.matmul(
                                    PS[bk], lhsT=lhsT, rhs=xnT[:, a, cch * 512:(cch + 1) * 512],
                                    start=(a == 0), stop=(a == DT - 1)))
                            VE.wait(t_mm)
                            t_c = vv.mark(VE.raw.tensor_tensor(
                                out=buf[:, cch * 512:(cch + 1) * 512], in0=PS[bk],
                                in1=S_full[:, cch * 512:(cch + 1) * 512], op=OP.mult))
                            bank_free[bk] = t_c
                            t_ev[wi] = t_c
                    barrier()
                    st_S.close()
                    if l0sub < 2:
                        return

                    kvA = sbl(st_l0, "kvA", (17, 8, 17), BF16)
                    kvB = sbl(st_l0, "kvB", (128, 8, 17), BF16)
                    kvC = sbl(st_l0, "kvC", (128, 8, 17), BF16)
                    with ExitStack() as st:
                        k_tok = sbl(st, "k_tok", (128, NT, 8, 17), BF16)
                        v_tok = sbl(st, "v_tok", (128, NT, 8, 17), BF16)
                        ksc = sbl(st, "ksc", (128, 8, 16), BF16)
                        kf2 = sbl(st, "kf2", (128, 2, 8, 256), BF16)
                        kvS = sbl(st, "kvS", (17, 8, 273), BF16)

                        t_qaug = [vv.mark(VE.raw.memset(q_aug[0:1, :, :], 1.0))]
                        for h in range(8):
                            t_qaug.append(dma(SY, sd, q_aug[1:17, h, :],
                                              qT[h * 16:(h + 1) * 16, :], deps=(t_ev[0],)))
                        t_kones = vv.mark(VE.raw.memset(k_tok[:, :, :, 0:1], 1.0))
                        VE.raw.drain()
                        t_vones = vv.mark(VE.raw.memset(v_tok[:, :, :, 0:1], 1.0))

                        t_tok = [[None] * 3 for _ in range(NT)]
                        for i in range(NT):
                            for wi, src in enumerate((qT, kT, vT)):
                                bk = (i * 3 + wi) % 4
                                use_bank(bk)
                                TE.wait(t_ev[wi], tk["consts"])
                                tr = mm.mark(TE.raw.matmul(
                                    PS[bk][:, 0:128], lhsT=src[:, i * 128:(i + 1) * 128],
                                    rhs=ident16[:], start=True, stop=True))
                                VE.wait(tr)
                                if wi == 0:
                                    t_tok[i][wi] = vv.mark(VE.raw.tensor_copy(
                                        q_tok[:, i, :], PS[bk][:, 0:128]))
                                else:
                                    dst = (k_tok if wi == 1 else v_tok)[:, i, :, 1:17]
                                    t_tok[i][wi] = vv.mark(VE.raw.tensor_copy(
                                        dst, PS[bk][:, 0:128].rearrange("p (h d) -> p h d", h=8)))
                                bank_free[bk] = t_tok[i][wi]

                        if l0sub < 3:
                            barrier()
                            return
                        kvps = [pA[0:17, 0:273], pA[0:17, 512:785],
                                pB[0:17, 0:273], pB[0:17, 512:785],
                                pC[0:17, 0:273], pC[0:17, 512:785],
                                pD[0:17, 0:273], pD[0:17, 512:785]]
                        t_kf2_used = None
                        for i in range(NT):
                            par = i % 2
                            VE.wait(t_tok[i][1], t_tok[i][2], t_kones, t_vones, t_kf2_used)
                            VE.raw.drain()
                            VE.raw.tensor_scalar(out=ksc[:], in0=k_tok[:, i, :, 1:17],
                                                 scalar1=TAYSC, scalar2=None, op0=OP.mult)
                            VE.raw.drain()
                            for j in range(16):
                                t_kf2 = vv.mark(VE.raw.tensor_tensor(
                                    out=kf2[:, par, :, j * 16:(j + 1) * 16], in0=ksc[:],
                                    in1=ksc[:, :, j:j + 1].broadcast_to([128, 8, 16]),
                                    op=OP.mult))
                            TE.wait(t_kf2)
                            if i == 0:
                                for bk in range(8):
                                    use_bank(bk)
                            for h in range(8):
                                TE.raw.matmul(kvps[h][:, 0:17], lhsT=v_tok[:, i, h, :],
                                              rhs=k_tok[:, i, h, :],
                                              start=(i == 0), stop=False)
                                t_kv = mm.mark(TE.raw.matmul(
                                    kvps[h][:, 17:273], lhsT=v_tok[:, i, h, :],
                                    rhs=kf2[:, par, h, :],
                                    start=False, stop=(i == NT - 1)))
                            if i >= 1:
                                t_kf2_used = t_kv

                        t_kvS = []
                        for h in range(8):
                            SE.wait(t_kv)
                            t_kvS.append(sc.mark(SE.raw.copy(kvS[:, h, :], kvps[h])))
                            bank_free[h] = t_kvS[-1]
                        for h in range(8):
                            for pi, (lo, hi, dst) in enumerate(((0, 17, kvA), (17, 145, kvB),
                                                                (145, 273, kvC))):
                                bk = (h * 3 + pi) % 2
                                use_bank(bk)
                                TE.wait(t_kvS[h])
                                tr = mm.mark(TE.raw.matmul(
                                    PS[bk][0:(hi - lo), 0:17], lhsT=kvS[:, h, lo:hi],
                                    rhs=ident16[0:17, 0:17], start=True, stop=True))
                                VE.wait(tr)
                                bank_free[bk] = vv.mark(VE.raw.tensor_copy(
                                    dst[:, h, :], PS[bk][0:(hi - lo), 0:17]))
                        t_kvT = vv.tok()
                        barrier()

                    if l0sub < 4:
                        return
                    with ExitStack() as st:
                        qsc = sbl(st, "qsc", (128, 8, 16), BF16)
                        qf2 = sbl(st, "qf2", (128, 8, 256), BF16)
                        qf2T = sbl(st, "qf2T", (128, 2, 8, 512), BF16)
                        outS = sbl(st, "outS", (17, 8, 512), BF16)
                        recS = sbl(st, "recS", (1, 2, 512), F32)
                        denL = sbl(st, "denL", (8, 512), F32)
                        recLb = sbl(st, "recLb", (8, 512), BF16)
                        wout2 = sbl(st, "wout2", (17, 8, D), BF16)
                        t_wo2z = vv.mark(VE.raw.memset(wout2[0:1, :, :], 0.0))
                        t_wo2 = dma(GE, gd, wout2[1:17, :, :],
                                    wo0.rearrange("(h e) m -> e h m", e=16),
                                    allow_slow_non_contiguous=True)
                        t_dstg2 = [None, None]
                        for cch in range(NCH):
                            t_qf2T = []
                            t_dens = []
                            t_qf2_used = mm.tok()
                            for jj in range(4):
                                i = 4 * cch + jj
                                VE.wait(t_tok[i][0], t_qf2_used)
                                VE.raw.drain()
                                VE.raw.tensor_scalar(
                                    out=qsc[:], in0=q_tok[:, i, :].rearrange(
                                        "p (h d) -> p h d", h=8),
                                    scalar1=TAYSC, scalar2=None, op0=OP.mult)
                                VE.raw.drain()
                                for j in range(16):
                                    t_qf2 = vv.mark(VE.raw.tensor_tensor(
                                        out=qf2[:, :, j * 16:(j + 1) * 16], in0=qsc[:],
                                        in1=qsc[:, :, j:j + 1].broadcast_to([128, 8, 16]),
                                        op=OP.mult))
                                TE.wait(t_qf2)
                                for h in range(8):
                                    for piece in range(2):
                                        bk = (h * 2 + piece) % 2
                                        use_bank(bk)
                                        tr = mm.mark(TE.raw.matmul(
                                            PS[bk][:, 0:128],
                                            lhsT=qf2[:, h, piece * 128:(piece + 1) * 128],
                                            rhs=ident16[:], start=True, stop=True))
                                        VE.wait(tr)
                                        bank_free[bk] = vv.mark(VE.raw.tensor_copy(
                                            qf2T[:, piece, h, jj * 128:(jj + 1) * 128],
                                            PS[bk][:, 0:128]))
                                        t_qf2T.append(bank_free[bk])
                                t_qf2_used = mm.tok()
                            if l0sub < 5:
                                continue
                            t_sc_h = []
                            for h in range(8):
                                bko = 2 + h % 2
                                use_bank(bko)
                                ps17 = PS[bko][0:17, :]
                                TE.wait(t_qaug, t_qf2T[-1], t_kvT)
                                TE.raw.matmul(ps17, lhsT=kvA[:, h, :],
                                              rhs=q_aug[:, h, cch * 512:(cch + 1) * 512],
                                              start=True, stop=False)
                                TE.raw.matmul(ps17, lhsT=kvB[:, h, :], rhs=qf2T[:, 0, h, :],
                                              start=False, stop=False)
                                t3 = mm.mark(TE.raw.matmul(ps17, lhsT=kvC[:, h, :],
                                                           rhs=qf2T[:, 1, h, :],
                                                           start=False, stop=True))
                                SE.wait(t3, t_dstg2[h % 2])
                                t_outS = sc.mark(SE.raw.copy(outS[:, h, :], ps17))
                                t_den = sc.mark(SE.raw.copy(recS[0:1, h % 2, :],
                                                            ps17[0:1, :]))
                                bank_free[bko] = t_den
                                t_dd = dma(SY, sd, denL[h:h + 1, :],
                                           recS[0:1, h % 2, :], deps=(t_den,))
                                t_dstg2[h % 2] = t_dd
                                t_dens.append(t_dd)
                            VE.wait(*t_dens)
                            VE.raw.drain()
                            VE.raw.tensor_scalar(out=denL[:], in0=denL[:],
                                                 scalar1=1e-5, scalar2=None, op0=OP.max)
                            VE.raw.drain()
                            VE.raw.reciprocal(denL[:], denL[:])
                            VE.raw.drain()
                            t_rcb = vv.mark(VE.raw.tensor_copy(recLb[:], denL[:]))
                            t_outS_all = sc.tok()
                            for h in range(8):
                                bk4 = 4 + h % 2
                                use_bank(bk4)
                                TE.wait(t_rcb, tk["consts"])
                                mmb = mm.mark(TE.raw.matmul(
                                    PS[bk4][0:17, :],
                                    lhsT=consts["sel8"][:, h * 17:(h + 1) * 17],
                                    rhs=recLb[:], start=True, stop=True))
                                VE.wait(mmb, t_outS_all)
                                t_sch = vv.mark(VE.raw.tensor_tensor(
                                    out=outS[:, h, :], in0=outS[:, h, :],
                                    in1=PS[bk4][0:17, :], op=OP.mult))
                                bank_free[bk4] = t_sch
                                t_sc_h.append(t_sch)
                            if l0sub < 6:
                                continue
                            for tt_ in range(4):
                                for dch in range(2):
                                    bk = 6 + (tt_ * 2 + dch) % 2
                                    use_bank(bk)
                                    TE.wait(t_sc_h, t_wo2z, t_wo2)
                                    for h in range(8):
                                        t_wo = mm.mark(TE.raw.matmul(
                                            PS[bk], lhsT=outS[:, h, tt_ * 128:(tt_ + 1) * 128],
                                            rhs=wout2[:, h, dch * 512:(dch + 1) * 512],
                                            start=(h == 0), stop=(h == 7)))
                                    VE.wait(t_wo)
                                    aslc = acc[:, 4 * cch + tt_, dch * 512:(dch + 1) * 512]
                                    bank_free[bk] = vv.mark(VE.raw.tensor_tensor(
                                        out=aslc, in0=aslc, in1=PS[bk], op=OP.add))
                    barrier()

            def l0_ffn(tern_gen):
                with ExitStack() as st:
                    W1q = sbl(st, "W1q", (128, 2, DT, 1024), BF16)
                    W2q = sbl(st, "W2q", (128, 2, 8, 1024), BF16)
                    h1 = sbl(st, "h1", (128, 8, 512), BF16)
                    b2s = sbl(st, "b2s", (1, 2, D), BF16)
                    G = sbl(st, "G", (128, NT, 8), F32)
                    lgt = sbl(st, "lgt", (128, 8), F32)
                    lgt2 = sbl(st, "lgt2", (128, 8), F32)
                    colsr = sbl(st, "colsr", (128, 6), F32)
                    for i in range(NT):
                        bk = 6 + i % 2
                        use_bank(bk)
                        TE.wait(tk["wr"], tk.get("smallpar"))
                        psl = PS[bk][:, 0:8]
                        for a in range(DT):
                            TE.raw.matmul(psl, lhsT=xnT[:, a, i * 128:(i + 1) * 128],
                                          rhs=wr_b[:, a, :], start=(a == 0), stop=False)
                        t_lg = mm.mark(TE.raw.matmul(psl, lhsT=ones_row_bf[:],
                                                     rhs=br_b[:], start=False, stop=True))
                        VE.wait(t_lg)
                        VE.raw.tensor_reduce(colsr[:, 0:1], psl, AX, OP.max)
                        VE.raw.drain()
                        t_mx = vv.mark(VE.raw.tensor_scalar(
                            out=colsr[:, 1:2], in0=colsr[:, 0:1],
                            scalar1=-1.0, scalar2=None, op0=OP.mult))
                        SE.wait(t_mx)
                        t_e = sc.mark(SE.raw.activation(lgt[:], psl, ACT.Exp,
                                                        bias=colsr[:, 1:2]))
                        VE.wait(t_e)
                        VE.raw.tensor_reduce(colsr[:, 2:3], lgt[:], AX, OP.add)
                        VE.raw.drain()
                        VE.raw.reciprocal(colsr[:, 3:4], colsr[:, 2:3])
                        VE.raw.tensor_scalar(out=lgt2[:], in0=psl, scalar1=colsr[:, 0:1],
                                             scalar2=-1e6, op0=OP.is_ge, op1=OP.mult)
                        VE.raw.drain()
                        bank_free[bk] = vv.mark(VE.raw.tensor_tensor(
                            out=lgt2[:], in0=lgt2[:], in1=consts["iota8big"][:], op=OP.add))
                        VE.raw.drain()
                        VE.raw.tensor_reduce(colsr[:, 4:5], lgt2[:], AX, OP.min)
                        VE.raw.drain()
                        VE.raw.tensor_scalar(out=lgt2[:], in0=consts["iota8"][:],
                                             scalar1=colsr[:, 4:5], scalar2=None,
                                             op0=OP.is_equal)
                        VE.raw.drain()
                        t_G = vv.mark(VE.raw.tensor_scalar(
                            out=G[:, i, :], in0=lgt2[:], scalar1=colsr[:, 3:4],
                            scalar2=None, op0=OP.mult))
                    if "G" in dbg_aps:
                        final_outs.append(dma(SY, sd, dbg_aps["G"], G[:], deps=(t_G,)))

                    t_w1use = [None, None]
                    t_w2use = [None, None]
                    t_b2use = [None, None]
                    t_h1use = mm.tok()
                    for e in range(8):
                        sl = e % 2
                        t_b2 = dma(GE, b2ss[sl], b2s[0:1, sl:sl + 1, :],
                                   b2_0[e:e + 1, :].rearrange("(o e) d -> o e d", o=1),
                                   deps=(t_b2use[sl],))
                        for qq in range(4):
                            s = (e * 4 + qq) % 2
                            t_w1 = dma(GE, moes[s], W1q[:, s, :, :],
                                       w1_0[e].rearrange("(a p) f -> p a f", p=128)
                                       [:, :, qq * 1024:(qq + 1) * 1024],
                                       deps=(t_w1use[s],))
                            t_w2 = dma(GE, moes[s], W2q[:, s, :, :],
                                       w2_0[e].rearrange("(b p) d -> p b d", p=128)
                                       [:, qq * 8:(qq + 1) * 8, :],
                                       deps=(t_w2use[s],))
                            for tq in range(max(1, NCH)):
                                for ft in range(8):
                                    bk = ft % 2
                                    use_bank(bk)
                                    TE.wait(t_w1)
                                    for a in range(DT):
                                        t_p1 = mm.mark(TE.raw.matmul(
                                            PS[bk], lhsT=W1q[:, s, a, ft * 128:(ft + 1) * 128],
                                            rhs=xnT[:, a, tq * 512:(tq + 1) * 512],
                                            start=(a == 0), stop=(a == DT - 1)))
                                    SE.wait(t_p1, t_h1use)
                                    bank_free[bk] = sc.mark(SE.raw.activation(
                                        h1[:, ft, :], PS[bk], ACT.Relu,
                                        bias=b1c0[:, e, qq * 8 + ft:qq * 8 + ft + 1]))
                                t_h1 = sc.tok()
                                VE.raw.drain()
                                for dch in range(2):
                                    for ts_ in range(4):
                                        bk = 2 + (dch * 4 + ts_) % 4
                                        use_bank(bk)
                                        TE.wait(t_h1, t_w2, t_G, t_b2)
                                        for ft in range(8):
                                            ins = TE.raw.matmul(
                                                PS[bk],
                                                lhsT=h1[:, ft, ts_ * 128:(ts_ + 1) * 128],
                                                rhs=W2q[:, s, ft, dch * 512:(dch + 1) * 512],
                                                start=(ft == 0),
                                                stop=(ft == 7 and qq != 0))
                                            if ft == 7 and qq == 0:
                                                ins = TE.raw.matmul(
                                                    PS[bk], lhsT=ones_row_bf[:],
                                                    rhs=b2s[0:1, sl, dch * 512:(dch + 1) * 512],
                                                    start=False, stop=True)
                                        t_p2 = mm.mark(ins)
                                        VE.wait(t_p2)
                                        aslc = acc[:, tq * 4 + ts_, dch * 512:(dch + 1) * 512]
                                        bank_free[bk] = vv.mark(VE.raw.scalar_tensor_tensor(
                                            out=aslc, in0=PS[bk],
                                            scalar=G[:, tq * 4 + ts_, e:e + 1],
                                            in1=aslc, op0=OP.mult, op1=OP.add))
                                t_h1use = mm.tok()
                            t_w1use[s] = mm.tok()
                            t_w2use[s] = t_w1use[s]
                            if tern_gen is not None:
                                for _ in range(2 if e < 6 else 5):
                                    if next(tern_gen, "END") == "END":
                                        tern_gen = None
                                        break
                        t_b2use[sl] = mm.tok()
                    if tern_gen is not None:
                        for _ in tern_gen:
                            pass
                    barrier()

            def tern_steps():
                st = ExitStack()
                wbuf = sbl(st, "wbuf", (128, 2, 2048), F32)
                redc = sbl(st, "redc", (128, 52), F32)
                thr = sbl(st, "thr", (1, 4), F32)
                thrc = sbl(st, "thrc", (128, 4), F32)
                ttmp = sbl(st, "ttmp", (128, 4, 128), BF16)
                buf1 = sbl(st, "buf1", (128, 8, 256), BF16)
                buf2 = sbl(st, "buf2", (128, 4, 512), BF16)
                slot_used = [None, None]
                t_thrc = None
                tern_ctr = [0]

                def tern_group4(bk, srcs, dsts, thc, deps):
                    use_bank(bk)
                    TE.wait(tk["consts"], *deps)
                    for i4, s_ap in enumerate(srcs):
                        tr = mm.mark(TE.raw.transpose(
                            PS[bk][:, i4 * 128:(i4 + 1) * 128], s_ap, ident32[:]))
                    VE.wait(tr, t_thrc)
                    VE.raw.drain()
                    for i4 in range(4):
                        VE.raw.tensor_scalar(out=ttmp[:, i4, :],
                                             in0=PS[bk][:, i4 * 128:(i4 + 1) * 128],
                                             scalar1=thrc[:, thc + 1:thc + 2],
                                             scalar2=None, op0=OP.is_lt)
                    VE.raw.drain()
                    for i4 in range(4):
                        tt_ = vv.mark(VE.raw.scalar_tensor_tensor(
                            out=dsts[i4], in0=PS[bk][:, i4 * 128:(i4 + 1) * 128],
                            scalar=thrc[:, thc:thc + 1],
                            in1=ttmp[:, i4, :], op0=OP.is_gt, op1=OP.subtract))
                    bank_free[bk] = tt_
                    return tt_

                def p1_src(t):
                    if t < 32:
                        return w1_1[t * 128:(t + 1) * 128, :], redc[:, t:t + 1], 1024
                    tt = t - 32
                    return (w2_1[(tt // 2) * 128:(tt // 2 + 1) * 128,
                                 (tt % 2) * 2048:(tt % 2 + 1) * 2048],
                            redc[:, 32 + tt:33 + tt], 2048)

                t_l = [None, None]
                srcap, _, rl = p1_src(0)
                t_l[0] = dma(SY, tls[0], wbuf[:, 0, 0:rl], srcap, deps=(None,))
                for t in range(48):
                    s = t % 2
                    if t + 1 < 48:
                        srcap, _, rl1 = p1_src(t + 1)
                        t_l[(t + 1) % 2] = dma(SY, tls[(t + 1) % 2],
                                               wbuf[:, (t + 1) % 2, 0:rl1], srcap,
                                               deps=(slot_used[(t + 1) % 2],))
                    _, cdst, rl0 = p1_src(t)
                    VE.wait(t_l[s])
                    slot_used[s] = vv.mark(VE.raw.tensor_reduce(
                        cdst, wbuf[:, s, 0:rl0], AX, OP.add,
                        apply_absolute_value=True))
                    if t % 2 == 1:
                        yield
                for wi in range(2):
                    VE.raw.drain()
                    csrc = redc[:, 0:32] if wi == 0 else redc[:, 32:48]
                    rcol = redc[:, 48 + wi:49 + wi]
                    t_red = vv.mark(VE.raw.tensor_reduce(rcol, csrc, AX, OP.add))
                    use_bank(6)
                    TE.wait(t_red)
                    t_ps = mm.mark(TE.raw.matmul(PS[6][0:1, 0:1], lhsT=rcol,
                                                 rhs=consts["ones_col_f32"][:],
                                                 start=True, stop=True))
                    VE.wait(t_ps)
                    nelem = float(1024 * 4096)
                    VE.raw.tensor_scalar(out=thr[0:1, 0:1], in0=PS[6][0:1, 0:1],
                                         scalar1=1.0 / nelem, scalar2=1e-8,
                                         op0=OP.mult, op1=OP.max)
                    VE.raw.drain()
                    VE.raw.tensor_scalar(out=thr[0:1, 1:2], in0=thr[0:1, 0:1],
                                         scalar1=0.5, scalar2=None, op0=OP.mult)
                    VE.raw.drain()
                    t_thr = vv.mark(VE.raw.tensor_scalar(
                        out=thr[0:1, 2:3], in0=thr[0:1, 1:2],
                        scalar1=-1.0, scalar2=None, op0=OP.mult))
                    bank_free[6] = t_thr
                    use_bank(6)
                    TE.wait(t_thr)
                    t_bc = mm.mark(TE.raw.matmul(PS[6][:, 0:2],
                                                 lhsT=ones_row_f32[:],
                                                 rhs=thr[0:1, 1:3], start=True, stop=True))
                    VE.wait(t_bc)
                    bank_free[6] = vv.mark(VE.raw.tensor_copy(
                        thrc[:, 2 * wi:2 * wi + 2], PS[6][:, 0:2]))
                    t_thrc = vv.tok()
                yield

                t_st = None
                w1r = w1_1.rearrange("(t p) d -> p t d", p=128)
                t_l[0] = dma(SY, tls[0],
                             wbuf[:, 0, 0:2048].rearrange("p (t d) -> p t d", t=2),
                             w1r[:, 0:2, :], deps=(slot_used[0],))
                for fb in range(16):
                    s = fb % 2
                    if fb + 1 < 16:
                        s1 = (fb + 1) % 2
                        t_l[s1] = dma(SY, tls[s1],
                                      wbuf[:, s1, 0:2048].rearrange("p (t d) -> p t d", t=2),
                                      w1r[:, (fb + 1) * 2:(fb + 2) * 2, :],
                                      deps=(slot_used[s1],))
                    for g4 in range(4):
                        ftl, a0 = g4 // 2, (g4 % 2) * 4
                        srcs = [wbuf[:, s, ftl * 1024 + (a0 + j) * 128:
                                     ftl * 1024 + (a0 + j + 1) * 128] for j in range(4)]
                        dsts = [buf1[:, a0 + j, ftl * 128:(ftl + 1) * 128]
                                for j in range(4)]
                        tt_ = tern_group4(6 + g4 % 2, srcs, dsts, 0, (t_l[s], t_st))
                    slot_used[s] = tt_
                    t_st = dma(SY, tst, w1t_sc[:, :, fb * 256:(fb + 1) * 256],
                               buf1[:], deps=(tt_,))
                    yield
                w2r = w2_1.rearrange("(dc p) f -> p dc f", p=128)

                def p2b_ap(u):
                    ftb, dc = u // 8, u % 8
                    return w2r[:, dc, ftb * 512:(ftb + 1) * 512]

                t_l[0] = dma(SY, tls[0], wbuf[:, 0, 0:512], p2b_ap(0),
                             deps=(slot_used[0],))
                for u in range(64):
                    ftb, dc = u // 8, u % 8
                    dh, dc2 = dc // 4, dc % 4
                    s = u % 2
                    if u + 1 < 64:
                        s1 = (u + 1) % 2
                        t_l[s1] = dma(SY, tls[s1], wbuf[:, s1, 0:512], p2b_ap(u + 1),
                                      deps=(slot_used[s1],))
                    srcs = [wbuf[:, s, fl * 128:(fl + 1) * 128] for fl in range(4)]
                    dsts = [buf2[:, fl, dc2 * 128:(dc2 + 1) * 128] for fl in range(4)]
                    tt_ = tern_group4(6 + u % 2, srcs, dsts, 2, (t_l[s], t_st))
                    slot_used[s] = tt_
                    if dc2 == 3:
                        t_st = dma(SY, tst,
                                   w2t_sc[:, ftb * 4:(ftb + 1) * 4,
                                          dh * 512:(dh + 1) * 512],
                                   buf2[:], deps=(tt_,))
                    if u % 2 == 1:
                        yield
                tk["tern"] = t_st
                st.close()

            def l1_attn():
                st_l1 = ExitStack()
                qrotT = sbl(st_l1, "qrotT", (128, 8, T), BF16)
                krotT = sbl(st_l1, "krotT", (128, 8, T + VKV), BF16)
                v_ext = sbl(st_l1, "v_ext", (128, NT + 1, 32, 65), BF16)
                with st_l1:
                    with ExitStack() as st:
                        sincos = sbl(st, "sincos", (128, 2, T), BF16)
                        Wb = sbl(st, "Wqk", (128, DT, D), BF16)
                        qb = sbl(st, "qb", (128, 2, 512), BF16)
                        rt1 = sbl(st, "rt1", (128, 512), BF16)
                        rt2 = sbl(st, "rt2", (128, 512), BF16)
                        tk["sc"] = dma(GE, gd, sincos[:, 0, :], sin_in)
                        tk["sc"] = dma(GE, gd, sincos[:, 1, :], cos_in)
                        t_W = None
                        t_Wuse = None
                        t_pq = [None, None]
                        t_evq = [None, None]
                        t_qb_used = [None, None]
                        items = [(w_, a2_, c_) for w_ in range(2)
                                 for a2_ in range(8) for c_ in range(NCH)]
                        nit = len(items)
                        half = nit // 2
                        for p in range(nit + 1):
                            if p < nit:
                                which, a2, cch = items[p]
                                if p % half == 0:
                                    t_W = dma(GE, gw, Wb[:],
                                              (wqf1 if which == 0 else wkf1)
                                              .rearrange("(a p) m -> p a m", p=128),
                                              deps=(t_Wuse,))
                                sl = p % 2
                                bk = p % 2
                                use_bank(bk)
                                TE.wait(t_W)
                                for a in range(DT):
                                    t_pq[sl] = mm.mark(TE.raw.matmul(
                                        PS[bk], lhsT=Wb[:, a, a2 * 128:(a2 + 1) * 128],
                                        rhs=xnT[:, a, cch * 512:(cch + 1) * 512],
                                        start=(a == 0), stop=(a == DT - 1)))
                                if p == half - 1:
                                    t_Wuse = t_pq[sl]
                                VE.wait(t_pq[sl], t_qb_used[sl])
                                VE.raw.drain()
                                if which == 0:
                                    t_evq[sl] = vv.mark(VE.raw.tensor_scalar(
                                        out=qb[:, sl, :], in0=PS[bk],
                                        scalar1=bq_c[:, a2:a2 + 1], scalar2=L1QSC,
                                        op0=OP.add, op1=OP.mult))
                                else:
                                    t_evq[sl] = vv.mark(VE.raw.tensor_scalar(
                                        out=qb[:, sl, :], in0=PS[bk],
                                        scalar1=bk_c[:, a2:a2 + 1], scalar2=None,
                                        op0=OP.add))
                                bank_free[bk] = t_evq[sl]
                            if p >= 1:
                                which, a2, cch = items[p - 1]
                                sl1 = (p - 1) % 2
                                dstT = qrotT if which == 0 else krotT
                                bk2 = 2 + (p - 1) % 2
                                use_bank(bk2)
                                TE.wait(t_evq[sl1], tk["consts"])
                                t_fl = mm.mark(TE.raw.matmul(
                                    PS[bk2], lhsT=consts["pflip"][:], rhs=qb[:, sl1, :],
                                    start=True, stop=True))
                                VE.wait(t_fl, tk["sc"])
                                VE.raw.drain()
                                VE.raw.tensor_tensor(
                                    out=rt1[:], in0=qb[:, sl1, :],
                                    in1=sincos[:, 1, cch * 512:(cch + 1) * 512], op=OP.mult)
                                VE.raw.tensor_tensor(
                                    out=rt2[:], in0=PS[bk2],
                                    in1=sincos[:, 0, cch * 512:(cch + 1) * 512], op=OP.mult)
                                VE.raw.drain()
                                t_rot = vv.mark(VE.raw.tensor_tensor(
                                    out=dstT[:, a2, cch * 512:(cch + 1) * 512],
                                    in0=rt1[:], in1=rt2[:], op=OP.add))
                                bank_free[bk2] = t_rot
                                t_qb_used[sl1] = t_rot
                        tk["rotdone"] = vv.tok()
                        tk["vk"] = dma(GE, gd, krotT[:, :, T:T + VKV],
                                       vk1.rearrange("(a p) m -> p a m", p=128),
                                       allow_slow_non_contiguous=True)
                        barrier()

                    with ExitStack() as st:
                        WVb = sbl(st, "WVb", (128, DT, 1024), BF16)
                        bv_r = sbl(st, "bv_r", (1, 2048), BF16)
                        t_bv = dma(GE, gd, bv_r[:], bv1)
                        t_vz = vv.mark(VE.raw.memset(v_ext[:], 0.0))
                        VE.raw.drain()
                        t_vo = vv.mark(VE.raw.memset(v_ext[:, :, :, 64:65], 1.0))
                        tk["vv"] = dma(GE, gd, v_ext[0:VKV, NT, :, 0:64],
                                       vv1_in.rearrange("v (h r) -> v h r", h=32),
                                       deps=(t_vz, t_vo))
                        t_WVuse = None
                        for half in range(2):
                            t_WV = dma(GE, gw, WVb[:],
                                       wvf1.rearrange("(a p) m -> p a m", p=128)
                                       [:, :, half * 1024:(half + 1) * 1024],
                                       deps=(t_WVuse,))
                            for kvt in range(NT):
                                for hc2 in range(2):
                                    hc = half * 2 + hc2
                                    bk = 4 + (kvt * 2 + hc2) % 4
                                    use_bank(bk)
                                    TE.wait(t_WV, t_bv)
                                    for a in range(DT):
                                        TE.raw.matmul(
                                            PS[bk], lhsT=xnT[:, a, kvt * 128:(kvt + 1) * 128],
                                            rhs=WVb[:, a, hc2 * 512:(hc2 + 1) * 512],
                                            start=(a == 0), stop=False)
                                    t_pv = mm.mark(TE.raw.matmul(
                                        PS[bk], lhsT=ones_row_bf[:],
                                        rhs=bv_r[0:1, hc * 512:(hc + 1) * 512],
                                        start=False, stop=True))
                                    VE.wait(t_pv, t_vo)
                                    bank_free[bk] = vv.mark(VE.raw.tensor_copy(
                                        v_ext[:, kvt, hc * 8:(hc + 1) * 8, 0:64],
                                        PS[bk].rearrange("p (h r) -> p h r", h=8)))
                            t_WVuse = mm.tok()
                        barrier()

                    if "qrot" in dbg_aps:
                        final_outs.append(dma(SY, sd, dbg_aps["qrot"], qrotT[:]))
                    if "krot" in dbg_aps:
                        final_outs.append(dma(SY, sd, dbg_aps["krot"], krotT[:]))
                    if "vext" in dbg_aps:
                        final_outs.append(dma(SY, sd, dbg_aps["vext"],
                                              v_ext[:].rearrange("p a b c -> p (a b c)")))

                    tk["xnspill"] = dma(SY, sd, xnsp, xnT[:])
                    barrier()
                    st_xn.close()

                    with ExitStack() as st:
                        zT = sbl(st, "zT", (128, 16, 512), BF16)
                        expT = sbl(st, "expT", (128, 8, 512), BF16)
                        denM = sbl(st, "denM", (32, 512), F32)
                        recMb = sbl(st, "recMb", (32, 512), BF16)
                        den_stg = sbl(st, "den_stg", (1, 2, 512), F32)
                        sel_sb = sbl(st, "sel_sb", (32, 2048), BF16)
                        WOb = sbl(st, "WOb", (128, 16, 512), BF16)
                        stg = sbl(st, "stg", (128, 2, 512), F32)
                        shcol = sbl(st, "shcol", (128, 1), F32)
                        s_masks = sbl(st, "s_masks", (128, 5, 512), BF16)
                        t_sh = vv.mark(VE.raw.memset(shcol[:], EXP_SHIFT))
                        SE.wait(t_sh)
                        t_mskld = dma(SY, sd, s_masks[:], masks_in.rearrange("m p q -> p m q"))
                        t_selld = dma(SY, sd, sel_sb[:], sel_in)
                        wof_r = wof1.rearrange("(vc p) d -> p vc d", p=128)
                        t_wob_used = None
                        t_stg_used = [None, None]
                        t_zT_used = None
                        t_dstg_used = [None, None]
                        t_exp_used = [[mm.tok(), mm.tok()] for _ in range(4)]
                        for cch in range(NCH):
                            full = [kvt for kvt in range(NT) if kvt * 128 <= cch * 512 + 527]
                            has_virt = T <= cch * 512 + 527
                            nsteps = len(full) + (1 if has_virt else 0)
                            t_zT = []
                            t_den_dmas = []
                            t_wob = dma(GE, gw, WOb[:], wof_r[:, :, 0:512],
                                        deps=(t_wob_used,))
                            for a2 in range(8):
                                t_e = [[None, None] for _ in range(4)]
                                t_z = [None] * 4
                                for hh in range(4):
                                    use_bank(4 + hh)
                                for p in range(nsteps + 1):
                                    if p < nsteps:
                                        virt = p >= len(full)
                                        kvt = full[p] if not virt else NT
                                        rows = 128 if not virt else VKV
                                        delta = (kvt * 128 if not virt else T) - cch * 512
                                        sl = p % 2
                                        for hh in range(4):
                                            hb = 32 * hh
                                            use_bank(hh)
                                            TE.wait(tk["rotdone"], tk["vk"])
                                            t_s = mm.mark(TE.raw.matmul(
                                                PS[hh][0:rows, :],
                                                lhsT=krotT[hb:hb + 32, a2,
                                                           kvt * 128:kvt * 128 + rows],
                                                rhs=qrotT[hb:hb + 32, a2,
                                                          cch * 512:(cch + 1) * 512],
                                                start=True, stop=True,
                                                tile_position=(hb, 0)))
                                            SE.wait(t_s, t_exp_used[hh][sl])
                                            t_ex = sc.mark(SE.raw.activation(
                                                expT[0:rows, 2 * hh + sl, :],
                                                PS[hh][0:rows, :],
                                                ACT.Exp, bias=shcol[0:rows, :]))
                                            bank_free[hh] = t_ex
                                            if 0 <= delta <= 512:
                                                VE.wait(t_ex, t_mskld)
                                                t_ex = vv.mark(VE.raw.tensor_tensor(
                                                    out=expT[0:rows, 2 * hh + sl, :],
                                                    in0=expT[0:rows, 2 * hh + sl, :],
                                                    in1=s_masks[0:rows, delta // 128, :],
                                                    op=OP.mult))
                                            t_e[hh][sl] = t_ex
                                    if p >= 1:
                                        pz = p - 1
                                        virt = pz >= len(full)
                                        kvt = full[pz] if not virt else NT
                                        rows = 128 if not virt else VKV
                                        sl = pz % 2
                                        for hh in range(4):
                                            h = a2 * 4 + hh
                                            TE.wait(t_e[hh][sl], tk["vv"], t_zT_used)
                                            t_z[hh] = mm.mark(TE.raw.matmul(
                                                PS[4 + hh][0:65, :],
                                                lhsT=v_ext[0:rows, kvt, h, :],
                                                rhs=expT[0:rows, 2 * hh + sl, :],
                                                start=(pz == 0), stop=(pz == nsteps - 1)))
                                            t_exp_used[hh][sl] = t_z[hh]
                                for hh in range(4):
                                    h = a2 * 4 + hh
                                    zoff, vc = 64 * (h % 2), h // 2
                                    sl = hh % 2
                                    VE.wait(t_z[hh], t_dstg_used[sl])
                                    VE.raw.drain()
                                    VE.raw.tensor_copy(den_stg[0:1, sl, :],
                                                       PS[4 + hh][64:65, :])
                                    VE.raw.drain()
                                    t_zc = vv.mark(VE.raw.tensor_copy(
                                        zT[zoff:zoff + 64, vc, :], PS[4 + hh][0:64, :]))
                                    bank_free[4 + hh] = t_zc
                                    t_dd = dma(SY, sd, denM[h:h + 1, :],
                                               den_stg[0:1, sl, :], deps=(t_zc,))
                                    t_dstg_used[sl] = t_dd
                                    t_den_dmas.append(t_dd)
                            VE.wait(*t_den_dmas)
                            VE.raw.drain()
                            VE.raw.reciprocal(denM[:], denM[:])
                            VE.raw.drain()
                            t_rcb = vv.mark(VE.raw.tensor_copy(recMb[:], denM[:]))
                            for h in range(32):
                                zoff, vc = 64 * (h % 2), h // 2
                                bk = h % 8
                                use_bank(bk)
                                TE.wait(t_rcb, t_selld)
                                t_rb = mm.mark(TE.raw.matmul(
                                    PS[bk][0:64, :],
                                    lhsT=sel_sb[:, h * 64:(h + 1) * 64],
                                    rhs=recMb[:], start=True, stop=True))
                                VE.wait(t_rb)
                                t_zn = vv.mark(VE.raw.tensor_tensor(
                                    out=zT[zoff:zoff + 64, vc, :],
                                    in0=zT[zoff:zoff + 64, vc, :],
                                    in1=PS[bk][0:64, :], op=OP.mult))
                                bank_free[bk] = t_zn
                                t_zT.append(t_zn)
                            for dch in range(2):
                                if dch == 1:
                                    t_wob = dma(GE, gw, WOb[:], wof_r[:, :, 512:1024],
                                                deps=(t_wob_used,))
                                for qt in range(4):
                                    bk = (qt + dch * 2) % 4
                                    use_bank(bk)
                                    TE.wait(t_zT, t_wob)
                                    for vc in range(16):
                                        t_wo = mm.mark(TE.raw.matmul(
                                            PS[bk], lhsT=zT[:, vc, qt * 128:(qt + 1) * 128],
                                            rhs=WOb[:, vc, :],
                                            start=(vc == 0), stop=(vc == 15)))
                                    sp = qt % 2
                                    SE.wait(t_wo, t_stg_used[sp])
                                    t_sg = sc.mark(SE.raw.copy(stg[:, sp, :], PS[bk]))
                                    bank_free[bk] = t_sg
                                    r0 = cch * 512 + qt * 128
                                    t_acd = dma(GE, accs[sp],
                                                o_out[r0:r0 + 128, dch * 512:(dch + 1) * 512],
                                                stg[:, sp, :], deps=(t_sg, tk["spill"]),
                                                accum_op=OP.add)
                                    t_stg_used[sp] = t_acd
                                    final_outs.append(t_acd)
                                t_wob_used = mm.tok()
                            t_zT_used = sc.tok()
                    barrier()

            def l1_ffn():
                with ExitStack() as st:
                    xnb = sbl(st, "xnb", (128, DT, T), BF16)
                    b2r1 = sbl(st, "b2r1", (1, D), BF16)
                    t_b2f = dma(GE, gd, b2r1[:], b2_1)
                    w1tb = sbl(st, "w1tb", (128, DT, 4096), BF16)
                    w2tb = sbl(st, "w2tb", (128, 2, 32, 512), BF16)
                    hbuf = sbl(st, "hbuf", (128, 32, 512), BF16)
                    stg = sbl(st, "stg", (128, 2, 512), F32)
                    t_xn = dma(SY, sd, xnb[:], xnsp, deps=(tk["xnspill"],))
                    t_w1t = dma(GE, ffws[0], w1tb[:], w1t_sc, deps=(tk["tern"],))
                    t_w2use = [None, None]
                    t_stg_used = [None, None]
                    t_huse = mm.tok()
                    for tq in range(max(1, NCH)):
                        for dch in range(2):
                            ws = dch
                            t_w2t = dma(GE, ffws[1 + dch], w2tb[:, ws, :, :],
                                        w2t_sc[:, :, dch * 512:(dch + 1) * 512],
                                        deps=(t_w2use[ws], tk["tern"]))
                            if dch == 0:
                                for fc in range(32):
                                    bk = fc % 2
                                    use_bank(bk)
                                    TE.wait(t_w1t, t_xn)
                                    for a in range(DT):
                                        t_p1 = mm.mark(TE.raw.matmul(
                                            PS[bk], lhsT=w1tb[:, a, fc * 128:(fc + 1) * 128],
                                            rhs=xnb[:, a, tq * 512:(tq + 1) * 512],
                                            start=(a == 0), stop=(a == DT - 1)))
                                    SE.wait(t_p1, t_huse)
                                    bank_free[bk] = sc.mark(SE.raw.activation(
                                        hbuf[:, fc, :], PS[bk], ACT.Gelu,
                                        bias=b1c1[:, fc:fc + 1]))
                                t_h = sc.tok()
                            for qt in range(4):
                                bk = 2 + (qt % 4)
                                use_bank(bk)
                                TE.wait(t_h, t_w2t, t_b2f)
                                for fc in range(32):
                                    TE.raw.matmul(
                                        PS[bk], lhsT=hbuf[:, fc, qt * 128:(qt + 1) * 128],
                                        rhs=w2tb[:, ws, fc, :],
                                        start=(fc == 0), stop=False)
                                t_p2 = mm.mark(TE.raw.matmul(
                                    PS[bk], lhsT=ones_row_bf[:],
                                    rhs=b2r1[0:1, dch * 512:(dch + 1) * 512],
                                    start=False, stop=True))
                                sp = qt % 2
                                SE.wait(t_p2, t_stg_used[sp])
                                t_sg = sc.mark(SE.raw.copy(stg[:, sp, :], PS[bk]))
                                bank_free[bk] = t_sg
                                r0 = tq * 512 + qt * 128
                                t_acd = dma(GE, accs[sp],
                                            o_out[r0:r0 + 128, dch * 512:(dch + 1) * 512],
                                            stg[:, sp, :], deps=(t_sg, tk["spill"]),
                                            accum_op=OP.add)
                                t_stg_used[sp] = t_acd
                                final_outs.append(t_acd)
                            t_w2use[ws] = mm.tok()
                        t_huse = mm.tok()
                    barrier()

            st_S = ExitStack()
            S_full = st_S.enter_context(
                nc.sbuf_tensor("S_full", [128, T], F32, side="right"))
            layernorm_phase(True, S_full)
            if "xnT" in dbg_aps and upto < 3:
                final_outs.append(dma(SY, sd, dbg_aps["xnT"], xnT[:]))
            if upto >= 2:
                l0_attn(S_full, st_S)
            else:
                st_S.close()
            if "acc1" in dbg_aps:
                final_outs.append(dma(SY, sd, dbg_aps["acc1"], acc[:],
                                      deps=(vv.tok(), mm.tok(), sc.tok())))
            if upto >= 3:
                l0_ffn(tern_steps() if upto >= 4 else None)
            if "acc2" in dbg_aps:
                final_outs.append(dma(SY, sd, dbg_aps["acc2"], acc[:],
                                      deps=(vv.tok(), mm.tok(), sc.tok())))
            if upto >= 5:
                layernorm_phase(False, None)
                if "xnT" in dbg_aps:
                    final_outs.append(dma(SY, sd, dbg_aps["xnT"], xnT[:]))
                tk["spill"] = dma(SY, sd, o_out.rearrange("(i p) d -> p i d", p=128),
                                  acc[:], deps=(vv.tok(), mm.tok()))
                final_outs.append(tk["spill"])
                barrier()
                st_acc.close()
            if upto >= 6:
                l1_attn()
            if upto >= 7:
                l1_ffn()
            if upto < 5:
                tk["spill"] = dma(SY, sd, o_out.rearrange("(i p) d -> p i d", p=128),
                                  acc[:], deps=(vv.tok(), mm.tok(), sc.tok()))
                final_outs.append(tk["spill"])
            SY.wait(*[s.tok() for s in ALLSEMS])
            SY.wait(*final_outs)

    return nc


def host_inputs(inputs, T=2048):
    f32 = np.float32
    c = host_constants(T)
    WQ = np.ascontiguousarray(
        inputs["L1_attn_W_Q"].reshape(8, 4, 1024, 32).transpose(2, 0, 1, 3).reshape(1024, 1024))
    WK = np.ascontiguousarray(
        inputs["L1_attn_W_K"].reshape(8, 4, 1024, 32).transpose(2, 0, 1, 3).reshape(1024, 1024))
    WV = np.ascontiguousarray(inputs["L1_attn_W_V"][:, :, 0].T)
    WO = np.ascontiguousarray(inputs["L1_attn_W_O"][:, 0, :])
    bQ = np.ascontiguousarray(inputs["L1_attn_b_Q"].reshape(8, 4, 32).reshape(8, 128))
    bK = np.ascontiguousarray(inputs["L1_attn_b_K"].reshape(8, 4, 32).reshape(8, 128))
    bV = np.ascontiguousarray(inputs["L1_attn_b_V"][:, 0][None, :])
    vk = np.ascontiguousarray(
        inputs["L1_attn_virtual_k"].reshape(16, 8, 4, 32).transpose(1, 2, 3, 0).reshape(1024, 16))
    vvv = np.ascontiguousarray(inputs["L1_attn_virtual_v"][:, :, 0])
    base = {
        "L0_gamma": np.ascontiguousarray(inputs["L0_attn_gamma"].reshape(8, 128), dtype=f32),
        "L0_w_q": np.ascontiguousarray(inputs["L0_attn_w_q"], dtype=f32),
        "L0_w_kv": np.ascontiguousarray(inputs["L0_attn_w_kv"], dtype=f32),
        "L0_w_out": np.ascontiguousarray(inputs["L0_attn_w_out"], dtype=f32),
        "L0_WrT": np.ascontiguousarray(inputs["L0_ffn_Wr"].T, dtype=f32),
        "L0_br": np.ascontiguousarray(inputs["L0_ffn_br"][None, :], dtype=f32),
        "L0_W1": np.ascontiguousarray(inputs["L0_ffn_W1"], dtype=f32),
        "L0_b1": np.ascontiguousarray(inputs["L0_ffn_b1"], dtype=f32),
        "L0_W2": np.ascontiguousarray(inputs["L0_ffn_W2"], dtype=f32),
        "L0_b2": np.ascontiguousarray(inputs["L0_ffn_b2"], dtype=f32),
        "L1_WQ": WQ.astype(f32), "L1_WK": WK.astype(f32),
        "L1_WV": WV.astype(f32), "L1_WO": WO.astype(f32),
        "L1_bQ": bQ.astype(f32), "L1_bK": bK.astype(f32), "L1_bV": bV.astype(f32),
        "L1_vk": vk.astype(f32), "L1_vv": vvv.astype(f32),
        "L1_W1": np.ascontiguousarray(inputs["L1_ffn_W1"], dtype=f32),
        "L1_b1": np.ascontiguousarray(inputs["L1_ffn_b1"].reshape(32, 128), dtype=f32),
        "L1_W2": np.ascontiguousarray(inputs["L1_ffn_W2"], dtype=f32),
        "L1_b2": np.ascontiguousarray(inputs["L1_ffn_b2"][None, :], dtype=f32),
        "c_sin": c["sin"][:, :T], "c_cos": c["cos"][:, :T],
        "c_masks": np.ascontiguousarray(c["masks"]),
    }
    for nm in CSHAPES:
        base["c_" + nm] = np.ascontiguousarray(c[nm])
    base["c_sel32"] = np.ascontiguousarray(c["sel32"])
    x = np.asarray(inputs["L0_attn_x"], dtype=f32)
    in_maps = []
    for i in range(8):
        m = dict(base)
        m["x"] = np.ascontiguousarray(x[i, :T, :])
        in_maps.append(m)
    return in_maps


def kernel(**inputs):
    T = 2048
    nc = build(T=T)
    in_maps = host_inputs(inputs, T=T)
    res = run_bass_kernel_spmd(nc, in_maps, core_ids=list(range(8)))
    out = np.stack([np.asarray(r["o"], dtype=np.float32) for r in res.results], axis=0)
    return out
